# version 25
# baseline (speedup 1.0000x reference)
import numpy as np
import ml_dtypes
from contextlib import ExitStack

import concourse.bass as bass
import concourse.mybir as mybir
from concourse.bass_utils import run_bass_kernel_spmd

F32 = mybir.dt.float32
BF16 = mybir.dt.bfloat16
AF = mybir.ActivationFunctionType
ALU = mybir.AluOpType
AX = mybir.AxisListType

D = 1024
DQKV = 3 * D
NH = 16
NPAIR = 8
NE = 16
NEX = 17
INTER = 512
ROUTE_SCALE = 2.5
LN_EPS = 1e-5
DEN_EPS = 1e-6
CT = 512
TPC = 4
RCPD = float(1.0 / D)
RSQD = float((1.0 / D) ** 0.5)
EGROUPS = [(0, 1), (2, 3), (4, 5), (6, 7), (8, 9), (10, 11), (12, 13), (14, 15), (16,)]
EGROUPS_OF = {e: gi for gi, grp in enumerate(EGROUPS) for e in grp}
NSLOT = 3


def _preload_tab(nc, s, idx):
    s.add_instruction(mybir.InstLoadActFuncSet(
        name=nc.get_next_instruction_name(), act_func_set_id=idx, ins=[], outs=[]))


class Sem:
    def __init__(self, es, nc, name):
        self.h = es.enter_context(nc.semaphore(name))
        self.n = 0

    def inc(self, inst, k=1):
        inst.then_inc(self.h, k)
        self.n += k
        return self.n

    def wait(self, eng, n=None):
        eng.wait_ge(self.h, self.n if n is None else n)


class SQueue:

    def __init__(self, sem):
        self.sem = sem
        self.q = []

    def add(self, emitfn, inc=1):
        mark = None
        if inc:
            self.sem.n += inc
            mark = self.sem.n
        self.q.append((emitfn, mark, inc))
        return mark

    def flush(self, blk):
        q, self.q = self.q, []
        if not q:
            return

        def body(s):
            for fn, mark, inc in q:
                inst = fn(s)
                if mark is not None:
                    inst.then_inc(self.sem.h, inc)
        blk.scalar(body)


def build_program(L):
    assert L % CT == 0
    NCH = L // CT
    NT = L // 128
    nc = bass.Bass("TRN2", target_bir_lowering=False)

    dr = lambda n, s, dt, k="Internal": nc.dram_tensor(n, s, dt, kind=k).ap()
    x_in = dr("x_in", [L, D], F32, "ExternalInput")
    c_row = dr("c_row", [1, D], F32, "ExternalInput")
    wmod_in = dr("wmod_in", [D, 6 * D], F32, "ExternalInput")
    gamma_in = dr("gamma_in", [1, D], F32, "ExternalInput")
    beta_in = dr("beta_in", [1, D], F32, "ExternalInput")
    wqkv_in = dr("wqkv_in", [D, DQKV], F32, "ExternalInput")
    wout_in = dr("wout_in", [D, D], F32, "ExternalInput")
    gatewT_in = dr("gatewT_in", [D, NE], F32, "ExternalInput")
    gateb_in = dr("gateb_in", [1, NE], F32, "ExternalInput")
    w13T_in = dr("w13T_in", [NEX, D, D], F32, "ExternalInput")
    w2T_in = dr("w2T_in", [NEX, INTER, D], F32, "ExternalInput")
    ident_in = dr("ident_in", [128, 128], BF16, "ExternalInput")
    onesrow_in = dr("onesrow_in", [1, 128], BF16, "ExternalInput")
    eexp_in = dr("eexp_in", [NH, NPAIR * 128], BF16, "ExternalInput")
    erow_in = dr("erow_in", [NH, NE * 128], BF16, "ExternalInput")
    out = dr("out", [L, D], F32, "ExternalOutput")
    w13bf = dr("w13bf", [NEX, D, D], BF16)
    w2bf = dr("w2bf", [NEX, INTER, D], BF16)
    xn_scr = dr("xn_scr", [L, D], BF16)
    q_scr = dr("q_scr", [L, D], BF16)
    x2_scr = dr("x2_scr", [L, D], F32)
    h2_scr = dr("h2_scr", [L, D], BF16)

    es = ExitStack()
    sbuf = lambda n, s, dt: es.enter_context(nc.sbuf_tensor(n, s, dt))
    S = lambda n: Sem(es, nc, n)
    r3 = lambda ap: ap.rearrange("(t p) d -> p t d", p=128)
    tok = lambda c: slice(c * CT, (c + 1) * CT)

    ca1_b = sbuf("ca1_b", [128, D], BF16)
    sa_b = sbuf("sa_b", [128, D], BF16)
    bw_a = sbuf("bw_a", [96, 512], BF16)
    ones96 = sbuf("ones96", [96, 128], BF16)
    bw_b = sbuf("bw_b", [96, 512], BF16)
    A_b = sbuf("A_b", [128, D], BF16)
    B_b = sbuf("B_b", [128, D], BF16)
    cb1_b = sbuf("cb1_b", [128, D], BF16)
    sbb_b = sbuf("sbb_b", [128, D], BF16)
    ga_b = sbuf("ga_b", [128, D], F32)
    gb_b = sbuf("gb_b", [128, D], F32)
    gateb_b = sbuf("gateb_b", [128, NE], F32)
    ident = sbuf("ident", [128, 128], BF16)
    onesrow = sbuf("onesrow", [1, 128], BF16)
    onescol = sbuf("onescol", [128, 1], BF16)
    eexp = sbuf("eexp", [NH, NPAIR * 128], BF16)
    erow = sbuf("erow", [NH, NE * 128], BF16)
    rs1 = sbuf("rs1", [128, NT], F32)
    nmurs1 = sbuf("nmurs1", [128, NT], F32)
    rs3 = sbuf("rs3", [128, NT], F32)
    nmurs3 = sbuf("nmurs3", [128, NT], F32)
    ksum_sb = sbuf("ksum_sb", [128, NPAIR], F32)
    kv_sb = sbuf("kv_sb", [128, NPAIR * 128], BF16)
    ksbd = sbuf("ksbd", [128, NPAIR * NH], BF16)
    junk4 = sbuf("junk4", [128, TPC, D], BF16)
    epsc = sbuf("epsc", [128, 1], F32)
    gatew_sb = sbuf("gatew_sb", [128, 8, NE], BF16)
    es_wo = ExitStack()
    wout_sb = es_wo.enter_context(nc.sbuf_tensor("wout_sb", [128, 8, D], BF16))
    es_wq = ExitStack()
    wqkv_sb = es_wq.enter_context(nc.sbuf_tensor("wqkv_sb", [128, 8, DQKV], BF16))

    wcast = S("wcast")

    with ExitStack() as es0:
        sb0 = lambda n, s, dt: es0.enter_context(nc.sbuf_tensor(n, s, dt))
        ps0 = lambda n, s: es0.enter_context(nc.psum_tensor(n, s, F32))
        c_sb = sb0("c_sb", [1, D], F32)
        sc_bf = sb0("sc_bf", [1, D], BF16)
        sc_f = sb0("sc_f", [128, 8], BF16)
        mod_bf = sb0("mod_bf", [1, 6 * D], BF16)
        gb_row = sb0("gb_row", [1, NE], BF16)
        wmod_bf = sb0("wmod_bf", [128, 8, 3 * D], BF16)
        sgn_t = sb0("sgn_t", [128, D], F32)
        amx_t = sb0("amx_t", [128, D], F32)
        gammaT = sb0("gammaT", [128, 8], F32)
        betaT = sb0("betaT", [128, 8], BF16)
        mod_ps = ps0("mod_ps", [1, 3 * D])
        bc_ps = ps0("bc_ps", [128, D])
        bw_ps = mod_ps[0:1, 0:512]
        ch = S("p0ch")
        chd = S("p0chd")
        chs = S("p0chs")

        with nc.Block("p0") as blk:
            @blk.gpsimd
            def _(gp):
                chd.inc(gp.dma_start(out=wqkv_sb[:], in_=wqkv_in.rearrange("(k p) n -> p k n", p=128)), 16)
                chd.inc(gp.dma_start(out=wout_sb[:], in_=wout_in.rearrange("(k p) n -> p k n", p=128)), 16)
                chd.inc(gp.dma_start(out=gatew_sb[:], in_=gatewT_in.rearrange("(k p) n -> p k n", p=128)), 16)
                chd.inc(gp.dma_start(out=gb_row[:], in_=gateb_in), 16)
                bTv = beta_in.rearrange("o (kc p) -> kc p o", kc=8)
                for kc in range(8):
                    chd.inc(gp.dma_start(out=betaT[:, kc:kc + 1], in_=bTv[kc]), 16)

            @blk.sync
            def _(sy):
                chs.inc(sy.dma_start(out=c_sb[:], in_=c_row), 16)
                gTv = gamma_in.rearrange("o (kc p) -> kc p o", kc=8)
                for kc in range(8):
                    chs.inc(sy.dma_start(out=gammaT[:, kc:kc + 1], in_=gTv[kc]), 16)
                chs.inc(sy.dma_start(out=ident[:], in_=ident_in), 16)
                chs.inc(sy.dma_start(out=onesrow[:], in_=onesrow_in), 16)
                chs.inc(sy.dma_start(out=eexp[:], in_=eexp_in), 16)
                chs.inc(sy.dma_start(out=erow[:], in_=erow_in), 16)

            def step(eng_attr, fn):
                def body(e):
                    ch.wait(e)
                    chd.wait(e)
                    chs.wait(e)
                    fn(e)
                getattr(blk, eng_attr)(body)

            step("scalar", lambda e: ch.inc(e.activation(c_sb[:], c_sb[:], AF.Silu)))

            def _v0(v):
                ch.inc(v.memset(epsc[:], LN_EPS))
                ch.wait(v)
                ch.inc(v.memset(onescol[:], 1.0))
                ch.wait(v)
                ch.inc(v.memset(ones96[:], 1.0))
                ch.wait(v)
                ch.inc(v.memset(ksum_sb[:], 0.0))
                ch.wait(v)
                ch.inc(v.tensor_copy(out=sc_bf[:], in_=c_sb[:]))
            step("vector", _v0)

            def _t0(tn):
                for kc in range(8):
                    m = tn.matmul(bc_ps[:, kc:kc + 1], sc_bf[0:1, kc * 128:(kc + 1) * 128],
                                  onescol[0:1, 0:1], start=(kc == 0), stop=(kc == 7))
                ch.inc(m)
            step("tensor", _t0)
            step("vector", lambda v: ch.inc(v.tensor_copy(out=sc_f[:], in_=bc_ps[:, 0:8])))

            for half in range(2):
                def _gw(gp, half=half):
                    chd.inc(gp.dma_start(out=wmod_bf[:],
                                        in_=wmod_in[:, half * 3072:(half + 1) * 3072]
                                        .rearrange("(k p) n -> p k n", p=128)), 16)
                step("gpsimd", _gw)

                def _tm(tn, half=half):
                    for ns in range(6):
                        for kc in range(8):
                            m = tn.matmul(mod_ps[0:1, ns * 512:(ns + 1) * 512],
                                          sc_f[:, kc:kc + 1],
                                          wmod_bf[:, kc, ns * 512:(ns + 1) * 512],
                                          start=(kc == 0), stop=(kc == 7))
                    ch.inc(m)
                step("tensor", _tm)
                step("vector", lambda v, half=half: ch.inc(
                    v.tensor_copy(out=mod_bf[0:1, half * 3072:(half + 1) * 3072], in_=mod_ps[:])))

            items = [("sa", mod_bf[0:1, 0:D]), ("ca", mod_bf[0:1, D:2 * D]),
                     ("ga", mod_bf[0:1, 2 * D:3 * D]), ("sb", mod_bf[0:1, 3 * D:4 * D]),
                     ("cb", mod_bf[0:1, 4 * D:5 * D]), ("gb", mod_bf[0:1, 5 * D:6 * D]),
                     ("gateb", gb_row[0:1, 0:NE])]
            for name, src in items:
                def _tb(tn, src=src):
                    n = src.shape[-1]
                    for ns in range((n + 511) // 512):
                        w = min(512, n - ns * 512)
                        m = tn.matmul(bc_ps[:, ns * 512:ns * 512 + w], onesrow[0:1, :],
                                      src[0:1, ns * 512:ns * 512 + w], start=True, stop=True)
                    ch.inc(m)
                step("tensor", _tb)
                if name in ("ga", "gb"):
                    step("scalar", lambda s: ch.inc(s.activation(amx_t[:], bc_ps[:], AF.Abs)))

                def _vb(v, name=name):
                    if name == "sa":
                        ch.inc(v.tensor_copy(out=sa_b[:], in_=bc_ps[:]))
                    elif name == "ca":
                        ch.inc(v.tensor_scalar(out=ca1_b[:], in0=bc_ps[:], scalar1=1.0,
                                               scalar2=None, op0=ALU.add))
                    elif name in ("ga", "gb"):
                        dst = ga_b if name == "ga" else gb_b
                        ch.inc(v.tensor_scalar(out=sgn_t[:], in0=bc_ps[:], scalar1=0.0,
                                               scalar2=2.0, op0=ALU.is_ge, op1=ALU.mult))
                        ch.wait(v)
                        ch.inc(v.tensor_scalar(out=sgn_t[:], in0=sgn_t[:], scalar1=-1.0,
                                               scalar2=None, op0=ALU.add))
                        ch.wait(v)
                        ch.inc(v.tensor_scalar(out=amx_t[:], in0=amx_t[:], scalar1=0.25,
                                               scalar2=None, op0=ALU.max))
                        ch.wait(v)
                        ch.inc(v.tensor_mul(out=dst[:], in0=sgn_t[:], in1=amx_t[:]))
                        if name == "ga":
                            ch.wait(v)
                            ch.inc(v.tensor_mul(out=A_b[:], in0=ga_b[:], in1=ca1_b[:]))
                            ch.wait(v)
                            ch.inc(v.tensor_mul(out=B_b[:], in0=ga_b[:], in1=sa_b[:]))
                    elif name == "sb":
                        ch.inc(v.tensor_copy(out=sbb_b[:], in_=bc_ps[:]))
                    elif name == "cb":
                        ch.inc(v.tensor_scalar(out=cb1_b[:], in0=bc_ps[:], scalar1=1.0,
                                               scalar2=None, op0=ALU.add))
                    elif name == "gateb":
                        ch.inc(v.tensor_copy(out=gateb_b[:], in_=bc_ps[:, 0:NE]))
                step("vector", _vb)

            for seg in range(6):
                def _tbw(tn, seg=seg):
                    for kc in range(8):
                        m = tn.matmul(bw_ps[0:1, :], betaT[:, kc:kc + 1],
                                      wqkv_sb[:, kc, seg * 512:(seg + 1) * 512],
                                      start=(kc == 0), stop=(kc == 7))
                    ch.inc(m)
                step("tensor", _tbw)
                dst = (bw_a if seg < 3 else bw_b)
                row = (seg % 3) * 32
                step("scalar", lambda s, dst=dst, row=row: ch.inc(
                    s.copy(out=dst[row:row + 1, 0:512], in_=bw_ps[0:1, :])))
            for kc in range(8):
                def _sgw(s, kc=kc):
                    ch.inc(s.activation(wqkv_sb[:, kc, :], wqkv_sb[:, kc, :], AF.Identity,
                                        scale=gammaT[:, kc:kc + 1]))
                step("scalar", _sgw)


    with ExitStack() as es1:
        sb1 = lambda n, s, dt: es1.enter_context(nc.sbuf_tensor(n, s, dt))
        ps1 = lambda n, s: es1.enter_context(nc.psum_tensor(n, s, F32))
        x_sb = sb1("x_sb", [128, 2, TPC, D], BF16)
        xn_t = sb1("xn_t", [128, 2, TPC, D], BF16)
        xn_f = sb1("xn_f", [128, 2, 8, CT], BF16)
        q_sb = sb1("q_sb", [128, 2, TPC, D], BF16)
        k_sb = sb1("k_sb", [128, D], BF16)
        v_sb = sb1("v_sb", [128, D], BF16)
        e1k = sb1("e1k", [128, D], BF16)
        t1b = sb1("t1b", [128, 2, D], BF16)
        h_t = sb1("h_t", [128, TPC, D], BF16)
        st_s = sb1("st_s", [128, 32], F32)
        bnx = sb1("bnx", [128, 2, TPC, 12], F32)
        mvx = sb1("mvx", [128, 2, TPC, 2], F32)
        bnh = sb1("bnh", [128, TPC, 12], F32)
        mvh = sb1("mvh", [128, TPC, 2], F32)
        rsh = sb1("rsh", [128, TPC], F32)
        nmh = sb1("nmh", [128, TPC], F32)
        K_ps = ps1("K_ps", [128, D])
        V_ps = ps1("V_ps", [128, D])
        Q_ps = ps1("Q_ps", [128, D])
        KV_ps = ps1("KV_ps", [128, NPAIR * 128])

        xld = [S("xld0"), S("xld1")]
        xnst = [S("xnst0"), S("xnst1")]
        xnfl = [S("xnfl0"), S("xnfl1")]
        qspl = [S("qspl0"), S("qspl1")]
        v1 = S("v1")
        s1 = S("s1")
        g1 = S("g1")
        t1 = S("t1")
        kmm = [0] * NT; vmm = [0] * NT; qmm = [0] * NT; gmm = [0] * NT
        gsum = [0] * NT
        kfin = [0] * NT; vcp = [0] * NT; qfin = [0] * NT; ksev = [0] * NT
        ek_mk = [0] * NT; eq_mk = [0] * NT
        xnf_ok = [0] * NCH
        xld_mk = [0] * (NCH + 2)
        xnst_mk = [0] * (NCH + 2)
        qspl_mk = [0] * (NCH + 2)
        t1last = [0] * (NCH + 1)
        t2last = [0] * (NCH + 1)
        t2mks_h = [0] * (NCH + 1)
        aggx_mk = [0] * (NCH + 1)
        rsx_mark = [0] * (NCH + 1)
        nmx_mark = [0] * (NCH + 1)
        aggh_mk = [0] * (NCH + 1)
        rsh_mark = [0] * (NCH + 1)
        nmh_mark = [0] * (NCH + 1)

        with nc.Block("p1") as blk:
            def _gx0(gp):
                xld_mk[0] = xld[0].inc(gp.dma_start(out=x_sb[:, 0], in_=r3(x_in[tok(0)])), 16)
                if NCH > 1:
                    xld_mk[1] = xld[1].inc(gp.dma_start(out=x_sb[:, 1], in_=r3(x_in[tok(1)])), 16)
            blk.gpsimd(_gx0)
            castq = [(w13bf[e], w13T_in[e]) for e in range(NEX)] + \
                    [(w2bf[e], w2T_in[e]) for e in range(NEX)]

            def emit_mm_kv(c, t, sq):
                g = c * TPC + t
                par = c % 2

                def _t(tn):
                    if t == 0:
                        tn.wait_ge(xnfl[par].h, xnf_ok[c])
                    if g > 0:
                        tn.wait_ge(v1.h, ksev[g - 1])
                    for kc in range(8):
                        for hh in range(2):
                            tn.matmul(K_ps[:, hh * 512:(hh + 1) * 512],
                                      xn_f[:, par, kc, t * 128:(t + 1) * 128],
                                      wqkv_sb[:, kc, D + hh * 512:D + (hh + 1) * 512],
                                      start=(kc == 0), stop=False)
                    for hh in range(2):
                        m = tn.matmul(K_ps[:, hh * 512:(hh + 1) * 512],
                                      ones96[64 * (1 - hh):64 * (1 - hh) + 1, 0:128],
                                      (bw_a[64:65, 0:512] if hh == 0 else bw_b[0:1, 0:512]),
                                      start=False, stop=True)
                    kmm[g] = t1.inc(m)
                    for kc in range(8):
                        for hh in range(2):
                            tn.matmul(V_ps[:, hh * 512:(hh + 1) * 512],
                                      xn_f[:, par, kc, t * 128:(t + 1) * 128],
                                      wqkv_sb[:, kc, 2 * D + hh * 512:2 * D + (hh + 1) * 512],
                                      start=(kc == 0), stop=False)
                    for hh in range(2):
                        m = tn.matmul(V_ps[:, hh * 512:(hh + 1) * 512],
                                      ones96[32 * (1 + hh):32 * (1 + hh) + 1, 0:128],
                                      bw_b[32 * (1 + hh):32 * (1 + hh) + 1, 0:512],
                                      start=False, stop=True)
                    vmm[g] = t1.inc(m)
                blk.tensor(_t)

                def _sk(s, g=g):
                    s.wait_ge(t1.h, kmm[g])
                    if g > 0:
                        s.wait_ge(v1.h, qfin[g - 1])
                    return s.activation(e1k[:], K_ps[:], AF.Exp)
                ek_mk[g] = sq.add(_sk)

            def emit_v_elukv(c, t):
                g = c * TPC + t

                def _v(v):
                    v.wait_ge(s1.h, ek_mk[g])
                    v1.inc(v.tensor_scalar(out=e1k[:], in0=e1k[:], scalar1=1.0,
                                           scalar2=None, op0=ALU.min))
                    v.wait_ge(v1.h, v1.n)
                    if g > 0:
                        v.wait_ge(t1.h, gmm[g - 1])
                    kfin[g] = v1.inc(v.scalar_tensor_tensor(out=k_sb[:], in0=K_ps[:], scalar=1.0,
                                                            in1=e1k[:], op0=ALU.add, op1=ALU.max))
                    v.wait_ge(t1.h, vmm[g])
                    vcp[g] = v1.inc(v.tensor_copy(out=v_sb[:], in_=V_ps[:]))
                blk.vector(_v)

            def emit_v_ksev(c, t):
                g = c * TPC + t

                def _v2(v):
                    v.wait_ge(t1.h, gsum[g])
                    ksev[g] = v1.inc(v.tensor_add(out=ksum_sb[:], in0=ksum_sb[:],
                                                  in1=K_ps[:, 0:NPAIR]))
                blk.vector(_v2)

            def emit_t_gram(c, t):
                g = c * TPC + t

                def _t2(tn):
                    tn.wait_ge(v1.h, vcp[g])
                    tn.wait_ge(v1.h, kfin[g])
                    for p in range(NPAIR):
                        m = tn.matmul(K_ps[:, p:p + 1],
                                      k_sb[:, p * 128:(p + 1) * 128], onescol[:],
                                      start=(p == 0), stop=(p == NPAIR - 1))
                    gsum[g] = t1.inc(m)
                    for p in range(NPAIR):
                        m = tn.matmul(KV_ps[:, p * 128:(p + 1) * 128],
                                      k_sb[:, p * 128:(p + 1) * 128],
                                      v_sb[:, p * 128:(p + 1) * 128],
                                      start=(g == 0 and p % 4 == 0), stop=(g == NT - 1 and p % 4 == 3))
                    gmm[g] = t1.inc(m)
                blk.tensor(_t2)

            def emit_t_q(c, t, sq):
                g = c * TPC + t
                par = c % 2

                def _t3(tn):
                    if g > 0:
                        tn.wait_ge(v1.h, qfin[g - 1])
                    for kc in range(8):
                        for hh in range(2):
                            tn.matmul(Q_ps[:, hh * 512:(hh + 1) * 512],
                                      xn_f[:, par, kc, t * 128:(t + 1) * 128],
                                      wqkv_sb[:, kc, hh * 512:(hh + 1) * 512],
                                      start=(kc == 0), stop=False)
                    for hh in range(2):
                        m = tn.matmul(Q_ps[:, hh * 512:(hh + 1) * 512],
                                      ones96[32 * hh:32 * hh + 1, 0:128],
                                      bw_a[32 * hh:32 * hh + 1, 0:512],
                                      start=False, stop=True)
                    qmm[g] = t1.inc(m)
                blk.tensor(_t3)

                def _sq_(s, g=g):
                    s.wait_ge(t1.h, qmm[g])
                    s.wait_ge(v1.h, kfin[g])
                    return s.activation(e1k[:], Q_ps[:], AF.Exp)
                eq_mk[g] = sq.add(_sq_)

            def emit_v_eluq(c, t):
                g = c * TPC + t

                def _v(v):
                    v.wait_ge(s1.h, eq_mk[g])
                    v1.inc(v.tensor_scalar(out=e1k[:], in0=e1k[:], scalar1=1.0,
                                           scalar2=None, op0=ALU.min))
                    v.wait_ge(v1.h, v1.n)
                    if c >= 2:
                        v.wait_ge(qspl[c % 2].h, qspl_mk[c - 2])
                    qfin[g] = v1.inc(v.scalar_tensor_tensor(out=q_sb[:, c % 2, t, :], in0=Q_ps[:],
                                                            scalar=1.0, in1=e1k[:],
                                                            op0=ALU.add, op1=ALU.max))
                blk.vector(_v)

            def emit_lnA(ca, sq):
                parA = ca % 2

                def _vbn(v, ca=ca, parA=parA):
                    v.wait_ge(xld[parA].h, xld_mk[ca])
                    if ca >= 2:
                        v.wait_ge(s1.h, rsx_mark[ca - 2])
                    for t in range(TPC):
                        v1.inc(v.bn_stats(out=bnx[:, parA, t, 0:6],
                                          in_=x_sb[:, parA, t, 0:512]))
                        v1.inc(v.bn_stats(out=bnx[:, parA, t, 6:12],
                                          in_=x_sb[:, parA, t, 512:1024]))
                    v.wait_ge(v1.h, v1.n)
                    for t in range(TPC):
                        v1.inc(v.bn_aggr(out=mvx[:, parA, t, :], in_=bnx[:, parA, t, :]))
                    aggx_mk[ca] = v1.n
                blk.vector(_vbn)

                def _srs(s, ca=ca, parA=parA):
                    s.wait_ge(v1.h, aggx_mk[ca])
                    for t in range(TPC):
                        m = s.activation(mvx[:, parA, t, 1:2], mvx[:, parA, t, 1:2],
                                         AF.Sqrt, bias=epsc[:, 0:1])
                    return m
                rsx_mark[ca] = sq.add(_srs, inc=4)

                def _vnm(v, ca=ca, parA=parA):
                    v.wait_ge(s1.h, rsx_mark[ca])
                    for t in range(TPC):
                        g = ca * TPC + t
                        v1.inc(v.reciprocal(rs1[:, g:g + 1], mvx[:, parA, t, 1:2]))
                    v.wait_ge(v1.h, v1.n)
                    for t in range(TPC):
                        g = ca * TPC + t
                        v1.inc(v.tensor_scalar(out=nmurs1[:, g:g + 1], in0=mvx[:, parA, t, 0:1],
                                               scalar1=rs1[:, g:g + 1], scalar2=-1.0,
                                               op0=ALU.mult, op1=ALU.mult))
                    nmx_mark[ca] = v1.n
                blk.vector(_vnm)

            for it in range(NCH + 1):
                cc, c = it, it - 1
                par = cc % 2
                B0 = 16 * par
                sq = SQueue(s1)
                if it == 0:
                    emit_lnA(0, sq)

                if cc < NCH:
                    t1mk = {}
                    hmk = {}
                    for t in range(TPC):
                        def _t1(s, cc=cc, t=t, hmk=hmk):
                            s.wait_ge(v1.h, nmx_mark[cc])
                            if t > 1:
                                s.wait_ge(v1.h, hmk[t - 2])
                            g = cc * TPC + t
                            return s.activation(t1b[:, t % 2, :], x_sb[:, cc % 2, t, :], AF.Identity,
                                                bias=nmurs1[:, g:g + 1], scale=rs1[:, g:g + 1])
                        t1mk[t] = sq.add(_t1)
                    t1last[cc] = t1mk[TPC - 1]

                    def _vh01(v, cc=cc, t1mk=t1mk, hmk=hmk):
                        for t in range(2):
                            v.wait_ge(s1.h, t1mk[t])
                            if cc >= 2 and t == 0:
                                v.wait_ge(s1.h, t2last[cc - 2])
                            v1.inc(v.tensor_mul(out=h_t[:, t, :], in0=t1b[:, t % 2, :],
                                                in1=ca1_b[:]))
                            v.wait_ge(v1.h, v1.n)
                            hmk[t] = v1.inc(v.tensor_add(out=h_t[:, t, :], in0=h_t[:, t, :],
                                                         in1=sa_b[:]))
                            v.wait_ge(v1.h, v1.n)
                            v1.inc(v.bn_stats(out=bnh[:, t, 0:6], in_=h_t[:, t, 0:512]))
                            v1.inc(v.bn_stats(out=bnh[:, t, 6:12], in_=h_t[:, t, 512:1024]))
                    blk.vector(_vh01)

                if c >= 0:
                    emit_mm_kv(c, 0, sq)
                    emit_v_elukv(c, 0)
                    emit_t_gram(c, 0)
                    emit_v_ksev(c, 0)
                    emit_t_q(c, 0, sq)
                    emit_v_eluq(c, 0)

                if cc < NCH:
                    def _vh23(v, cc=cc, t1mk=t1mk, hmk=hmk):
                        for t in range(2, TPC):
                            v.wait_ge(s1.h, t1mk[t])
                            v1.inc(v.tensor_mul(out=h_t[:, t, :], in0=t1b[:, t % 2, :],
                                                in1=ca1_b[:]))
                            v.wait_ge(v1.h, v1.n)
                            hmk[t] = v1.inc(v.tensor_add(out=h_t[:, t, :], in0=h_t[:, t, :],
                                                         in1=sa_b[:]))
                            v.wait_ge(v1.h, v1.n)
                            v1.inc(v.bn_stats(out=bnh[:, t, 0:6], in_=h_t[:, t, 0:512]))
                            v1.inc(v.bn_stats(out=bnh[:, t, 6:12], in_=h_t[:, t, 512:1024]))
                        v.wait_ge(v1.h, v1.n)
                        if cc >= 1:
                            v.wait_ge(s1.h, rsh_mark[cc - 1])
                        for t in range(TPC):
                            v1.inc(v.bn_aggr(out=mvh[:, t, :], in_=bnh[:, t, :]))
                        aggh_mk[cc] = v1.n
                    blk.vector(_vh23)

                if cc < NCH:
                    def _srs2(s, cc=cc):
                        s.wait_ge(v1.h, aggh_mk[cc])
                        for t in range(TPC):
                            m = s.activation(mvh[:, t, 1:2], mvh[:, t, 1:2],
                                             AF.Sqrt, bias=epsc[:, 0:1])
                        return m
                    rsh_mark[cc] = sq.add(_srs2, inc=4)

                    def _vnmh(v, cc=cc):
                        v.wait_ge(s1.h, rsh_mark[cc])
                        if cc >= 1:
                            v.wait_ge(s1.h, t2last[cc - 1])
                        for t in range(TPC):
                            v1.inc(v.reciprocal(rsh[:, t:t + 1], mvh[:, t, 1:2]))
                        v.wait_ge(v1.h, v1.n)
                        for t in range(TPC):
                            v1.inc(v.tensor_scalar(out=nmh[:, t:t + 1], in0=mvh[:, t, 0:1],
                                                   scalar1=rsh[:, t:t + 1], scalar2=-1.0,
                                                   op0=ALU.mult, op1=ALU.mult))
                        nmh_mark[cc] = v1.n
                    blk.vector(_vnmh)

                if c >= 0:
                    emit_mm_kv(c, 1, sq)
                    emit_v_elukv(c, 1)
                    emit_t_gram(c, 1)
                    emit_v_ksev(c, 1)
                    emit_t_q(c, 1, sq)
                    emit_v_eluq(c, 1)

                if cc < NCH:
                    t2mks = {}
                    for t in range(TPC):
                        def _t2(s, cc=cc, t=t, par=par):
                            s.wait_ge(v1.h, nmh_mark[cc])
                            if t == 0 and cc >= 2:
                                s.wait_ge(xnst[par].h, xnst_mk[cc - 2])
                            return s.activation(xn_t[:, par, t, :], h_t[:, t, :], AF.Identity,
                                                bias=nmh[:, t:t + 1],
                                                scale=rsh[:, t:t + 1])
                        t2mks[t] = sq.add(_t2)
                    t2last[cc] = t2mks[TPC - 1]
                    t2mks_h[cc] = t2mks[TPC - 1]

                    def _sspq(s, cc=cc, par=par):
                        s.wait_ge(s1.h, t2mks_h[cc])
                        s.wait_ge(xnst[par].h, xnst[par].n)
                        xnst_mk[cc] = xnst[par].inc(
                            s.dma_start(out=r3(xn_scr[tok(cc)]), in_=xn_t[:, par]), 16)
                    sq.add(_sspq, inc=0)

                if c >= 0:
                    emit_mm_kv(c, 2, sq)
                    emit_v_elukv(c, 2)
                    emit_t_gram(c, 2)
                    emit_v_ksev(c, 2)
                    emit_t_q(c, 2, sq)
                    emit_v_eluq(c, 2)

                if cc + 1 < NCH:
                    emit_lnA(cc + 1, sq)

                if c >= 0:
                    emit_mm_kv(c, 3, sq)
                    emit_v_elukv(c, 3)
                    emit_t_gram(c, 3)
                    emit_v_ksev(c, 3)
                    emit_t_q(c, 3, sq)
                    emit_v_eluq(c, 3)

                sq.flush(blk)

                if cc < NCH:
                    def _syt(sy, cc=cc, par=par):
                        sy.wait_ge(xnst[par].h, xnst_mk[cc])
                        if cc >= 2:
                            sy.wait_ge(t1.h, qmm[(cc - 2) * TPC + TPC - 1])
                        sy.wait_ge(xnfl[par].h, xnfl[par].n)
                        for kc in range(8):
                            m = sy.dma_start_transpose(out=xn_f[:, par, kc, :],
                                                       in_=xn_scr[tok(cc), kc * 128:(kc + 1) * 128])
                            xnfl[par].inc(m, 16)
                        xnf_ok[cc] = xnfl[par].n
                    blk.sync(_syt)

                    def _gxl(gp, cc=cc, par=par):
                        if cc + 2 < NCH:
                            gp.wait_ge(s1.h, t1last[cc])
                            gp.wait_ge(xld[par].h, xld[par].n)
                            xld_mk[cc + 2] = xld[par].inc(
                                gp.dma_start(out=x_sb[:, par], in_=r3(x_in[tok(cc + 2)])), 16)
                        for _ in range(3):
                            if castq:
                                o, i = castq.pop(0)
                                wcast.inc(gp.dma_start(out=o, in_=i), 16)
                    blk.gpsimd(_gxl)

                if c >= 0:
                    def _sq2(sc, c=c):
                        sc.wait_ge(v1.h, qfin[c * TPC + TPC - 1])
                        sc.wait_ge(qspl[c % 2].h, qspl[c % 2].n)
                        qspl_mk[c] = qspl[c % 2].inc(
                            sc.dma_start(out=r3(q_scr[tok(c)]), in_=q_sb[:, c % 2]), 16)
                    blk.scalar(_sq2)

            if castq:
                def _gcrest(gp):
                    for o, i in castq:
                        wcast.inc(gp.dma_start(out=o, in_=i), 16)
                    castq.clear()
                blk.gpsimd(_gcrest)

            def _vprep(v):
                v.wait_ge(t1.h, gmm[NT - 1])
                v.wait_ge(v1.h, ksev[NT - 1])
                v1.inc(v.memset(kv_sb[:], 0.0))
                v1.inc(v.memset(ksbd[:], 0.0))
                v.wait_ge(v1.h, v1.n)
                for p in range(NPAIR):
                    v1.inc(v.tensor_copy(out=kv_sb[0:64, p * 128:p * 128 + 64],
                                         in_=KV_ps[0:64, p * 128:p * 128 + 64]))
                    v1.inc(v.tensor_copy(out=kv_sb[64:128, p * 128 + 64:p * 128 + 128],
                                         in_=KV_ps[64:128, p * 128 + 64:p * 128 + 128]))
                    v1.inc(v.tensor_copy(out=ksbd[0:64, p * NH + 2 * p:p * NH + 2 * p + 1],
                                         in_=ksum_sb[0:64, p:p + 1]))
                    v1.inc(v.tensor_copy(out=ksbd[64:128, p * NH + 2 * p + 1:p * NH + 2 * p + 2],
                                         in_=ksum_sb[64:128, p:p + 1]))
            blk.vector(_vprep)

    es_wq.close()

    with ExitStack() as es2:
        sb2 = lambda n, s, dt: es2.enter_context(nc.sbuf_tensor(n, s, dt))
        ps2 = lambda n, s: es2.enter_context(nc.psum_tensor(n, s, F32))
        x_sb = sb2("x_sb2", [128, 2, TPC, D], F32)
        qT = sb2("qT", [128, 2, 8, CT], BF16)
        attn_f = sb2("attn_f", [128, NPAIR, CT], BF16)
        rb_sb = sb2("rb_sb", [128, 2, CT], F32)
        de_sb = sb2("de_sb", [NH, CT], F32)
        rcp_bf = sb2("rcp_bf", [NH, CT], BF16)
        t1b2 = sb2("t1b2", [128, D], BF16)
        t3b2 = sb2("t3b2", [128, D], BF16)
        u_b = sb2("u_b", [128, D], BF16)
        u2_b = sb2("u2_b", [128, D], BF16)
        r_b = sb2("r_b", [128, D], F32)
        vv_b = sb2("vv_b", [128, D], F32)
        x2_sb = sb2("x2_sb", [128, TPC, D], F32)
        h2A = sb2("h2A", [128, D], BF16)
        h2_t = sb2("h2_t", [128, TPC, D], BF16)
        st2 = sb2("st2", [128, 16], F32)
        bn3 = sb2("bn3", [128, TPC, 12], F32)
        mv3 = sb2("mv3", [128, TPC, 2], F32)
        DEN_ps = ps2("DEN_ps", [NH, CT])
        RB_ps = ps2("RB_ps", [128, 2 * CT])
        QO_ps = ps2("QO_ps", [128, 2 * CT])
        WO_ps = ps2("WO_ps", [128, D])

        xld2 = [S("xld2a"), S("xld2b")]
        qTl = [S("qTla"), S("qTlb")]
        xld2_mk = [0] * (NCH + 2)
        qTl_mk = [0] * (NCH + 2)
        v2 = S("v2")
        s2 = S("s2")
        g2 = S("g2")
        t2 = S("t2")
        x2st = S("x2st")
        h2st = S("h2st")
        den_free = [0] * NCH
        qT_free = [0] * NCH
        x_free_v = [0] * NCH
        x_free_s = [0] * NCH
        x_free_g = [0] * NCH
        wo_free = [0] * NT
        r_done = [0] * NT
        h2a_done = [0] * NT
        wo_last_ldw = [0] * NCH
        agg3_mk = [0] * NCH
        rs3q_mark = [0] * NCH
        nm3_mark = [0] * NCH

        with nc.Block("p2a") as blk:
            def _sy0(sy):
                qspl[0].wait(sy)
                qspl[1].wait(sy)
                xld2_mk[0] = xld2[0].inc(sy.dma_start(out=x_sb[:, 0], in_=r3(x_in[tok(0)])), 16)
                for kc in range(8):
                    qTl[0].inc(sy.dma_start_transpose(out=qT[:, 0, kc, :],
                                                      in_=q_scr[tok(0), kc * 128:(kc + 1) * 128]), 16)
                qTl_mk[0] = qTl[0].n
            blk.sync(_sy0)

            for c in range(NCH):
                par = c % 2

                def _syc(sy, c=c):
                    if c + 1 < NCH:
                        if c >= 1:
                            sy.wait_ge(v2.h, x_free_v[c - 1])
                            sy.wait_ge(s2.h, x_free_s[c - 1])
                            sy.wait_ge(g2.h, x_free_g[c - 1])
                            sy.wait_ge(t2.h, qT_free[c - 1])
                        sy.wait_ge(xld2[(c + 1) % 2].h, xld2[(c + 1) % 2].n)
                        sy.wait_ge(qTl[(c + 1) % 2].h, qTl[(c + 1) % 2].n)
                        xld2_mk[c + 1] = xld2[(c + 1) % 2].inc(
                            sy.dma_start(out=x_sb[:, (c + 1) % 2], in_=r3(x_in[tok(c + 1)])), 16)
                        for kc in range(8):
                            qTl[(c + 1) % 2].inc(
                                sy.dma_start_transpose(out=qT[:, (c + 1) % 2, kc, :],
                                                       in_=q_scr[tok(c + 1), kc * 128:(kc + 1) * 128]), 16)
                        qTl_mk[c + 1] = qTl[(c + 1) % 2].n
                blk.sync(_syc)

                def _td(tn, c=c):
                    tn.wait_ge(qTl[c % 2].h, qTl_mk[c])
                    if c >= 1:
                        tn.wait_ge(v2.h, den_free[c - 1])
                    for p in range(NPAIR):
                        m = tn.matmul(DEN_ps[:], ksbd[:, p * NH:(p + 1) * NH],
                                      qT[:, par, p, :], start=(p == 0), stop=(p == NPAIR - 1))
                    t2.inc(m)
                blk.tensor(_td)
                tden = t2.n

                def _vd(v, c=c, tden=tden):
                    v.wait_ge(t2.h, tden)
                    den_free[c] = v2.inc(v.tensor_scalar(out=de_sb[:], in0=DEN_ps[:],
                                                         scalar1=DEN_EPS, scalar2=None, op0=ALU.add))
                    v.wait_ge(v2.h, v2.n)
                    v2.inc(v.reciprocal(de_sb[:], de_sb[:]))
                    v.wait_ge(v2.h, v2.n)
                    v2.inc(v.tensor_copy(out=rcp_bf[:], in_=de_sb[:]))
                blk.vector(_vd)
                vrcp = v2.n

                sq2 = SQueue(s2)
                rbev = [0] * NPAIR
                aev = [0] * NPAIR
                for p in range(NPAIR):
                    def _tp(tn, p=p, c=c):
                        if p == 0:
                            tn.wait_ge(v2.h, vrcp)
                        if p >= 2:
                            tn.wait_ge(s2.h, rbev[p - 2])
                        m = tn.matmul(RB_ps[:, (p % 2) * CT:(p % 2 + 1) * CT],
                                      eexp[:, p * 128:(p + 1) * 128], rcp_bf[:],
                                      start=True, stop=True)
                        t2.inc(m)
                        if p >= 2:
                            tn.wait_ge(v2.h, aev[p - 2])
                        if c >= 1 and p == 0:
                            tn.wait_ge(v2.h, wo_free[c * TPC - 1])
                        m = tn.matmul(QO_ps[:, (p % 2) * CT:(p % 2 + 1) * CT],
                                      kv_sb[:, p * 128:(p + 1) * 128],
                                      qT[:, par, p, :], start=True, stop=True)
                        t2.inc(m)
                    blk.tensor(_tp)
                    tpp = t2.n

                    def _srb(s, p=p, tpp=tpp):
                        s.wait_ge(t2.h, tpp - 1)
                        if p >= 2:
                            s.wait_ge(v2.h, aev[p - 2])
                        return s.copy(out=rb_sb[:, p % 2, :],
                                      in_=RB_ps[:, (p % 2) * CT:(p % 2 + 1) * CT])
                    rbev[p] = sq2.add(_srb)

                    def _vp(v, p=p, tpp=tpp):
                        v.wait_ge(t2.h, tpp)
                        v.wait_ge(s2.h, rbev[p])
                        aev[p] = v2.inc(v.tensor_mul(out=attn_f[:, p, :],
                                                     in0=QO_ps[:, (p % 2) * CT:(p % 2 + 1) * CT],
                                                     in1=rb_sb[:, p % 2, :]))
                    blk.vector(_vp)
                qT_free[c] = t2.n

                twms = {}
                vx2 = {}
                gus = {}
                t3mks = {}
                vh2m = {}
                for t in range(TPC):
                    gt = c * TPC + t

                    def _tw(tn, t=t, gt=gt):
                        if t == 0:
                            tn.wait_ge(v2.h, aev[NPAIR - 1])
                        if gt > 0:
                            tn.wait_ge(v2.h, wo_free[gt - 1])
                        for kc in range(8):
                            for hh in range(2):
                                m = tn.matmul(WO_ps[:, hh * 512:(hh + 1) * 512],
                                              attn_f[:, kc, t * 128:(t + 1) * 128],
                                              wout_sb[:, kc, hh * 512:(hh + 1) * 512],
                                              start=(kc == 0), stop=(kc == 7))
                        t2.inc(m)
                    blk.tensor(_tw)
                    twms[t] = t2.n

                    def _sep(s, c=c, t=t, gt=gt):
                        s.wait_ge(xld2[c % 2].h, xld2_mk[c])
                        if t > 0:
                            s.wait_ge(g2.h, gus[t - 1])
                        return s.activation(t1b2[:], x_sb[:, c % 2, t, :], AF.Identity,
                                            bias=nmurs1[:, gt:gt + 1], scale=rs1[:, gt:gt + 1])
                    st1 = sq2.add(_sep)

                    def _gep(gp, c=c, gt=gt, st1=st1, t=t):
                        gp.wait_ge(s2.h, st1)
                        if gt > 0:
                            gp.wait_ge(v2.h, r_done[gt - 1])
                        g2.inc(gp.tensor_mul(out=u_b[:], in0=t1b2[:], in1=A_b[:]))
                        gp.wait_ge(g2.h, g2.n)
                        g2.inc(gp.tensor_add(out=u2_b[:], in0=u_b[:], in1=B_b[:]))
                        gp.wait_ge(g2.h, g2.n)
                        gus[t] = g2.inc(gp.tensor_add(out=r_b[:], in0=u2_b[:],
                                                      in1=x_sb[:, c % 2, t, :]))
                    blk.gpsimd(_gep)

                    def _vep(v, c=c, t=t, gt=gt):
                        v.wait_ge(t2.h, twms[t])
                        wo_free[gt] = v2.inc(v.tensor_mul(out=vv_b[:], in0=WO_ps[:], in1=ga_b[:]))
                        v.wait_ge(v2.h, v2.n)
                        v.wait_ge(g2.h, gus[t])
                        if c >= 1 and t == 0:
                            v.wait_ge(x2st.h, 16 * c)
                        r_done[gt] = v2.inc(v.tensor_add(out=x2_sb[:, t, :], in0=r_b[:],
                                                         in1=vv_b[:]))
                        v.wait_ge(v2.h, v2.n)
                        v2.inc(v.bn_stats(out=bn3[:, t, 0:6], in_=x2_sb[:, t, 0:512]))
                        vx2[t] = v2.inc(v.bn_stats(out=bn3[:, t, 6:12], in_=x2_sb[:, t, 512:1024]))
                    blk.vector(_vep)

                def _vagg3(v, c=c):
                    v.wait_ge(v2.h, vx2[TPC - 1])
                    if c >= 1:
                        v.wait_ge(s2.h, rs3q_mark[c - 1])
                    for t in range(TPC):
                        v2.inc(v.bn_aggr(out=mv3[:, t, :], in_=bn3[:, t, :]))
                    agg3_mk[c] = v2.n
                blk.vector(_vagg3)

                def _srs3(s, c=c):
                    s.wait_ge(v2.h, agg3_mk[c])
                    for t in range(TPC):
                        m = s.activation(mv3[:, t, 1:2], mv3[:, t, 1:2],
                                         AF.Sqrt, bias=epsc[:, 0:1])
                    return m
                rs3q_mark[c] = sq2.add(_srs3, inc=4)

                def _vnm3(v, c=c):
                    v.wait_ge(s2.h, rs3q_mark[c])
                    for t in range(TPC):
                        gt = c * TPC + t
                        v2.inc(v.reciprocal(rs3[:, gt:gt + 1], mv3[:, t, 1:2]))
                    v.wait_ge(v2.h, v2.n)
                    for t in range(TPC):
                        gt = c * TPC + t
                        v2.inc(v.tensor_scalar(out=nmurs3[:, gt:gt + 1], in0=mv3[:, t, 0:1],
                                               scalar1=rs3[:, gt:gt + 1], scalar2=-1.0,
                                               op0=ALU.mult, op1=ALU.mult))
                    nm3_mark[c] = v2.n
                blk.vector(_vnm3)

                for t in range(TPC):
                    gt = c * TPC + t

                    def _sh2(s, c=c, t=t, gt=gt):
                        s.wait_ge(v2.h, nm3_mark[c])
                        if t > 0:
                            s.wait_ge(v2.h, vh2m[t - 1])
                        return s.activation(t3b2[:], x2_sb[:, t, :], AF.Identity,
                                            bias=nmurs3[:, gt:gt + 1], scale=rs3[:, gt:gt + 1])
                    t3mks[t] = sq2.add(_sh2)

                    def _vh2(v, c=c, t=t):
                        if t > 0:
                            v.wait_ge(v2.h, v2.n)
                        v.wait_ge(s2.h, t3mks[t])
                        vh2m[t] = v2.inc(v.tensor_mul(out=h2A[:], in0=t3b2[:], in1=cb1_b[:]))
                        v.wait_ge(v2.h, v2.n)
                        if c >= 1 and t == 0:
                            v.wait_ge(h2st.h, 16 * c)
                        v2.inc(v.tensor_add(out=h2_t[:, t, :], in0=h2A[:], in1=sbb_b[:]))
                    blk.vector(_vh2)

                sq2.flush(blk)

                x_free_v[c] = v2.n
                x_free_s[c] = s2.n
                x_free_g[c] = g2.n
                vln = v2.n
                gln = g2.n

                def _gspill(sc, c=c, vln=vln, gln=gln):
                    sc.wait_ge(v2.h, vln)
                    sc.wait_ge(x2st.h, x2st.n)
                    x2st.inc(sc.dma_start(out=r3(x2_scr[tok(c)]), in_=x2_sb[:]), 16)
                    sc.wait_ge(g2.h, gln)
                    sc.wait_ge(h2st.h, h2st.n)
                    h2st.inc(sc.dma_start(out=r3(h2_scr[tok(c)]), in_=h2_t[:]), 16)
                blk.scalar(_gspill)

            def _gfin(sc):
                x2st.wait(sc)
                h2st.wait(sc)
            blk.scalar(_gfin)

    es_wo.close()

    with ExitStack() as es3:
        sb3 = lambda n, s, dt: es3.enter_context(nc.sbuf_tensor(n, s, dt))
        ps3 = lambda n, s: es3.enter_context(nc.psum_tensor(n, s, F32))
        h2f = sb3("h2f", [128, 2, 8, CT], BF16)
        w13_sb = sb3("w13_sb", [128, 2, 8, D], BF16)
        w2_sb = sb3("w2_sb", [128, 2, 2, 4, D], BF16)
        act_sl = sb3("act_sl", [128, NSLOT, 4, CT], BF16)
        gbuf = sb3("gbuf", [128, 4, CT], BF16)
        cE = sb3("cE", [128, 2, CT], BF16)
        sc_all = sb3("sc_all", [128, TPC * NE], F32)
        bb_all = sb3("bb_all", [128, TPC * NE], F32)
        oh_all = sb3("oh_all", [128, TPC * NE], F32)
        pk_t = sb3("pk_t", [128, TPC * NE], F32)
        sel_pk = sb3("sel_pk", [128, TPC * NE], F32)
        mx_s = sb3("mx_s", [128, TPC], F32)
        ws_s = sb3("ws_s", [128, TPC], F32)
        comb_bf = sb3("comb_bf", [128, TPC * NE], BF16)
        comb_f = sb3("comb_f", [NH, CT], BF16)
        x2_sb3 = sb3("x2_sb3", [128, 2, TPC, D], F32)
        yf_b = sb3("yf_b", [128, TPC, D], F32)
        ob_b = sb3("ob_b", [128, TPC, D], F32)
        I1_ps = ps3("I1_ps", [128, 2 * 512])
        Y_ps = ps3("Y_ps", [128, 2 * D])
        CE_ps = ps3("CE_ps", [128, 2 * 512])

        h2l = [S("h2la"), S("h2lb")]
        w13l = [S("w13l0"), S("w13l1")]
        w2l = {(p, q): S(f"w2l{p}{q}") for p in range(2) for q in range(2)}
        x2l = [S("x2la"), S("x2lb")]
        h2l_mk = [0] * (NCH + 2)
        x2l_mk = [0] * (NCH + 2)
        w13_mk = {}
        w2_mk = {}
        v3 = S("v3")
        s3 = S("s3")
        g3 = S("g3")
        t3 = S("t3")
        outst = S("outst")
        i1done = {}
        tvec = {}
        actv = {}
        ymm = {}
        yev = {}
        ceev = {}
        ce_mm = {}
        gate_done = [0] * NCH
        gate_sig = [0] * NCH
        comb_done = [0] * NCH
        ct_free = [0] * NCH
        h2f_free = [0] * NCH
        x2_free = [0] * NCH
        vzmul = [0] * NCH
        w2slot_user = {}
        w13_user = {}

        eidx = lambda c, e: c * NEX + e

        with nc.Block("p2b") as blk:
            def _sy00(sy):
                wcast.wait(sy)
                h2st.wait(sy)
                for kc in range(8):
                    h2l[0].inc(sy.dma_start_transpose(out=h2f[:, 0, kc, :],
                                                      in_=h2_scr[tok(0), kc * 128:(kc + 1) * 128]), 16)
                h2l_mk[0] = h2l[0].n
            blk.sync(_sy00)

            def _gx0(gp):
                x2st.wait(gp)
                x2l_mk[0] = x2l[0].inc(gp.dma_start(out=x2_sb3[:, 0], in_=r3(x2_scr[tok(0)])), 16)
            blk.gpsimd(_gx0)

            tct_mm = [0] * NCH
            vct_mark = [0] * NCH

            def em_gate(c):
                def _tg(tn, c=c):
                    tn.wait_ge(h2l[c % 2].h, h2l_mk[c])
                    if c >= 1:
                        tn.wait_ge(v3.h, vct_mark[c - 1])
                        tn.wait_ge(s3.h, gate_sig[c - 1])
                    for t in range(TPC):
                        for kc in range(8):
                            m = tn.matmul(CE_ps[:, 512 + t * NE:512 + (t + 1) * NE],
                                          h2f[:, c % 2, kc, t * 128:(t + 1) * 128],
                                          gatew_sb[:, kc, :],
                                          start=(kc == 0), stop=(kc == 7))
                    gate_done[c] = t3.inc(m)
                blk.tensor(_tg)

            def em_sig(c, sq):
                def _sg(s, c=c):
                    s.wait_ge(t3.h, gate_done[c])
                    if c >= 1:
                        s.wait_ge(v3.h, comb_done[c - 1])
                    return s.activation(sc_all[:], CE_ps[:, 512:512 + TPC * NE], AF.Sigmoid)
                gate_sig[c] = sq.add(_sg)

            def em_topk(c, phase):
                def _v(v, c=c, phase=phase):
                    if phase == 0:
                        v.wait_ge(s3.h, gate_sig[c])
                        for t in range(TPC):
                            v3.inc(v.tensor_add(out=bb_all[:, t * NE:(t + 1) * NE],
                                                in0=sc_all[:, t * NE:(t + 1) * NE], in1=gateb_b[:]))
                        v.wait_ge(v3.h, v3.n)
                        its = [0]
                    elif phase == 1:
                        its = [1, 2]
                    else:
                        its = [3]
                    for it4 in its:
                        for t in range(TPC):
                            v3.inc(v.reduce_max(mx_s[:, t:t + 1], bb_all[:, t * NE:(t + 1) * NE],
                                                axis=AX.X))
                        v.wait_ge(v3.h, v3.n)
                        for t in range(TPC):
                            v3.inc(v.tensor_scalar(out=oh_all[:, t * NE:(t + 1) * NE],
                                                   in0=bb_all[:, t * NE:(t + 1) * NE],
                                                   scalar1=mx_s[:, t:t + 1], scalar2=0.0,
                                                   op0=ALU.subtract, op1=ALU.is_ge))
                        v.wait_ge(v3.h, v3.n)
                        for t in range(TPC):
                            sl = slice(t * NE, (t + 1) * NE)
                            if it4 == 0:
                                v3.inc(v.tensor_mul(out=sel_pk[:, sl], in0=oh_all[:, sl],
                                                    in1=sc_all[:, sl]))
                            else:
                                v3.inc(v.tensor_mul(out=pk_t[:, sl], in0=oh_all[:, sl],
                                                    in1=sc_all[:, sl]))
                        if it4 < 3:
                            for t in range(TPC):
                                sl = slice(t * NE, (t + 1) * NE)
                                v3.inc(v.scalar_tensor_tensor(out=bb_all[:, sl], in0=oh_all[:, sl],
                                                              scalar=-1e30, in1=bb_all[:, sl],
                                                              op0=ALU.mult, op1=ALU.add))
                        v.wait_ge(v3.h, v3.n)
                        if it4 > 0:
                            for t in range(TPC):
                                sl = slice(t * NE, (t + 1) * NE)
                                v3.inc(v.tensor_add(out=sel_pk[:, sl], in0=sel_pk[:, sl],
                                                    in1=pk_t[:, sl]))
                            v.wait_ge(v3.h, v3.n)
                    if phase == 2:
                        for t in range(TPC):
                            v3.inc(v.reduce_sum(ws_s[:, t:t + 1], sel_pk[:, t * NE:(t + 1) * NE],
                                                axis=AX.X))
                        v.wait_ge(v3.h, v3.n)
                        v3.inc(v.reciprocal(ws_s[:], ws_s[:]))
                        v.wait_ge(v3.h, v3.n)
                        if c >= 1:
                            v.wait_ge(t3.h, tct_mm[c - 1])
                        for t in range(TPC):
                            comb_done[c] = v3.inc(v.tensor_scalar(out=comb_bf[:, t * NE:(t + 1) * NE],
                                                                  in0=sel_pk[:, t * NE:(t + 1) * NE],
                                                                  scalar1=ws_s[:, t:t + 1],
                                                                  scalar2=ROUTE_SCALE,
                                                                  op0=ALU.mult, op1=ALU.mult))
                blk.vector(_v)

            def em_tct(c):
                def _t(tn, c=c):
                    tn.wait_ge(v3.h, comb_done[c])
                    for t in range(TPC):
                        m = tn.matmul(CE_ps[0:NH, 512 + t * 128:512 + (t + 1) * 128],
                                      comb_bf[:, t * NE:(t + 1) * NE], ident[:],
                                      start=True, stop=True)
                    tct_mm[c] = t3.inc(m)
                blk.tensor(_t)

            def em_vct(c):
                def _v(v, c=c):
                    v.wait_ge(t3.h, tct_mm[c])
                    if c >= 1:
                        v.wait_ge(t3.h, ce_mm.get(eidx(c - 1, NE - 1), 0))
                    vct_mark[c] = v3.inc(v.tensor_copy(out=comb_f[:], in_=CE_ps[0:NH, 512:1024]))
                blk.vector(_v)

            for c in range(NCH):
                par = c % 2

                def _gxp(gp, c=c):
                    if c + 1 < NCH:
                        if c >= 1:
                            gp.wait_ge(v3.h, x2_free[c - 1])
                        gp.wait_ge(x2l[(c + 1) % 2].h, x2l[(c + 1) % 2].n)
                        x2l_mk[c + 1] = x2l[(c + 1) % 2].inc(
                            gp.dma_start(out=x2_sb3[:, (c + 1) % 2], in_=r3(x2_scr[tok(c + 1)])), 16)
                blk.gpsimd(_gxp)

                sq3 = SQueue(s3)

                if c == 0:
                    em_gate(0)
                    em_sig(0, sq3)
                    em_topk(0, 0)
                    em_topk(0, 1)
                    em_topk(0, 2)
                    em_tct(0)
                    em_vct(0)

                for gi, grp in enumerate(EGROUPS):
                    for e in grp:
                        ei = eidx(c, e)
                        slot = ei % NSLOT
                        pos = grp.index(e)
                        _mc = {}

                        def _syw(sy, c=c, e=e, ei=ei, gi=gi, pos=pos):
                            s13 = ei % 2
                            if s13 in w13_user:
                                sy.wait_ge(t3.h, i1done[w13_user[s13]])
                            w13_user[s13] = ei
                            if c == 0 and e < 2:
                                wcast.wait(sy)
                            sy.wait_ge(w13l[s13].h, w13l[s13].n)
                            w13_mk[ei] = w13l[s13].inc(
                                sy.dma_start(out=w13_sb[:, s13],
                                             in_=w13bf[e].rearrange("(k p) n -> p k n", p=128)), 16)
                            key = (gi % 2, pos)
                            if key in w2slot_user:
                                pc, pgi = w2slot_user[key]
                                sy.wait_ge(t3.h, ymm.get((pc, pgi, TPC - 1), 0))
                            w2slot_user[key] = (c, gi)
                            sy.wait_ge(w2l[key].h, w2l[key].n)
                            w2_mk[ei] = w2l[key].inc(
                                sy.dma_start(out=w2_sb[:, gi % 2, pos],
                                             in_=w2bf[e].rearrange("(k p) n -> p k n", p=128)), 16)
                        blk.sync(_syw)

                        if e < NE:
                            def _tce(tn, c=c, e=e, ei=ei):
                                tn.wait_ge(v3.h, vct_mark[c])
                                if e >= 1:
                                    tn.wait_ge(v3.h, ceev.get(eidx(c, e - 1), 0))
                                elif c >= 1:
                                    tn.wait_ge(v3.h, ceev.get(eidx(c - 1, NE - 1), 0))
                                m = tn.matmul(CE_ps[:, 0:512],
                                              erow[:, e * 128:(e + 1) * 128], comb_f[:],
                                              start=True, stop=True)
                                ce_mm[ei] = t3.inc(m)
                                _mc["ce"] = ce_mm[ei]
                            blk.tensor(_tce)

                            def _vce(v, e=e, ei=ei):
                                v.wait_ge(t3.h, _mc["ce"])
                                ceev[ei] = v3.inc(v.tensor_copy(out=cE[:, e % 2, :],
                                                                in_=CE_ps[:, 0:512]))
                            blk.vector(_vce)

                        s3_base = s3.n
                        v3_base = v3.n

                        def _tm1(tn, c=c, e=e, ei=ei, s3_base=s3_base, v3_base=v3_base):
                            tn.wait_ge(w13l[ei % 2].h, w13_mk[ei])
                            tn.wait_ge(h2l[c % 2].h, h2l_mk[c])
                            if ei >= 1:
                                tn.wait_ge(s3.h, s3_base)
                                tn.wait_ge(v3.h, tvec.get(ei - 1, 0))
                            for oc in range(8):
                                if oc in (2, 3, 4, 5):
                                    tn.wait_ge(s3.h, s3_base + (oc - 1))
                                elif oc in (6, 7):
                                    tn.wait_ge(v3.h, v3_base + (oc - 5))
                                for kc in range(8):
                                    m = tn.matmul(I1_ps[:, (oc % 2) * 512:(oc % 2 + 1) * 512],
                                                  w13_sb[:, ei % 2, kc, oc * 128:(oc + 1) * 128],
                                                  h2f[:, par, kc, :],
                                                  start=(kc == 0), stop=(kc == 7))
                                _mc[("mm", oc)] = t3.inc(m)
                        blk.tensor(_tm1)
                        i1done[ei] = t3.n
                        h2f_free[c] = t3.n

                        for j in range(4):
                            def _sm1(s, ei=ei, j=j, _mc=_mc):
                                s.wait_ge(t3.h, _mc[("mm", j)])
                                if ei >= 1:
                                    s.wait_ge(v3.h, tvec.get(ei - 1, 0))
                                return s.activation(gbuf[:, j, :],
                                                    I1_ps[:, (j % 2) * 512:(j % 2 + 1) * 512],
                                                    AF.Silu)
                            sq3.add(_sm1)
                        assert s3.n == s3_base + 4

                        def _vm1(v, c=c, e=e, ei=ei, slot=slot, s3_base=s3_base):
                            if ei >= 1:
                                v.wait_ge(v3.h, actv[ei - 1])
                            if ei >= NSLOT:
                                pe = ei - NSLOT
                                pc, pee = divmod(pe, NEX)
                                v.wait_ge(t3.h, ymm.get((pc, EGROUPS_OF[pee], TPC - 1), 0))
                            for j in range(4):
                                v.wait_ge(t3.h, _mc[("mm", 4 + j)])
                                v.wait_ge(s3.h, s3_base + j + 1)
                                tvec[ei] = v3.inc(v.tensor_mul(out=act_sl[:, slot, j, :],
                                                               in0=gbuf[:, j, :],
                                                               in1=I1_ps[:, (j % 2) * 512:(j % 2 + 1) * 512]))
                            v.wait_ge(v3.h, v3.n)
                            if e < NE:
                                v.wait_ge(v3.h, ceev[ei])
                                for j in range(4):
                                    actv[ei] = v3.inc(v.tensor_mul(out=act_sl[:, slot, j, :],
                                                                   in0=act_sl[:, slot, j, :],
                                                                   in1=cE[:, e % 2, :]))
                            else:
                                actv[ei] = tvec[ei]
                        blk.vector(_vm1)
                        assert v3.n == v3_base + (1 if e < NE else 0) * 0 + 8 or True

                    if gi == 0:
                        def _syp(sy, c=c):
                            if c + 1 < NCH:
                                if c >= 1:
                                    sy.wait_ge(t3.h, h2f_free[c - 1])
                                sy.wait_ge(h2l[(c + 1) % 2].h, h2l[(c + 1) % 2].n)
                                for kc in range(8):
                                    h2l[(c + 1) % 2].inc(
                                        sy.dma_start_transpose(out=h2f[:, (c + 1) % 2, kc, :],
                                                               in_=h2_scr[tok(c + 1), kc * 128:(kc + 1) * 128]), 16)
                                h2l_mk[c + 1] = h2l[(c + 1) % 2].n
                        blk.sync(_syp)

                    if gi == len(EGROUPS) - 1 and c + 1 < NCH:
                        em_tct(c + 1)
                        em_vct(c + 1)

                    for t in range(TPC):
                        def _tm2(tn, c=c, gi=gi, grp=grp, t=t):
                            for e2 in grp:
                                tn.wait_ge(v3.h, actv[eidx(c, e2)])
                                tn.wait_ge(w2l[(gi % 2, grp.index(e2))].h, w2_mk[eidx(c, e2)])
                            prev = None
                            if t >= 2:
                                prev = (c, gi, t - 2)
                            elif gi >= 1:
                                prev = (c, gi - 1, t)
                            elif c >= 1:
                                prev = (c - 1, len(EGROUPS) - 1, t)
                            if prev is not None and prev in yev:
                                tn.wait_ge(v3.h, yev[prev])
                            for e2 in grp:
                                slot2 = eidx(c, e2) % NSLOT
                                for kc in range(4):
                                    for hh in range(2):
                                        m = tn.matmul(
                                            Y_ps[:, (t % 2) * D + hh * 512:(t % 2) * D + (hh + 1) * 512],
                                            act_sl[:, slot2, kc, t * 128:(t + 1) * 128],
                                            w2_sb[:, gi % 2, grp.index(e2), kc, hh * 512:(hh + 1) * 512],
                                            start=(e2 == grp[0] and kc == 0),
                                            stop=(e2 == grp[-1] and kc == 3))
                            ymm[(c, gi, t)] = t3.inc(m)
                        blk.tensor(_tm2)

                        def _vy(v, c=c, gi=gi, t=t):
                            v.wait_ge(t3.h, ymm[(c, gi, t)])
                            if gi == 0:
                                yev[(c, gi, t)] = v3.inc(v.tensor_copy(out=yf_b[:, t, :],
                                                                       in_=Y_ps[:, (t % 2) * D:(t % 2 + 1) * D]))
                            else:
                                yev[(c, gi, t)] = v3.inc(v.tensor_add(out=yf_b[:, t, :],
                                                                      in0=yf_b[:, t, :],
                                                                      in1=Y_ps[:, (t % 2) * D:(t % 2 + 1) * D]))
                        blk.vector(_vy)

                    if c + 1 < NCH:
                        if gi == 0:
                            em_gate(c + 1)
                            em_sig(c + 1, sq3)
                        elif gi == 2:
                            em_topk(c + 1, 0)
                        elif gi == 4:
                            em_topk(c + 1, 1)
                        elif gi == 6:
                            em_topk(c + 1, 2)

                sq3.flush(blk)

                def _vz(v, c=c, par=par):
                    v.wait_ge(v3.h, yev[(c, len(EGROUPS) - 1, TPC - 1)])
                    if c >= 1:
                        v.wait_ge(outst.h, 16 * c)
                    for t in range(TPC):
                        v3.inc(v.tensor_mul(out=ob_b[:, t, :], in0=yf_b[:, t, :], in1=gb_b[:]))
                    vzmul[c] = v3.n
                    v.wait_ge(v3.h, v3.n)
                    v.wait_ge(x2l[c % 2].h, x2l_mk[c])
                    for t in range(TPC):
                        v3.inc(v.tensor_add(out=ob_b[:, t, :], in0=ob_b[:, t, :],
                                            in1=x2_sb3[:, par, t, :]))
                    x2_free[c] = v3.n
                blk.vector(_vz)
                vzm = v3.n

                def _gout(gp, c=c, vzm=vzm):
                    gp.wait_ge(v3.h, vzm)
                    gp.wait_ge(outst.h, outst.n)
                    outst.inc(gp.dma_start(out=r3(out[tok(c)]), in_=ob_b[:]), 16)
                blk.gpsimd(_gout)

            def _gend(gp):
                outst.wait(gp)
            blk.gpsimd(_gend)

    es.close()
    return nc


def _host_prep(inputs):
    f32 = np.float32
    asn = np.ascontiguousarray
    w13 = np.concatenate([np.asarray(inputs["L0_ffn_w13"], f32),
                          np.asarray(inputs["L0_ffn_w13_s"], f32)[None]], axis=0)
    w2 = np.concatenate([np.asarray(inputs["L0_ffn_w2"], f32),
                         np.asarray(inputs["L0_ffn_w2_s"], f32)[None]], axis=0)
    w13T = asn(np.transpose(w13, (0, 2, 1)))
    w2T = asn(np.transpose(w2, (0, 2, 1)))
    gatewT = asn(np.asarray(inputs["L0_ffn_gate_w"], f32).T)
    ident = np.eye(128, dtype=ml_dtypes.bfloat16)
    onesrow = np.ones((1, 128), dtype=ml_dtypes.bfloat16)
    eexp = np.zeros((NH, NPAIR * 128), dtype=ml_dtypes.bfloat16)
    for p in range(NPAIR):
        for j in range(128):
            eexp[2 * p + j // 64, p * 128 + j] = 1
    erow = np.zeros((NH, NE * 128), dtype=ml_dtypes.bfloat16)
    for e in range(NE):
        erow[e, e * 128:(e + 1) * 128] = 1
    return {
        "wmod_in": asn(np.asarray(inputs["L0_wmod"], f32)),
        "gamma_in": asn(np.asarray(inputs["L0_attn_gamma"], f32).reshape(1, D)),
        "beta_in": asn(np.asarray(inputs["L0_attn_beta"], f32).reshape(1, D)),
        "wqkv_in": asn(np.asarray(inputs["L0_attn_Wqkv"], f32)),
        "wout_in": asn(np.asarray(inputs["L0_attn_Wout"], f32)),
        "gatewT_in": gatewT,
        "gateb_in": asn(np.asarray(inputs["L0_ffn_gate_b"], f32).reshape(1, NE)),
        "w13T_in": w13T,
        "w2T_in": w2T,
        "ident_in": ident,
        "onesrow_in": onesrow,
        "eexp_in": eexp,
        "erow_in": erow,
    }


def kernel(**inputs):
    x = np.asarray(inputs["L0_attn_x"], np.float32)
    cmat = np.asarray(inputs["L0_c"], np.float32)
    B, L, _ = x.shape
    com = _host_prep(inputs)
    nc = build_program(L)
    in_maps = []
    for b in range(B):
        m = dict(com)
        m["x_in"] = np.ascontiguousarray(x[b])
        m["c_row"] = np.ascontiguousarray(cmat[b:b + 1])
        in_maps.append(m)
    res = run_bass_kernel_spmd(nc, in_maps, core_ids=list(range(B)))
    return np.stack([res.results[b]["out"] for b in range(B)], axis=0)


# version 27
# speedup vs baseline: 1.1117x; 1.1117x over previous
import numpy as np
import ml_dtypes
from contextlib import ExitStack

import concourse.bass as bass
import concourse.mybir as mybir
from concourse.bass_utils import run_bass_kernel_spmd

F32 = mybir.dt.float32
BF16 = mybir.dt.bfloat16
AF = mybir.ActivationFunctionType
ALU = mybir.AluOpType
AX = mybir.AxisListType

D = 1024
DQKV = 3 * D
NH = 16
NPAIR = 8
NE = 16
NEX = 17
INTER = 512
ROUTE_SCALE = 2.5
LN_EPS = 1e-5
DEN_EPS = 1e-6
CT = 512
TPC = 4
RCPD = float(1.0 / D)
RSQD = float((1.0 / D) ** 0.5)
EGROUPS = [(0, 1), (2, 3), (4, 5), (6, 7), (8, 9), (10, 11), (12, 13), (14, 15), (16,)]
EGROUPS_OF = {e: gi for gi, grp in enumerate(EGROUPS) for e in grp}
NSLOT = 3


def _preload_tab(nc, s, idx):
    s.add_instruction(mybir.InstLoadActFuncSet(
        name=nc.get_next_instruction_name(), act_func_set_id=idx, ins=[], outs=[]))


class Sem:
    def __init__(self, es, nc, name):
        self.h = es.enter_context(nc.semaphore(name))
        self.n = 0

    def inc(self, inst, k=1):
        inst.then_inc(self.h, k)
        self.n += k
        return self.n

    def wait(self, eng, n=None):
        eng.wait_ge(self.h, self.n if n is None else n)


class SQueue:

    def __init__(self, sem):
        self.sem = sem
        self.q = []

    def add(self, emitfn, inc=1):
        mark = None
        if inc:
            self.sem.n += inc
            mark = self.sem.n
        self.q.append((emitfn, mark, inc))
        return mark

    def flush(self, blk):
        q, self.q = self.q, []
        if not q:
            return

        def body(s):
            for fn, mark, inc in q:
                inst = fn(s)
                if mark is not None:
                    inst.then_inc(self.sem.h, inc)
        blk.scalar(body)


def build_program(L):
    assert L % CT == 0
    NCH = L // CT
    NT = L // 128
    nc = bass.Bass("TRN2", target_bir_lowering=False)

    dr = lambda n, s, dt, k="Internal": nc.dram_tensor(n, s, dt, kind=k).ap()
    x_in = dr("x_in", [L, D], F32, "ExternalInput")
    c_row = dr("c_row", [1, D], F32, "ExternalInput")
    wmod_in = dr("wmod_in", [D, 6 * D], F32, "ExternalInput")
    gamma_in = dr("gamma_in", [1, D], F32, "ExternalInput")
    beta_in = dr("beta_in", [1, D], F32, "ExternalInput")
    wqkv_in = dr("wqkv_in", [D, DQKV], F32, "ExternalInput")
    wout_in = dr("wout_in", [D, D], F32, "ExternalInput")
    gatewT_in = dr("gatewT_in", [D, NE], F32, "ExternalInput")
    gateb_in = dr("gateb_in", [1, NE], F32, "ExternalInput")
    w13T_in = dr("w13T_in", [NEX, D, D], F32, "ExternalInput")
    w2T_in = dr("w2T_in", [NEX, INTER, D], F32, "ExternalInput")
    ident_in = dr("ident_in", [128, 128], BF16, "ExternalInput")
    onesrow_in = dr("onesrow_in", [1, 128], BF16, "ExternalInput")
    eexp_in = dr("eexp_in", [NH, NPAIR * 128], BF16, "ExternalInput")
    erow_in = dr("erow_in", [NH, NE * 128], BF16, "ExternalInput")
    out = dr("out", [L, D], F32, "ExternalOutput")
    w13bf = dr("w13bf", [NEX, D, D], BF16)
    w2bf = dr("w2bf", [NEX, INTER, D], BF16)
    xn_scr = dr("xn_scr", [L, D], BF16)
    q_scr = dr("q_scr", [L, D], BF16)
    x2_scr = dr("x2_scr", [L, D], F32)
    h2_scr = dr("h2_scr", [L, D], BF16)

    es = ExitStack()
    sbuf = lambda n, s, dt: es.enter_context(nc.sbuf_tensor(n, s, dt))
    S = lambda n: Sem(es, nc, n)
    r3 = lambda ap: ap.rearrange("(t p) d -> p t d", p=128)
    tok = lambda c: slice(c * CT, (c + 1) * CT)

    ca1_b = sbuf("ca1_b", [128, D], BF16)
    sa_b = sbuf("sa_b", [128, D], BF16)
    bw_a = sbuf("bw_a", [96, 512], BF16)
    ones96 = sbuf("ones96", [96, 128], BF16)
    bw_b = sbuf("bw_b", [96, 512], BF16)
    A_b = sbuf("A_b", [128, D], BF16)
    B_b = sbuf("B_b", [128, D], BF16)
    cb1_b = sbuf("cb1_b", [128, D], BF16)
    sbb_b = sbuf("sbb_b", [128, D], BF16)
    ga_b = sbuf("ga_b", [128, D], F32)
    gb_b = sbuf("gb_b", [128, D], F32)
    gateb_b = sbuf("gateb_b", [128, NE], F32)
    ident = sbuf("ident", [128, 128], BF16)
    onesrow = sbuf("onesrow", [1, 128], BF16)
    onescol = sbuf("onescol", [128, 1], BF16)
    eexp = sbuf("eexp", [NH, NPAIR * 128], BF16)
    erow = sbuf("erow", [NH, NE * 128], BF16)
    rs1 = sbuf("rs1", [128, NT], F32)
    nmurs1 = sbuf("nmurs1", [128, NT], F32)
    rs3 = sbuf("rs3", [128, NT], F32)
    nmurs3 = sbuf("nmurs3", [128, NT], F32)
    ksum_sb = sbuf("ksum_sb", [128, NPAIR], F32)
    kv_sb = sbuf("kv_sb", [128, NPAIR * 128], BF16)
    ksbd = sbuf("ksbd", [128, NPAIR * NH], BF16)
    junk4 = sbuf("junk4", [128, TPC, D], BF16)
    epsc = sbuf("epsc", [128, 1], F32)
    gatew_sb = sbuf("gatew_sb", [128, 8, NE], BF16)
    es_wo = ExitStack()
    wout_sb = es_wo.enter_context(nc.sbuf_tensor("wout_sb", [128, 8, D], BF16))
    es_wq = ExitStack()
    wqkv_sb = es_wq.enter_context(nc.sbuf_tensor("wqkv_sb", [128, 8, DQKV], BF16))

    wcast = S("wcast")

    with ExitStack() as es0:
        sb0 = lambda n, s, dt: es0.enter_context(nc.sbuf_tensor(n, s, dt))
        ps0 = lambda n, s: es0.enter_context(nc.psum_tensor(n, s, F32))
        c_sb = sb0("c_sb", [1, D], F32)
        sc_bf = sb0("sc_bf", [1, D], BF16)
        sc_f = sb0("sc_f", [128, 8], BF16)
        mod_bf = sb0("mod_bf", [1, 6 * D], BF16)
        gb_row = sb0("gb_row", [1, NE], BF16)
        wmod_bf = sb0("wmod_bf", [128, 8, 3 * D], BF16)
        sgn_t = sb0("sgn_t", [128, D], F32)
        amx_t = sb0("amx_t", [128, D], F32)
        gammaT = sb0("gammaT", [128, 8], F32)
        betaT = sb0("betaT", [128, 8], BF16)
        mod_ps = ps0("mod_ps", [1, 3 * D])
        bc_ps = ps0("bc_ps", [128, D])
        bw_ps = mod_ps[0:1, 0:512]
        ch = S("p0ch")
        chd = S("p0chd")
        chs = S("p0chs")

        with nc.Block("p0") as blk:
            @blk.gpsimd
            def _(gp):
                chd.inc(gp.dma_start(out=wqkv_sb[:], in_=wqkv_in.rearrange("(k p) n -> p k n", p=128)), 16)
                chd.inc(gp.dma_start(out=wout_sb[:], in_=wout_in.rearrange("(k p) n -> p k n", p=128)), 16)
                chd.inc(gp.dma_start(out=gatew_sb[:], in_=gatewT_in.rearrange("(k p) n -> p k n", p=128)), 16)
                chd.inc(gp.dma_start(out=gb_row[:], in_=gateb_in), 16)
                bTv = beta_in.rearrange("o (kc p) -> kc p o", kc=8)
                for kc in range(8):
                    chd.inc(gp.dma_start(out=betaT[:, kc:kc + 1], in_=bTv[kc]), 16)

            @blk.sync
            def _(sy):
                chs.inc(sy.dma_start(out=c_sb[:], in_=c_row), 16)
                gTv = gamma_in.rearrange("o (kc p) -> kc p o", kc=8)
                for kc in range(8):
                    chs.inc(sy.dma_start(out=gammaT[:, kc:kc + 1], in_=gTv[kc]), 16)
                chs.inc(sy.dma_start(out=ident[:], in_=ident_in), 16)
                chs.inc(sy.dma_start(out=onesrow[:], in_=onesrow_in), 16)
                chs.inc(sy.dma_start(out=eexp[:], in_=eexp_in), 16)
                chs.inc(sy.dma_start(out=erow[:], in_=erow_in), 16)

            def step(eng_attr, fn):
                def body(e):
                    ch.wait(e)
                    chd.wait(e)
                    chs.wait(e)
                    fn(e)
                getattr(blk, eng_attr)(body)

            step("scalar", lambda e: ch.inc(e.activation(c_sb[:], c_sb[:], AF.Silu)))

            def _v0(v):
                ch.inc(v.memset(epsc[:], LN_EPS))
                ch.wait(v)
                ch.inc(v.memset(onescol[:], 1.0))
                ch.wait(v)
                ch.inc(v.memset(ones96[:], 1.0))
                ch.wait(v)
                ch.inc(v.memset(ksum_sb[:], 0.0))
                ch.wait(v)
                ch.inc(v.tensor_copy(out=sc_bf[:], in_=c_sb[:]))
            step("vector", _v0)

            def _t0(tn):
                for kc in range(8):
                    m = tn.matmul(bc_ps[:, kc:kc + 1], sc_bf[0:1, kc * 128:(kc + 1) * 128],
                                  onescol[0:1, 0:1], start=(kc == 0), stop=(kc == 7))
                ch.inc(m)
            step("tensor", _t0)
            step("vector", lambda v: ch.inc(v.tensor_copy(out=sc_f[:], in_=bc_ps[:, 0:8])))

            for half in range(2):
                def _gw(gp, half=half):
                    chd.inc(gp.dma_start(out=wmod_bf[:],
                                        in_=wmod_in[:, half * 3072:(half + 1) * 3072]
                                        .rearrange("(k p) n -> p k n", p=128)), 16)
                step("gpsimd", _gw)

                def _tm(tn, half=half):
                    for ns in range(6):
                        for kc in range(8):
                            m = tn.matmul(mod_ps[0:1, ns * 512:(ns + 1) * 512],
                                          sc_f[:, kc:kc + 1],
                                          wmod_bf[:, kc, ns * 512:(ns + 1) * 512],
                                          start=(kc == 0), stop=(kc == 7))
                    ch.inc(m)
                step("tensor", _tm)
                step("vector", lambda v, half=half: ch.inc(
                    v.tensor_copy(out=mod_bf[0:1, half * 3072:(half + 1) * 3072], in_=mod_ps[:])))

            items = [("sa", mod_bf[0:1, 0:D]), ("ca", mod_bf[0:1, D:2 * D]),
                     ("ga", mod_bf[0:1, 2 * D:3 * D]), ("sb", mod_bf[0:1, 3 * D:4 * D]),
                     ("cb", mod_bf[0:1, 4 * D:5 * D]), ("gb", mod_bf[0:1, 5 * D:6 * D]),
                     ("gateb", gb_row[0:1, 0:NE])]
            for name, src in items:
                def _tb(tn, src=src):
                    n = src.shape[-1]
                    for ns in range((n + 511) // 512):
                        w = min(512, n - ns * 512)
                        m = tn.matmul(bc_ps[:, ns * 512:ns * 512 + w], onesrow[0:1, :],
                                      src[0:1, ns * 512:ns * 512 + w], start=True, stop=True)
                    ch.inc(m)
                step("tensor", _tb)
                if name in ("ga", "gb"):
                    step("scalar", lambda s: ch.inc(s.activation(amx_t[:], bc_ps[:], AF.Abs)))

                def _vb(v, name=name):
                    if name == "sa":
                        ch.inc(v.tensor_copy(out=sa_b[:], in_=bc_ps[:]))
                    elif name == "ca":
                        ch.inc(v.tensor_scalar(out=ca1_b[:], in0=bc_ps[:], scalar1=1.0,
                                               scalar2=None, op0=ALU.add))
                    elif name in ("ga", "gb"):
                        dst = ga_b if name == "ga" else gb_b
                        ch.inc(v.tensor_scalar(out=sgn_t[:], in0=bc_ps[:], scalar1=0.0,
                                               scalar2=2.0, op0=ALU.is_ge, op1=ALU.mult))
                        ch.wait(v)
                        ch.inc(v.tensor_scalar(out=sgn_t[:], in0=sgn_t[:], scalar1=-1.0,
                                               scalar2=None, op0=ALU.add))
                        ch.wait(v)
                        ch.inc(v.tensor_scalar(out=amx_t[:], in0=amx_t[:], scalar1=0.25,
                                               scalar2=None, op0=ALU.max))
                        ch.wait(v)
                        ch.inc(v.tensor_mul(out=dst[:], in0=sgn_t[:], in1=amx_t[:]))
                        if name == "ga":
                            ch.wait(v)
                            ch.inc(v.tensor_mul(out=A_b[:], in0=ga_b[:], in1=ca1_b[:]))
                            ch.wait(v)
                            ch.inc(v.tensor_mul(out=B_b[:], in0=ga_b[:], in1=sa_b[:]))
                    elif name == "sb":
                        ch.inc(v.tensor_copy(out=sbb_b[:], in_=bc_ps[:]))
                    elif name == "cb":
                        ch.inc(v.tensor_scalar(out=cb1_b[:], in0=bc_ps[:], scalar1=1.0,
                                               scalar2=None, op0=ALU.add))
                    elif name == "gateb":
                        ch.inc(v.tensor_copy(out=gateb_b[:], in_=bc_ps[:, 0:NE]))
                step("vector", _vb)

            for seg in range(6):
                def _tbw(tn, seg=seg):
                    for kc in range(8):
                        m = tn.matmul(bw_ps[0:1, :], betaT[:, kc:kc + 1],
                                      wqkv_sb[:, kc, seg * 512:(seg + 1) * 512],
                                      start=(kc == 0), stop=(kc == 7))
                    ch.inc(m)
                step("tensor", _tbw)
                dst = (bw_a if seg < 3 else bw_b)
                row = (seg % 3) * 32
                step("scalar", lambda s, dst=dst, row=row: ch.inc(
                    s.copy(out=dst[row:row + 1, 0:512], in_=bw_ps[0:1, :])))
            for kc in range(8):
                def _sgw(s, kc=kc):
                    ch.inc(s.activation(wqkv_sb[:, kc, :], wqkv_sb[:, kc, :], AF.Identity,
                                        scale=gammaT[:, kc:kc + 1]))
                step("scalar", _sgw)
            for kc in range(8):
                def _vgw(v, kc=kc):
                    ch.inc(v.tensor_mul(out=wout_sb[:, kc, :], in0=wout_sb[:, kc, :],
                                        in1=ga_b[:]))
                step("vector", _vgw)


    with ExitStack() as es1:
        sb1 = lambda n, s, dt: es1.enter_context(nc.sbuf_tensor(n, s, dt))
        ps1 = lambda n, s: es1.enter_context(nc.psum_tensor(n, s, F32))
        x_sb = sb1("x_sb", [128, 2, TPC, D], BF16)
        xn_t = sb1("xn_t", [128, 2, TPC, D], BF16)
        xn_f = sb1("xn_f", [128, 2, 8, CT], BF16)
        q_sb = sb1("q_sb", [128, 2, TPC, D], BF16)
        k_sb = sb1("k_sb", [128, D], BF16)
        v_sb = sb1("v_sb", [128, D], BF16)
        e1k = sb1("e1k", [128, D], BF16)
        t1b = sb1("t1b", [128, 2, D], BF16)
        h_t = sb1("h_t", [128, TPC, D], BF16)
        st_s = sb1("st_s", [128, 32], F32)
        bnx = sb1("bnx", [128, 2, TPC, 12], F32)
        mvx = sb1("mvx", [128, 2, TPC, 2], F32)
        bnh = sb1("bnh", [128, TPC, 12], F32)
        mvh = sb1("mvh", [128, TPC, 2], F32)
        rsh = sb1("rsh", [128, TPC], F32)
        nmh = sb1("nmh", [128, TPC], F32)
        K_ps = ps1("K_ps", [128, D])
        V_ps = ps1("V_ps", [128, D])
        Q_ps = ps1("Q_ps", [128, D])
        KV_ps = ps1("KV_ps", [128, NPAIR * 128])

        xld = [S("xld0"), S("xld1")]
        xnst = [S("xnst0"), S("xnst1")]
        xnfl = [S("xnfl0"), S("xnfl1")]
        qspl = [S("qspl0"), S("qspl1")]
        v1 = S("v1")
        s1 = S("s1")
        g1 = S("g1")
        t1 = S("t1")
        kmm = [0] * NT; vmm = [0] * NT; qmm = [0] * NT; gmm = [0] * NT
        gsum = [0] * NT
        kfin = [0] * NT; vcp = [0] * NT; qfin = [0] * NT; ksev = [0] * NT
        ek_mk = [0] * NT; eq_mk = [0] * NT
        xnf_ok = [0] * NCH
        xld_mk = [0] * (NCH + 2)
        xnst_mk = [0] * (NCH + 2)
        qspl_mk = [0] * (NCH + 2)
        t1last = [0] * (NCH + 1)
        t2last = [0] * (NCH + 1)
        t2mks_h = [0] * (NCH + 1)
        aggx_mk = [0] * (NCH + 1)
        rsx_mark = [0] * (NCH + 1)
        nmx_mark = [0] * (NCH + 1)
        aggh_mk = [0] * (NCH + 1)
        rsh_mark = [0] * (NCH + 1)
        nmh_mark = [0] * (NCH + 1)

        with nc.Block("p1") as blk:
            def _gx0(gp):
                xld_mk[0] = xld[0].inc(gp.dma_start(out=x_sb[:, 0], in_=r3(x_in[tok(0)])), 16)
                if NCH > 1:
                    xld_mk[1] = xld[1].inc(gp.dma_start(out=x_sb[:, 1], in_=r3(x_in[tok(1)])), 16)
            blk.gpsimd(_gx0)
            castq = [(w13bf[e], w13T_in[e]) for e in range(NEX)] + \
                    [(w2bf[e], w2T_in[e]) for e in range(NEX)]

            def emit_mm_kv(c, t, sq):
                g = c * TPC + t
                par = c % 2

                def _t(tn):
                    if t == 0:
                        tn.wait_ge(xnfl[par].h, xnf_ok[c])
                    if g > 0:
                        tn.wait_ge(v1.h, ksev[g - 1])
                    for kc in range(8):
                        for hh in range(2):
                            tn.matmul(K_ps[:, hh * 512:(hh + 1) * 512],
                                      xn_f[:, par, kc, t * 128:(t + 1) * 128],
                                      wqkv_sb[:, kc, D + hh * 512:D + (hh + 1) * 512],
                                      start=(kc == 0), stop=False)
                    for hh in range(2):
                        m = tn.matmul(K_ps[:, hh * 512:(hh + 1) * 512],
                                      ones96[64 * (1 - hh):64 * (1 - hh) + 1, 0:128],
                                      (bw_a[64:65, 0:512] if hh == 0 else bw_b[0:1, 0:512]),
                                      start=False, stop=True)
                    kmm[g] = t1.inc(m)
                    for kc in range(8):
                        for hh in range(2):
                            tn.matmul(V_ps[:, hh * 512:(hh + 1) * 512],
                                      xn_f[:, par, kc, t * 128:(t + 1) * 128],
                                      wqkv_sb[:, kc, 2 * D + hh * 512:2 * D + (hh + 1) * 512],
                                      start=(kc == 0), stop=False)
                    for hh in range(2):
                        m = tn.matmul(V_ps[:, hh * 512:(hh + 1) * 512],
                                      ones96[32 * (1 + hh):32 * (1 + hh) + 1, 0:128],
                                      bw_b[32 * (1 + hh):32 * (1 + hh) + 1, 0:512],
                                      start=False, stop=True)
                    vmm[g] = t1.inc(m)
                blk.tensor(_t)

                def _sk(s, g=g):
                    s.wait_ge(t1.h, kmm[g])
                    if g > 0:
                        s.wait_ge(v1.h, qfin[g - 1])
                    return s.activation(e1k[:], K_ps[:], AF.Exp)
                ek_mk[g] = sq.add(_sk)

            def emit_v_elukv(c, t):
                g = c * TPC + t

                def _v(v):
                    v.wait_ge(s1.h, ek_mk[g])
                    v1.inc(v.tensor_scalar(out=e1k[:], in0=e1k[:], scalar1=1.0,
                                           scalar2=None, op0=ALU.min))
                    v.wait_ge(v1.h, v1.n)
                    if g > 0:
                        v.wait_ge(t1.h, gmm[g - 1])
                    kfin[g] = v1.inc(v.scalar_tensor_tensor(out=k_sb[:], in0=K_ps[:], scalar=1.0,
                                                            in1=e1k[:], op0=ALU.add, op1=ALU.max))
                    v.wait_ge(t1.h, vmm[g])
                    vcp[g] = v1.inc(v.tensor_copy(out=v_sb[:], in_=V_ps[:]))
                blk.vector(_v)

            def emit_v_ksev(c, t):
                g = c * TPC + t

                def _v2(v):
                    v.wait_ge(t1.h, gsum[g])
                    ksev[g] = v1.inc(v.tensor_add(out=ksum_sb[:], in0=ksum_sb[:],
                                                  in1=K_ps[:, 0:NPAIR]))
                blk.vector(_v2)

            def emit_t_gram(c, t):
                g = c * TPC + t

                def _t2(tn):
                    tn.wait_ge(v1.h, vcp[g])
                    tn.wait_ge(v1.h, kfin[g])
                    for p in range(NPAIR):
                        m = tn.matmul(K_ps[:, p:p + 1],
                                      k_sb[:, p * 128:(p + 1) * 128], onescol[:],
                                      start=(p == 0), stop=(p == NPAIR - 1))
                    gsum[g] = t1.inc(m)
                    for p in range(NPAIR):
                        m = tn.matmul(KV_ps[:, p * 128:(p + 1) * 128],
                                      k_sb[:, p * 128:(p + 1) * 128],
                                      v_sb[:, p * 128:(p + 1) * 128],
                                      start=(g == 0 and p % 4 == 0), stop=(g == NT - 1 and p % 4 == 3))
                    gmm[g] = t1.inc(m)
                blk.tensor(_t2)

            def emit_t_q(c, t, sq):
                g = c * TPC + t
                par = c % 2

                def _t3(tn):
                    if g > 0:
                        tn.wait_ge(v1.h, qfin[g - 1])
                    for kc in range(8):
                        for hh in range(2):
                            tn.matmul(Q_ps[:, hh * 512:(hh + 1) * 512],
                                      xn_f[:, par, kc, t * 128:(t + 1) * 128],
                                      wqkv_sb[:, kc, hh * 512:(hh + 1) * 512],
                                      start=(kc == 0), stop=False)
                    for hh in range(2):
                        m = tn.matmul(Q_ps[:, hh * 512:(hh + 1) * 512],
                                      ones96[32 * hh:32 * hh + 1, 0:128],
                                      bw_a[32 * hh:32 * hh + 1, 0:512],
                                      start=False, stop=True)
                    qmm[g] = t1.inc(m)
                blk.tensor(_t3)

                def _sq_(s, g=g):
                    s.wait_ge(t1.h, qmm[g])
                    s.wait_ge(v1.h, kfin[g])
                    return s.activation(e1k[:], Q_ps[:], AF.Exp)
                eq_mk[g] = sq.add(_sq_)

            def emit_v_eluq(c, t):
                g = c * TPC + t

                def _v(v):
                    v.wait_ge(s1.h, eq_mk[g])
                    v1.inc(v.tensor_scalar(out=e1k[:], in0=e1k[:], scalar1=1.0,
                                           scalar2=None, op0=ALU.min))
                    v.wait_ge(v1.h, v1.n)
                    if c >= 2:
                        v.wait_ge(qspl[c % 2].h, qspl_mk[c - 2])
                    qfin[g] = v1.inc(v.scalar_tensor_tensor(out=q_sb[:, c % 2, t, :], in0=Q_ps[:],
                                                            scalar=1.0, in1=e1k[:],
                                                            op0=ALU.add, op1=ALU.max))
                blk.vector(_v)

            def emit_lnA(ca, sq):
                parA = ca % 2

                def _vbn(v, ca=ca, parA=parA):
                    v.wait_ge(xld[parA].h, xld_mk[ca])
                    if ca >= 2:
                        v.wait_ge(s1.h, rsx_mark[ca - 2])
                    for t in range(TPC):
                        v1.inc(v.bn_stats(out=bnx[:, parA, t, 0:6],
                                          in_=x_sb[:, parA, t, 0:512]))
                        v1.inc(v.bn_stats(out=bnx[:, parA, t, 6:12],
                                          in_=x_sb[:, parA, t, 512:1024]))
                    v.wait_ge(v1.h, v1.n)
                    for t in range(TPC):
                        v1.inc(v.bn_aggr(out=mvx[:, parA, t, :], in_=bnx[:, parA, t, :]))
                    aggx_mk[ca] = v1.n
                blk.vector(_vbn)

                def _srs(s, ca=ca, parA=parA):
                    s.wait_ge(v1.h, aggx_mk[ca])
                    for t in range(TPC):
                        m = s.activation(mvx[:, parA, t, 1:2], mvx[:, parA, t, 1:2],
                                         AF.Sqrt, bias=epsc[:, 0:1])
                    return m
                rsx_mark[ca] = sq.add(_srs, inc=4)

                def _vnm(v, ca=ca, parA=parA):
                    v.wait_ge(s1.h, rsx_mark[ca])
                    for t in range(TPC):
                        g = ca * TPC + t
                        v1.inc(v.reciprocal(rs1[:, g:g + 1], mvx[:, parA, t, 1:2]))
                    v.wait_ge(v1.h, v1.n)
                    for t in range(TPC):
                        g = ca * TPC + t
                        v1.inc(v.tensor_scalar(out=nmurs1[:, g:g + 1], in0=mvx[:, parA, t, 0:1],
                                               scalar1=rs1[:, g:g + 1], scalar2=-1.0,
                                               op0=ALU.mult, op1=ALU.mult))
                    nmx_mark[ca] = v1.n
                blk.vector(_vnm)

            for it in range(NCH + 1):
                cc, c = it, it - 1
                par = cc % 2
                B0 = 16 * par
                sq = SQueue(s1)
                if it == 0:
                    emit_lnA(0, sq)

                if cc < NCH:
                    t1mk = {}
                    hmk = {}
                    for t in range(TPC):
                        def _t1(s, cc=cc, t=t, hmk=hmk):
                            s.wait_ge(v1.h, nmx_mark[cc])
                            if t > 1:
                                s.wait_ge(v1.h, hmk[t - 2])
                            g = cc * TPC + t
                            return s.activation(t1b[:, t % 2, :], x_sb[:, cc % 2, t, :], AF.Identity,
                                                bias=nmurs1[:, g:g + 1], scale=rs1[:, g:g + 1])
                        t1mk[t] = sq.add(_t1)
                    t1last[cc] = t1mk[TPC - 1]

                    def _vh01(v, cc=cc, t1mk=t1mk, hmk=hmk):
                        for t in range(2):
                            v.wait_ge(s1.h, t1mk[t])
                            if cc >= 2 and t == 0:
                                v.wait_ge(s1.h, t2last[cc - 2])
                            v1.inc(v.tensor_mul(out=h_t[:, t, :], in0=t1b[:, t % 2, :],
                                                in1=ca1_b[:]))
                            v.wait_ge(v1.h, v1.n)
                            hmk[t] = v1.inc(v.tensor_add(out=h_t[:, t, :], in0=h_t[:, t, :],
                                                         in1=sa_b[:]))
                            v.wait_ge(v1.h, v1.n)
                            v1.inc(v.bn_stats(out=bnh[:, t, 0:6], in_=h_t[:, t, 0:512]))
                            v1.inc(v.bn_stats(out=bnh[:, t, 6:12], in_=h_t[:, t, 512:1024]))
                    blk.vector(_vh01)

                if c >= 0:
                    emit_mm_kv(c, 0, sq)
                    emit_v_elukv(c, 0)
                    emit_t_gram(c, 0)
                    emit_v_ksev(c, 0)
                    emit_t_q(c, 0, sq)
                    emit_v_eluq(c, 0)

                if cc < NCH:
                    def _vh23(v, cc=cc, t1mk=t1mk, hmk=hmk):
                        for t in range(2, TPC):
                            v.wait_ge(s1.h, t1mk[t])
                            v1.inc(v.tensor_mul(out=h_t[:, t, :], in0=t1b[:, t % 2, :],
                                                in1=ca1_b[:]))
                            v.wait_ge(v1.h, v1.n)
                            hmk[t] = v1.inc(v.tensor_add(out=h_t[:, t, :], in0=h_t[:, t, :],
                                                         in1=sa_b[:]))
                            v.wait_ge(v1.h, v1.n)
                            v1.inc(v.bn_stats(out=bnh[:, t, 0:6], in_=h_t[:, t, 0:512]))
                            v1.inc(v.bn_stats(out=bnh[:, t, 6:12], in_=h_t[:, t, 512:1024]))
                        v.wait_ge(v1.h, v1.n)
                        if cc >= 1:
                            v.wait_ge(s1.h, rsh_mark[cc - 1])
                        for t in range(TPC):
                            v1.inc(v.bn_aggr(out=mvh[:, t, :], in_=bnh[:, t, :]))
                        aggh_mk[cc] = v1.n
                    blk.vector(_vh23)

                if cc < NCH:
                    def _srs2(s, cc=cc):
                        s.wait_ge(v1.h, aggh_mk[cc])
                        for t in range(TPC):
                            m = s.activation(mvh[:, t, 1:2], mvh[:, t, 1:2],
                                             AF.Sqrt, bias=epsc[:, 0:1])
                        return m
                    rsh_mark[cc] = sq.add(_srs2, inc=4)

                    def _vnmh(v, cc=cc):
                        v.wait_ge(s1.h, rsh_mark[cc])
                        if cc >= 1:
                            v.wait_ge(s1.h, t2last[cc - 1])
                        for t in range(TPC):
                            v1.inc(v.reciprocal(rsh[:, t:t + 1], mvh[:, t, 1:2]))
                        v.wait_ge(v1.h, v1.n)
                        for t in range(TPC):
                            v1.inc(v.tensor_scalar(out=nmh[:, t:t + 1], in0=mvh[:, t, 0:1],
                                                   scalar1=rsh[:, t:t + 1], scalar2=-1.0,
                                                   op0=ALU.mult, op1=ALU.mult))
                        nmh_mark[cc] = v1.n
                    blk.vector(_vnmh)

                if c >= 0:
                    emit_mm_kv(c, 1, sq)
                    emit_v_elukv(c, 1)
                    emit_t_gram(c, 1)
                    emit_v_ksev(c, 1)
                    emit_t_q(c, 1, sq)
                    emit_v_eluq(c, 1)

                if cc < NCH:
                    t2mks = {}
                    for t in range(TPC):
                        def _t2(s, cc=cc, t=t, par=par):
                            s.wait_ge(v1.h, nmh_mark[cc])
                            if t == 0 and cc >= 2:
                                s.wait_ge(xnst[par].h, xnst_mk[cc - 2])
                            return s.activation(xn_t[:, par, t, :], h_t[:, t, :], AF.Identity,
                                                bias=nmh[:, t:t + 1],
                                                scale=rsh[:, t:t + 1])
                        t2mks[t] = sq.add(_t2)
                    t2last[cc] = t2mks[TPC - 1]
                    t2mks_h[cc] = t2mks[TPC - 1]

                    def _sspq(s, cc=cc, par=par):
                        s.wait_ge(s1.h, t2mks_h[cc])
                        s.wait_ge(xnst[par].h, xnst[par].n)
                        xnst_mk[cc] = xnst[par].inc(
                            s.dma_start(out=r3(xn_scr[tok(cc)]), in_=xn_t[:, par]), 16)
                    sq.add(_sspq, inc=0)

                if c >= 0:
                    emit_mm_kv(c, 2, sq)
                    emit_v_elukv(c, 2)
                    emit_t_gram(c, 2)
                    emit_v_ksev(c, 2)
                    emit_t_q(c, 2, sq)
                    emit_v_eluq(c, 2)

                if cc + 1 < NCH:
                    emit_lnA(cc + 1, sq)

                if c >= 0:
                    emit_mm_kv(c, 3, sq)
                    emit_v_elukv(c, 3)
                    emit_t_gram(c, 3)
                    emit_v_ksev(c, 3)
                    emit_t_q(c, 3, sq)
                    emit_v_eluq(c, 3)

                sq.flush(blk)

                if cc < NCH:
                    def _syt(sy, cc=cc, par=par):
                        sy.wait_ge(xnst[par].h, xnst_mk[cc])
                        if cc >= 2:
                            sy.wait_ge(t1.h, qmm[(cc - 2) * TPC + TPC - 1])
                        sy.wait_ge(xnfl[par].h, xnfl[par].n)
                        for kc in range(8):
                            m = sy.dma_start_transpose(out=xn_f[:, par, kc, :],
                                                       in_=xn_scr[tok(cc), kc * 128:(kc + 1) * 128])
                            xnfl[par].inc(m, 16)
                        xnf_ok[cc] = xnfl[par].n
                    blk.sync(_syt)

                    def _gxl(gp, cc=cc, par=par):
                        if cc + 2 < NCH:
                            gp.wait_ge(s1.h, t1last[cc])
                            gp.wait_ge(xld[par].h, xld[par].n)
                            xld_mk[cc + 2] = xld[par].inc(
                                gp.dma_start(out=x_sb[:, par], in_=r3(x_in[tok(cc + 2)])), 16)
                        for _ in range(3):
                            if castq:
                                o, i = castq.pop(0)
                                wcast.inc(gp.dma_start(out=o, in_=i), 16)
                    blk.gpsimd(_gxl)

                if c >= 0:
                    def _sq2(sc, c=c):
                        sc.wait_ge(v1.h, qfin[c * TPC + TPC - 1])
                        sc.wait_ge(qspl[c % 2].h, qspl[c % 2].n)
                        qspl_mk[c] = qspl[c % 2].inc(
                            sc.dma_start(out=r3(q_scr[tok(c)]), in_=q_sb[:, c % 2]), 16)
                    blk.scalar(_sq2)

            if castq:
                def _gcrest(gp):
                    for o, i in castq:
                        wcast.inc(gp.dma_start(out=o, in_=i), 16)
                    castq.clear()
                blk.gpsimd(_gcrest)

            def _vprep(v):
                v.wait_ge(t1.h, gmm[NT - 1])
                v.wait_ge(v1.h, ksev[NT - 1])
                v1.inc(v.memset(kv_sb[:], 0.0))
                v1.inc(v.memset(ksbd[:], 0.0))
                v.wait_ge(v1.h, v1.n)
                for p in range(NPAIR):
                    v1.inc(v.tensor_copy(out=kv_sb[0:64, p * 128:p * 128 + 64],
                                         in_=KV_ps[0:64, p * 128:p * 128 + 64]))
                    v1.inc(v.tensor_copy(out=kv_sb[64:128, p * 128 + 64:p * 128 + 128],
                                         in_=KV_ps[64:128, p * 128 + 64:p * 128 + 128]))
                    v1.inc(v.tensor_copy(out=ksbd[0:64, p * NH + 2 * p:p * NH + 2 * p + 1],
                                         in_=ksum_sb[0:64, p:p + 1]))
                    v1.inc(v.tensor_copy(out=ksbd[64:128, p * NH + 2 * p + 1:p * NH + 2 * p + 2],
                                         in_=ksum_sb[64:128, p:p + 1]))
            blk.vector(_vprep)

    es_wq.close()

    with ExitStack() as es2:
        sb2 = lambda n, s, dt: es2.enter_context(nc.sbuf_tensor(n, s, dt))
        ps2 = lambda n, s: es2.enter_context(nc.psum_tensor(n, s, F32))
        x_sb = sb2("x_sb2", [128, 2, TPC, D], F32)
        qT = sb2("qT", [128, 2, 8, CT], BF16)
        attn_f = sb2("attn_f", [128, NPAIR, CT], BF16)
        rb_sb = sb2("rb_sb", [128, 2, CT], F32)
        de_sb = sb2("de_sb", [NH, CT], F32)
        rcp_bf = sb2("rcp_bf", [NH, CT], BF16)
        t1b2 = sb2("t1b2", [128, D], BF16)
        t3b2 = sb2("t3b2", [128, D], BF16)
        u_b = sb2("u_b", [128, D], BF16)
        u2_b = sb2("u2_b", [128, D], BF16)
        r_b = sb2("r_b", [128, D], F32)
        vv_b = sb2("vv_b", [128, D], F32)
        x2_sb = sb2("x2_sb", [128, TPC, D], F32)
        h2A = sb2("h2A", [128, D], BF16)
        h2_t = sb2("h2_t", [128, TPC, D], BF16)
        st2 = sb2("st2", [128, 16], F32)
        bn3 = sb2("bn3", [128, TPC, 12], F32)
        mv3 = sb2("mv3", [128, TPC, 2], F32)
        DEN_ps = ps2("DEN_ps", [NH, CT])
        RB_ps = ps2("RB_ps", [128, 2 * CT])
        QO_ps = ps2("QO_ps", [128, 2 * CT])
        WO_ps = ps2("WO_ps", [128, D])

        xld2 = [S("xld2a"), S("xld2b")]
        qTl = [S("qTla"), S("qTlb")]
        xld2_mk = [0] * (NCH + 2)
        qTl_mk = [0] * (NCH + 2)
        v2 = S("v2")
        s2 = S("s2")
        g2 = S("g2")
        t2 = S("t2")
        x2st = S("x2st")
        h2st = S("h2st")
        den_free = [0] * NCH
        qT_free = [0] * NCH
        x_free_v = [0] * NCH
        x_free_s = [0] * NCH
        x_free_g = [0] * NCH
        wo_free = [0] * NT
        r_done = [0] * NT
        h2a_done = [0] * NT
        wo_last_ldw = [0] * NCH
        agg3_mk = [0] * NCH
        rs3q_mark = [0] * NCH
        nm3_mark = [0] * NCH

        with nc.Block("p2a") as blk:
            def _sy0(sy):
                qspl[0].wait(sy)
                qspl[1].wait(sy)
                xld2_mk[0] = xld2[0].inc(sy.dma_start(out=x_sb[:, 0], in_=r3(x_in[tok(0)])), 16)
                for kc in range(8):
                    qTl[0].inc(sy.dma_start_transpose(out=qT[:, 0, kc, :],
                                                      in_=q_scr[tok(0), kc * 128:(kc + 1) * 128]), 16)
                qTl_mk[0] = qTl[0].n
            blk.sync(_sy0)

            for c in range(NCH):
                par = c % 2

                def _syc(sy, c=c):
                    if c + 1 < NCH:
                        if c >= 1:
                            sy.wait_ge(v2.h, x_free_v[c - 1])
                            sy.wait_ge(s2.h, x_free_s[c - 1])
                            sy.wait_ge(t2.h, qT_free[c - 1])
                        sy.wait_ge(xld2[(c + 1) % 2].h, xld2[(c + 1) % 2].n)
                        sy.wait_ge(qTl[(c + 1) % 2].h, qTl[(c + 1) % 2].n)
                        xld2_mk[c + 1] = xld2[(c + 1) % 2].inc(
                            sy.dma_start(out=x_sb[:, (c + 1) % 2], in_=r3(x_in[tok(c + 1)])), 16)
                        for kc in range(8):
                            qTl[(c + 1) % 2].inc(
                                sy.dma_start_transpose(out=qT[:, (c + 1) % 2, kc, :],
                                                       in_=q_scr[tok(c + 1), kc * 128:(kc + 1) * 128]), 16)
                        qTl_mk[c + 1] = qTl[(c + 1) % 2].n
                blk.sync(_syc)

                def _td(tn, c=c):
                    tn.wait_ge(qTl[c % 2].h, qTl_mk[c])
                    if c >= 1:
                        tn.wait_ge(v2.h, den_free[c - 1])
                    for p in range(NPAIR):
                        m = tn.matmul(DEN_ps[:], ksbd[:, p * NH:(p + 1) * NH],
                                      qT[:, par, p, :], start=(p == 0), stop=(p == NPAIR - 1))
                    t2.inc(m)
                blk.tensor(_td)
                tden = t2.n

                def _vd(v, c=c, tden=tden):
                    v.wait_ge(t2.h, tden)
                    den_free[c] = v2.inc(v.tensor_scalar(out=de_sb[:], in0=DEN_ps[:],
                                                         scalar1=DEN_EPS, scalar2=None, op0=ALU.add))
                    v.wait_ge(v2.h, v2.n)
                    v2.inc(v.reciprocal(de_sb[:], de_sb[:]))
                    v.wait_ge(v2.h, v2.n)
                    v2.inc(v.tensor_copy(out=rcp_bf[:], in_=de_sb[:]))
                blk.vector(_vd)
                vrcp = v2.n

                sq2 = SQueue(s2)
                rbev = [0] * NPAIR
                aev = [0] * NPAIR
                for p in range(NPAIR):
                    def _tp(tn, p=p, c=c):
                        if p == 0:
                            tn.wait_ge(v2.h, vrcp)
                        if p >= 2:
                            tn.wait_ge(v2.h, rbev[p - 2])
                        m = tn.matmul(RB_ps[:, (p % 2) * CT:(p % 2 + 1) * CT],
                                      eexp[:, p * 128:(p + 1) * 128], rcp_bf[:],
                                      start=True, stop=True)
                        t2.inc(m)
                        if p >= 2:
                            tn.wait_ge(v2.h, aev[p - 2])
                        if c >= 1 and p == 0:
                            tn.wait_ge(v2.h, wo_free[c * TPC - 1])
                        m = tn.matmul(QO_ps[:, (p % 2) * CT:(p % 2 + 1) * CT],
                                      kv_sb[:, p * 128:(p + 1) * 128],
                                      qT[:, par, p, :], start=True, stop=True)
                        t2.inc(m)
                    blk.tensor(_tp)
                    tpp = t2.n

                    def _vp(v, p=p, tpp=tpp):
                        v.wait_ge(t2.h, tpp - 1)
                        rbev[p] = v2.inc(v.tensor_copy(out=rb_sb[:, p % 2, :],
                                                       in_=RB_ps[:, (p % 2) * CT:(p % 2 + 1) * CT]))
                        v.wait_ge(t2.h, tpp)
                        v.wait_ge(v2.h, rbev[p])
                        aev[p] = v2.inc(v.tensor_mul(out=attn_f[:, p, :],
                                                     in0=QO_ps[:, (p % 2) * CT:(p % 2 + 1) * CT],
                                                     in1=rb_sb[:, p % 2, :]))
                    blk.vector(_vp)
                qT_free[c] = t2.n

                twms = {}
                vx2 = {}
                gus = {}
                t3mks = {}
                vh2m = {}
                for t in range(TPC):
                    gt = c * TPC + t

                    def _tw(tn, t=t, gt=gt):
                        if t == 0:
                            tn.wait_ge(v2.h, aev[NPAIR - 1])
                        if gt > 0:
                            tn.wait_ge(v2.h, wo_free[gt - 1])
                        for kc in range(8):
                            for hh in range(2):
                                m = tn.matmul(WO_ps[:, hh * 512:(hh + 1) * 512],
                                              attn_f[:, kc, t * 128:(t + 1) * 128],
                                              wout_sb[:, kc, hh * 512:(hh + 1) * 512],
                                              start=(kc == 0), stop=(kc == 7))
                        t2.inc(m)
                    blk.tensor(_tw)
                    twms[t] = t2.n

                    def _sep(s, c=c, t=t, gt=gt):
                        s.wait_ge(xld2[c % 2].h, xld2_mk[c])
                        if t > 0:
                            s.wait_ge(g2.h, gus[t - 1])
                        return s.activation(t1b2[:], x_sb[:, c % 2, t, :], AF.Identity,
                                            bias=nmurs1[:, gt:gt + 1], scale=rs1[:, gt:gt + 1])
                    st1 = sq2.add(_sep)

                    def _gep(gp, gt=gt, st1=st1, t=t):
                        gp.wait_ge(s2.h, st1)
                        if gt > 0:
                            gp.wait_ge(v2.h, r_done[gt - 1])
                        g2.inc(gp.tensor_mul(out=u_b[:], in0=t1b2[:], in1=A_b[:]))
                        gp.wait_ge(g2.h, g2.n)
                        gus[t] = g2.inc(gp.tensor_add(out=u2_b[:], in0=u_b[:], in1=B_b[:]))
                    blk.gpsimd(_gep)

                    def _vep(v, c=c, t=t, gt=gt):
                        v.wait_ge(g2.h, gus[t])
                        r_done[gt] = v2.inc(v.tensor_add(out=r_b[:], in0=u2_b[:],
                                                         in1=x_sb[:, c % 2, t, :]))
                        v.wait_ge(t2.h, twms[t])
                        v.wait_ge(v2.h, v2.n)
                        if c >= 1 and t == 0:
                            v.wait_ge(x2st.h, 16 * c)
                        wo_free[gt] = v2.inc(v.tensor_add(out=x2_sb[:, t, :], in0=r_b[:],
                                                          in1=WO_ps[:]))
                        v.wait_ge(v2.h, v2.n)
                        v2.inc(v.bn_stats(out=bn3[:, t, 0:6], in_=x2_sb[:, t, 0:512]))
                        vx2[t] = v2.inc(v.bn_stats(out=bn3[:, t, 6:12], in_=x2_sb[:, t, 512:1024]))
                    blk.vector(_vep)

                def _vagg3(v, c=c):
                    v.wait_ge(v2.h, vx2[TPC - 1])
                    if c >= 1:
                        v.wait_ge(s2.h, rs3q_mark[c - 1])
                    for t in range(TPC):
                        v2.inc(v.bn_aggr(out=mv3[:, t, :], in_=bn3[:, t, :]))
                    agg3_mk[c] = v2.n
                blk.vector(_vagg3)

                def _srs3(s, c=c):
                    s.wait_ge(v2.h, agg3_mk[c])
                    for t in range(TPC):
                        m = s.activation(mv3[:, t, 1:2], mv3[:, t, 1:2],
                                         AF.Sqrt, bias=epsc[:, 0:1])
                    return m
                rs3q_mark[c] = sq2.add(_srs3, inc=4)

                def _vnm3(v, c=c):
                    v.wait_ge(s2.h, rs3q_mark[c])
                    for t in range(TPC):
                        gt = c * TPC + t
                        v2.inc(v.reciprocal(rs3[:, gt:gt + 1], mv3[:, t, 1:2]))
                    v.wait_ge(v2.h, v2.n)
                    for t in range(TPC):
                        gt = c * TPC + t
                        v2.inc(v.tensor_scalar(out=nmurs3[:, gt:gt + 1], in0=mv3[:, t, 0:1],
                                               scalar1=rs3[:, gt:gt + 1], scalar2=-1.0,
                                               op0=ALU.mult, op1=ALU.mult))
                    nm3_mark[c] = v2.n
                blk.vector(_vnm3)

                for t in range(TPC):
                    gt = c * TPC + t

                    def _sh2(s, c=c, t=t, gt=gt):
                        s.wait_ge(v2.h, nm3_mark[c])
                        if t > 0:
                            s.wait_ge(v2.h, vh2m[t - 1])
                        return s.activation(t3b2[:], x2_sb[:, t, :], AF.Identity,
                                            bias=nmurs3[:, gt:gt + 1], scale=rs3[:, gt:gt + 1])
                    t3mks[t] = sq2.add(_sh2)

                    def _vh2(v, c=c, t=t):
                        if t > 0:
                            v.wait_ge(v2.h, v2.n)
                        v.wait_ge(s2.h, t3mks[t])
                        vh2m[t] = v2.inc(v.tensor_mul(out=h2A[:], in0=t3b2[:], in1=cb1_b[:]))
                        v.wait_ge(v2.h, v2.n)
                        if c >= 1 and t == 0:
                            v.wait_ge(h2st.h, 16 * c)
                        v2.inc(v.tensor_add(out=h2_t[:, t, :], in0=h2A[:], in1=sbb_b[:]))
                    blk.vector(_vh2)

                sq2.flush(blk)

                x_free_v[c] = v2.n
                x_free_s[c] = s2.n
                x_free_g[c] = g2.n
                vln = v2.n
                gln = g2.n

                def _gspill(sc, c=c, vln=vln, gln=gln):
                    sc.wait_ge(v2.h, vln)
                    sc.wait_ge(x2st.h, x2st.n)
                    x2st.inc(sc.dma_start(out=r3(x2_scr[tok(c)]), in_=x2_sb[:]), 16)
                    sc.wait_ge(h2st.h, h2st.n)
                    h2st.inc(sc.dma_start(out=r3(h2_scr[tok(c)]), in_=h2_t[:]), 16)
                blk.scalar(_gspill)

            def _gfin(sc):
                x2st.wait(sc)
                h2st.wait(sc)
            blk.scalar(_gfin)

    es_wo.close()

    with ExitStack() as es3:
        sb3 = lambda n, s, dt: es3.enter_context(nc.sbuf_tensor(n, s, dt))
        ps3 = lambda n, s: es3.enter_context(nc.psum_tensor(n, s, F32))
        h2f = sb3("h2f", [128, 2, 8, CT], BF16)
        w13_sb = sb3("w13_sb", [128, 2, 8, D], BF16)
        w2_sb = sb3("w2_sb", [128, 2, 2, 4, D], BF16)
        act_sl = sb3("act_sl", [128, NSLOT, 4, CT], BF16)
        gbuf = sb3("gbuf", [128, 4, CT], BF16)
        cE = sb3("cE", [128, 2, CT], BF16)
        sc_all = sb3("sc_all", [128, TPC * NE], F32)
        bb_all = sb3("bb_all", [128, TPC * NE], F32)
        oh_all = sb3("oh_all", [128, TPC * NE], F32)
        pk_t = sb3("pk_t", [128, TPC * NE], F32)
        sel_pk = sb3("sel_pk", [128, TPC * NE], F32)
        mx_s = sb3("mx_s", [128, TPC], F32)
        ws_s = sb3("ws_s", [128, TPC], F32)
        comb_bf = sb3("comb_bf", [128, TPC * NE], BF16)
        comb_f = sb3("comb_f", [NH, CT], BF16)
        x2_sb3 = sb3("x2_sb3", [128, 2, TPC, D], F32)
        yf_b = sb3("yf_b", [128, TPC, D], F32)
        ob_b = sb3("ob_b", [128, TPC, D], F32)
        I1_ps = ps3("I1_ps", [128, 2 * 512])
        Y_ps = ps3("Y_ps", [128, 2 * D])
        CE_ps = ps3("CE_ps", [128, 2 * 512])

        h2l = [S("h2la"), S("h2lb")]
        w13l = [S("w13l0"), S("w13l1")]
        w2l = {(p, q): S(f"w2l{p}{q}") for p in range(2) for q in range(2)}
        x2l = [S("x2la"), S("x2lb")]
        h2l_mk = [0] * (NCH + 2)
        x2l_mk = [0] * (NCH + 2)
        w13_mk = {}
        w2_mk = {}
        v3 = S("v3")
        s3 = S("s3")
        g3 = S("g3")
        t3 = S("t3")
        outst = S("outst")
        i1done = {}
        tvec = {}
        actv = {}
        ymm = {}
        yev = {}
        ceev = {}
        ce_mm = {}
        gate_done = [0] * NCH
        gate_sig = [0] * NCH
        comb_done = [0] * NCH
        ct_free = [0] * NCH
        h2f_free = [0] * NCH
        x2_free = [0] * NCH
        vzmul = [0] * NCH
        w2slot_user = {}
        w13_user = {}

        eidx = lambda c, e: c * NEX + e

        with nc.Block("p2b") as blk:
            def _sy00(sy):
                wcast.wait(sy)
                h2st.wait(sy)
                for kc in range(8):
                    h2l[0].inc(sy.dma_start_transpose(out=h2f[:, 0, kc, :],
                                                      in_=h2_scr[tok(0), kc * 128:(kc + 1) * 128]), 16)
                h2l_mk[0] = h2l[0].n
            blk.sync(_sy00)

            def _gx0(gp):
                x2st.wait(gp)
                x2l_mk[0] = x2l[0].inc(gp.dma_start(out=x2_sb3[:, 0], in_=r3(x2_scr[tok(0)])), 16)
            blk.gpsimd(_gx0)

            tct_mm = [0] * NCH
            vct_mark = [0] * NCH

            def em_gate(c):
                def _tg(tn, c=c):
                    tn.wait_ge(h2l[c % 2].h, h2l_mk[c])
                    if c >= 1:
                        tn.wait_ge(v3.h, vct_mark[c - 1])
                        tn.wait_ge(s3.h, gate_sig[c - 1])
                    for t in range(TPC):
                        for kc in range(8):
                            m = tn.matmul(CE_ps[:, 512 + t * NE:512 + (t + 1) * NE],
                                          h2f[:, c % 2, kc, t * 128:(t + 1) * 128],
                                          gatew_sb[:, kc, :],
                                          start=(kc == 0), stop=(kc == 7))
                    gate_done[c] = t3.inc(m)
                blk.tensor(_tg)

            def em_sig(c, sq):
                def _sg(s, c=c):
                    s.wait_ge(t3.h, gate_done[c])
                    if c >= 1:
                        s.wait_ge(v3.h, comb_done[c - 1])
                    return s.activation(sc_all[:], CE_ps[:, 512:512 + TPC * NE], AF.Sigmoid)
                gate_sig[c] = sq.add(_sg)

            def em_topk(c, phase):
                def _v(v, c=c, phase=phase):
                    if phase == 0:
                        v.wait_ge(s3.h, gate_sig[c])
                        for t in range(TPC):
                            v3.inc(v.tensor_add(out=bb_all[:, t * NE:(t + 1) * NE],
                                                in0=sc_all[:, t * NE:(t + 1) * NE], in1=gateb_b[:]))
                        v.wait_ge(v3.h, v3.n)
                        its = [0]
                    elif phase == 1:
                        its = [1, 2]
                    else:
                        its = [3]
                    for it4 in its:
                        for t in range(TPC):
                            v3.inc(v.reduce_max(mx_s[:, t:t + 1], bb_all[:, t * NE:(t + 1) * NE],
                                                axis=AX.X))
                        v.wait_ge(v3.h, v3.n)
                        for t in range(TPC):
                            v3.inc(v.tensor_scalar(out=oh_all[:, t * NE:(t + 1) * NE],
                                                   in0=bb_all[:, t * NE:(t + 1) * NE],
                                                   scalar1=mx_s[:, t:t + 1], scalar2=0.0,
                                                   op0=ALU.subtract, op1=ALU.is_ge))
                        v.wait_ge(v3.h, v3.n)
                        for t in range(TPC):
                            sl = slice(t * NE, (t + 1) * NE)
                            if it4 == 0:
                                v3.inc(v.tensor_mul(out=sel_pk[:, sl], in0=oh_all[:, sl],
                                                    in1=sc_all[:, sl]))
                            else:
                                v3.inc(v.tensor_mul(out=pk_t[:, sl], in0=oh_all[:, sl],
                                                    in1=sc_all[:, sl]))
                        if it4 < 3:
                            for t in range(TPC):
                                sl = slice(t * NE, (t + 1) * NE)
                                v3.inc(v.scalar_tensor_tensor(out=bb_all[:, sl], in0=oh_all[:, sl],
                                                              scalar=-1e30, in1=bb_all[:, sl],
                                                              op0=ALU.mult, op1=ALU.add))
                        v.wait_ge(v3.h, v3.n)
                        if it4 > 0:
                            for t in range(TPC):
                                sl = slice(t * NE, (t + 1) * NE)
                                v3.inc(v.tensor_add(out=sel_pk[:, sl], in0=sel_pk[:, sl],
                                                    in1=pk_t[:, sl]))
                            v.wait_ge(v3.h, v3.n)
                    if phase == 2:
                        for t in range(TPC):
                            v3.inc(v.reduce_sum(ws_s[:, t:t + 1], sel_pk[:, t * NE:(t + 1) * NE],
                                                axis=AX.X))
                        v.wait_ge(v3.h, v3.n)
                        v3.inc(v.reciprocal(ws_s[:], ws_s[:]))
                        v.wait_ge(v3.h, v3.n)
                        if c >= 1:
                            v.wait_ge(t3.h, tct_mm[c - 1])
                        for t in range(TPC):
                            comb_done[c] = v3.inc(v.tensor_scalar(out=comb_bf[:, t * NE:(t + 1) * NE],
                                                                  in0=sel_pk[:, t * NE:(t + 1) * NE],
                                                                  scalar1=ws_s[:, t:t + 1],
                                                                  scalar2=ROUTE_SCALE,
                                                                  op0=ALU.mult, op1=ALU.mult))
                blk.vector(_v)

            def em_tct(c):
                def _t(tn, c=c):
                    tn.wait_ge(v3.h, comb_done[c])
                    for t in range(TPC):
                        m = tn.matmul(CE_ps[0:NH, 512 + t * 128:512 + (t + 1) * 128],
                                      comb_bf[:, t * NE:(t + 1) * NE], ident[:],
                                      start=True, stop=True)
                    tct_mm[c] = t3.inc(m)
                blk.tensor(_t)

            def em_vct(c):
                def _v(v, c=c):
                    v.wait_ge(t3.h, tct_mm[c])
                    if c >= 1:
                        v.wait_ge(t3.h, ce_mm.get(eidx(c - 1, NE - 1), 0))
                    vct_mark[c] = v3.inc(v.tensor_copy(out=comb_f[:], in_=CE_ps[0:NH, 512:1024]))
                blk.vector(_v)

            for c in range(NCH):
                par = c % 2

                def _gxp(gp, c=c):
                    if c + 1 < NCH:
                        if c >= 1:
                            gp.wait_ge(v3.h, x2_free[c - 1])
                        gp.wait_ge(x2l[(c + 1) % 2].h, x2l[(c + 1) % 2].n)
                        x2l_mk[c + 1] = x2l[(c + 1) % 2].inc(
                            gp.dma_start(out=x2_sb3[:, (c + 1) % 2], in_=r3(x2_scr[tok(c + 1)])), 16)
                blk.gpsimd(_gxp)

                sq3 = SQueue(s3)

                if c == 0:
                    em_gate(0)
                    em_sig(0, sq3)
                    em_topk(0, 0)
                    em_topk(0, 1)
                    em_topk(0, 2)
                    em_tct(0)
                    em_vct(0)

                for gi, grp in enumerate(EGROUPS):
                    for e in grp:
                        ei = eidx(c, e)
                        slot = ei % NSLOT
                        pos = grp.index(e)
                        _mc = {}

                        def _syw(sy, c=c, e=e, ei=ei, gi=gi, pos=pos):
                            s13 = ei % 2
                            if s13 in w13_user:
                                sy.wait_ge(t3.h, i1done[w13_user[s13]])
                            w13_user[s13] = ei
                            if c == 0 and e < 2:
                                wcast.wait(sy)
                            sy.wait_ge(w13l[s13].h, w13l[s13].n)
                            w13_mk[ei] = w13l[s13].inc(
                                sy.dma_start(out=w13_sb[:, s13],
                                             in_=w13bf[e].rearrange("(k p) n -> p k n", p=128)), 16)
                            key = (gi % 2, pos)
                            if key in w2slot_user:
                                pc, pgi = w2slot_user[key]
                                sy.wait_ge(t3.h, ymm.get((pc, pgi, TPC - 1), 0))
                            w2slot_user[key] = (c, gi)
                            sy.wait_ge(w2l[key].h, w2l[key].n)
                            w2_mk[ei] = w2l[key].inc(
                                sy.dma_start(out=w2_sb[:, gi % 2, pos],
                                             in_=w2bf[e].rearrange("(k p) n -> p k n", p=128)), 16)
                        blk.sync(_syw)

                        if e < NE:
                            def _tce(tn, c=c, e=e, ei=ei):
                                tn.wait_ge(v3.h, vct_mark[c])
                                if e >= 1:
                                    tn.wait_ge(v3.h, ceev.get(eidx(c, e - 1), 0))
                                elif c >= 1:
                                    tn.wait_ge(v3.h, ceev.get(eidx(c - 1, NE - 1), 0))
                                m = tn.matmul(CE_ps[:, 0:512],
                                              erow[:, e * 128:(e + 1) * 128], comb_f[:],
                                              start=True, stop=True)
                                ce_mm[ei] = t3.inc(m)
                                _mc["ce"] = ce_mm[ei]
                            blk.tensor(_tce)

                            def _vce(v, e=e, ei=ei):
                                v.wait_ge(t3.h, _mc["ce"])
                                ceev[ei] = v3.inc(v.tensor_copy(out=cE[:, e % 2, :],
                                                                in_=CE_ps[:, 0:512]))
                            blk.vector(_vce)

                        s3_base = s3.n
                        v3_base = v3.n

                        def _tm1(tn, c=c, e=e, ei=ei, s3_base=s3_base, v3_base=v3_base):
                            tn.wait_ge(w13l[ei % 2].h, w13_mk[ei])
                            tn.wait_ge(h2l[c % 2].h, h2l_mk[c])
                            if ei >= 1:
                                tn.wait_ge(s3.h, s3_base)
                                tn.wait_ge(v3.h, tvec.get(ei - 1, 0))
                            for oc in range(8):
                                if oc in (2, 3, 4, 5):
                                    tn.wait_ge(s3.h, s3_base + (oc - 1))
                                elif oc in (6, 7):
                                    tn.wait_ge(v3.h, v3_base + (oc - 5))
                                for kc in range(8):
                                    m = tn.matmul(I1_ps[:, (oc % 2) * 512:(oc % 2 + 1) * 512],
                                                  w13_sb[:, ei % 2, kc, oc * 128:(oc + 1) * 128],
                                                  h2f[:, par, kc, :],
                                                  start=(kc == 0), stop=(kc == 7))
                                _mc[("mm", oc)] = t3.inc(m)
                        blk.tensor(_tm1)
                        i1done[ei] = t3.n
                        h2f_free[c] = t3.n

                        for j in range(4):
                            def _sm1(s, ei=ei, j=j, _mc=_mc):
                                s.wait_ge(t3.h, _mc[("mm", j)])
                                if ei >= 1:
                                    s.wait_ge(v3.h, tvec.get(ei - 1, 0))
                                return s.activation(gbuf[:, j, :],
                                                    I1_ps[:, (j % 2) * 512:(j % 2 + 1) * 512],
                                                    AF.Silu)
                            sq3.add(_sm1)
                        assert s3.n == s3_base + 4

                        def _vm1(v, c=c, e=e, ei=ei, slot=slot, s3_base=s3_base):
                            if ei >= 1:
                                v.wait_ge(v3.h, actv[ei - 1])
                            if ei >= NSLOT:
                                pe = ei - NSLOT
                                pc, pee = divmod(pe, NEX)
                                v.wait_ge(t3.h, ymm.get((pc, EGROUPS_OF[pee], TPC - 1), 0))
                            for j in range(4):
                                v.wait_ge(t3.h, _mc[("mm", 4 + j)])
                                v.wait_ge(s3.h, s3_base + j + 1)
                                tvec[ei] = v3.inc(v.tensor_mul(out=act_sl[:, slot, j, :],
                                                               in0=gbuf[:, j, :],
                                                               in1=I1_ps[:, (j % 2) * 512:(j % 2 + 1) * 512]))
                            v.wait_ge(v3.h, v3.n)
                            if e < NE:
                                v.wait_ge(v3.h, ceev[ei])
                                for j in range(4):
                                    actv[ei] = v3.inc(v.tensor_mul(out=act_sl[:, slot, j, :],
                                                                   in0=act_sl[:, slot, j, :],
                                                                   in1=cE[:, e % 2, :]))
                            else:
                                actv[ei] = tvec[ei]
                        blk.vector(_vm1)
                        assert v3.n == v3_base + (1 if e < NE else 0) * 0 + 8 or True

                    if gi == 0:
                        def _syp(sy, c=c):
                            if c + 1 < NCH:
                                if c >= 1:
                                    sy.wait_ge(t3.h, h2f_free[c - 1])
                                sy.wait_ge(h2l[(c + 1) % 2].h, h2l[(c + 1) % 2].n)
                                for kc in range(8):
                                    h2l[(c + 1) % 2].inc(
                                        sy.dma_start_transpose(out=h2f[:, (c + 1) % 2, kc, :],
                                                               in_=h2_scr[tok(c + 1), kc * 128:(kc + 1) * 128]), 16)
                                h2l_mk[c + 1] = h2l[(c + 1) % 2].n
                        blk.sync(_syp)

                    if gi == len(EGROUPS) - 1 and c + 1 < NCH:
                        em_tct(c + 1)
                        em_vct(c + 1)

                    for t in range(TPC):
                        def _tm2(tn, c=c, gi=gi, grp=grp, t=t):
                            for e2 in grp:
                                tn.wait_ge(v3.h, actv[eidx(c, e2)])
                                tn.wait_ge(w2l[(gi % 2, grp.index(e2))].h, w2_mk[eidx(c, e2)])
                            prev = None
                            if t >= 2:
                                prev = (c, gi, t - 2)
                            elif gi >= 1:
                                prev = (c, gi - 1, t)
                            elif c >= 1:
                                prev = (c - 1, len(EGROUPS) - 1, t)
                            if prev is not None and prev in yev:
                                tn.wait_ge(v3.h, yev[prev])
                            for e2 in grp:
                                slot2 = eidx(c, e2) % NSLOT
                                for kc in range(4):
                                    for hh in range(2):
                                        m = tn.matmul(
                                            Y_ps[:, (t % 2) * D + hh * 512:(t % 2) * D + (hh + 1) * 512],
                                            act_sl[:, slot2, kc, t * 128:(t + 1) * 128],
                                            w2_sb[:, gi % 2, grp.index(e2), kc, hh * 512:(hh + 1) * 512],
                                            start=(e2 == grp[0] and kc == 0),
                                            stop=(e2 == grp[-1] and kc == 3))
                            ymm[(c, gi, t)] = t3.inc(m)
                        blk.tensor(_tm2)

                        def _vy(v, c=c, gi=gi, t=t):
                            v.wait_ge(t3.h, ymm[(c, gi, t)])
                            if gi == 0:
                                yev[(c, gi, t)] = v3.inc(v.tensor_copy(out=yf_b[:, t, :],
                                                                       in_=Y_ps[:, (t % 2) * D:(t % 2 + 1) * D]))
                            else:
                                yev[(c, gi, t)] = v3.inc(v.tensor_add(out=yf_b[:, t, :],
                                                                      in0=yf_b[:, t, :],
                                                                      in1=Y_ps[:, (t % 2) * D:(t % 2 + 1) * D]))
                        blk.vector(_vy)

                    if c + 1 < NCH:
                        if gi == 0:
                            em_gate(c + 1)
                            em_sig(c + 1, sq3)
                        elif gi == 2:
                            em_topk(c + 1, 0)
                        elif gi == 4:
                            em_topk(c + 1, 1)
                        elif gi == 6:
                            em_topk(c + 1, 2)

                sq3.flush(blk)

                def _vz(v, c=c, par=par):
                    v.wait_ge(v3.h, yev[(c, len(EGROUPS) - 1, TPC - 1)])
                    if c >= 1:
                        v.wait_ge(outst.h, 16 * c)
                    for t in range(TPC):
                        v3.inc(v.tensor_mul(out=ob_b[:, t, :], in0=yf_b[:, t, :], in1=gb_b[:]))
                    vzmul[c] = v3.n
                    v.wait_ge(v3.h, v3.n)
                    v.wait_ge(x2l[c % 2].h, x2l_mk[c])
                    for t in range(TPC):
                        v3.inc(v.tensor_add(out=ob_b[:, t, :], in0=ob_b[:, t, :],
                                            in1=x2_sb3[:, par, t, :]))
                    x2_free[c] = v3.n
                blk.vector(_vz)
                vzm = v3.n

                def _gout(gp, c=c, vzm=vzm):
                    gp.wait_ge(v3.h, vzm)
                    gp.wait_ge(outst.h, outst.n)
                    outst.inc(gp.dma_start(out=r3(out[tok(c)]), in_=ob_b[:]), 16)
                blk.gpsimd(_gout)

            def _gend(gp):
                outst.wait(gp)
            blk.gpsimd(_gend)

    es.close()
    return nc


def _host_prep(inputs):
    f32 = np.float32
    asn = np.ascontiguousarray
    w13 = np.concatenate([np.asarray(inputs["L0_ffn_w13"], f32),
                          np.asarray(inputs["L0_ffn_w13_s"], f32)[None]], axis=0)
    w2 = np.concatenate([np.asarray(inputs["L0_ffn_w2"], f32),
                         np.asarray(inputs["L0_ffn_w2_s"], f32)[None]], axis=0)
    w13T = asn(np.transpose(w13, (0, 2, 1)))
    w2T = asn(np.transpose(w2, (0, 2, 1)))
    gatewT = asn(np.asarray(inputs["L0_ffn_gate_w"], f32).T)
    ident = np.eye(128, dtype=ml_dtypes.bfloat16)
    onesrow = np.ones((1, 128), dtype=ml_dtypes.bfloat16)
    eexp = np.zeros((NH, NPAIR * 128), dtype=ml_dtypes.bfloat16)
    for p in range(NPAIR):
        for j in range(128):
            eexp[2 * p + j // 64, p * 128 + j] = 1
    erow = np.zeros((NH, NE * 128), dtype=ml_dtypes.bfloat16)
    for e in range(NE):
        erow[e, e * 128:(e + 1) * 128] = 1
    return {
        "wmod_in": asn(np.asarray(inputs["L0_wmod"], f32)),
        "gamma_in": asn(np.asarray(inputs["L0_attn_gamma"], f32).reshape(1, D)),
        "beta_in": asn(np.asarray(inputs["L0_attn_beta"], f32).reshape(1, D)),
        "wqkv_in": asn(np.asarray(inputs["L0_attn_Wqkv"], f32)),
        "wout_in": asn(np.asarray(inputs["L0_attn_Wout"], f32)),
        "gatewT_in": gatewT,
        "gateb_in": asn(np.asarray(inputs["L0_ffn_gate_b"], f32).reshape(1, NE)),
        "w13T_in": w13T,
        "w2T_in": w2T,
        "ident_in": ident,
        "onesrow_in": onesrow,
        "eexp_in": eexp,
        "erow_in": erow,
    }


def kernel(**inputs):
    x = np.asarray(inputs["L0_attn_x"], np.float32)
    cmat = np.asarray(inputs["L0_c"], np.float32)
    B, L, _ = x.shape
    com = _host_prep(inputs)
    nc = build_program(L)
    in_maps = []
    for b in range(B):
        m = dict(com)
        m["x_in"] = np.ascontiguousarray(x[b])
        m["c_row"] = np.ascontiguousarray(cmat[b:b + 1])
        in_maps.append(m)
    res = run_bass_kernel_spmd(nc, in_maps, core_ids=list(range(B)))
    return np.stack([res.results[b]["out"] for b in range(B)], axis=0)


# version 28
# speedup vs baseline: 1.1240x; 1.0110x over previous
import numpy as np
import ml_dtypes
from contextlib import ExitStack

import concourse.bass as bass
import concourse.mybir as mybir
from concourse.bass_utils import run_bass_kernel_spmd

F32 = mybir.dt.float32
BF16 = mybir.dt.bfloat16
AF = mybir.ActivationFunctionType
ALU = mybir.AluOpType
AX = mybir.AxisListType

D = 1024
DQKV = 3 * D
NH = 16
NPAIR = 8
NE = 16
NEX = 17
INTER = 512
ROUTE_SCALE = 2.5
LN_EPS = 1e-5
DEN_EPS = 1e-6
CT = 512
TPC = 4
RCPD = float(1.0 / D)
RSQD = float((1.0 / D) ** 0.5)
EGROUPS = [(0, 1), (2, 3), (4, 5), (6, 7), (8, 9), (10, 11), (12, 13), (14, 15), (16,)]
EGROUPS_OF = {e: gi for gi, grp in enumerate(EGROUPS) for e in grp}
NSLOT = 3


def _preload_tab(nc, s, idx):
    s.add_instruction(mybir.InstLoadActFuncSet(
        name=nc.get_next_instruction_name(), act_func_set_id=idx, ins=[], outs=[]))


class Sem:
    def __init__(self, es, nc, name):
        self.h = es.enter_context(nc.semaphore(name))
        self.n = 0

    def inc(self, inst, k=1):
        inst.then_inc(self.h, k)
        self.n += k
        return self.n

    def wait(self, eng, n=None):
        eng.wait_ge(self.h, self.n if n is None else n)


class SQueue:

    def __init__(self, sem):
        self.sem = sem
        self.q = []

    def add(self, emitfn, inc=1):
        mark = None
        if inc:
            self.sem.n += inc
            mark = self.sem.n
        self.q.append((emitfn, mark, inc))
        return mark

    def flush(self, blk):
        q, self.q = self.q, []
        if not q:
            return

        def body(s):
            for fn, mark, inc in q:
                inst = fn(s)
                if mark is not None:
                    inst.then_inc(self.sem.h, inc)
        blk.scalar(body)


def build_program(L):
    assert L % CT == 0
    NCH = L // CT
    NT = L // 128
    nc = bass.Bass("TRN2", target_bir_lowering=False)

    dr = lambda n, s, dt, k="Internal": nc.dram_tensor(n, s, dt, kind=k).ap()
    x_in = dr("x_in", [L, D], F32, "ExternalInput")
    c_row = dr("c_row", [1, D], F32, "ExternalInput")
    wmod_in = dr("wmod_in", [D, 6 * D], F32, "ExternalInput")
    gamma_in = dr("gamma_in", [1, D], F32, "ExternalInput")
    beta_in = dr("beta_in", [1, D], F32, "ExternalInput")
    wqkv_in = dr("wqkv_in", [D, DQKV], F32, "ExternalInput")
    wout_in = dr("wout_in", [D, D], F32, "ExternalInput")
    gatewT_in = dr("gatewT_in", [D, NE], F32, "ExternalInput")
    gateb_in = dr("gateb_in", [1, NE], F32, "ExternalInput")
    w13T_in = dr("w13T_in", [NEX, D, D], F32, "ExternalInput")
    w2T_in = dr("w2T_in", [NEX, INTER, D], F32, "ExternalInput")
    ident_in = dr("ident_in", [128, 128], BF16, "ExternalInput")
    onesrow_in = dr("onesrow_in", [1, 128], BF16, "ExternalInput")
    eexp_in = dr("eexp_in", [NH, NPAIR * 128], BF16, "ExternalInput")
    erow_in = dr("erow_in", [NH, NE * 128], BF16, "ExternalInput")
    out = dr("out", [L, D], F32, "ExternalOutput")
    w13bf = dr("w13bf", [NEX, D, D], BF16)
    w2bf = dr("w2bf", [NEX, INTER, D], BF16)
    xn_scr = dr("xn_scr", [L, D], BF16)
    q_scr = dr("q_scr", [L, D], BF16)
    x2_scr = dr("x2_scr", [L, D], F32)
    h2_scr = dr("h2_scr", [L, D], BF16)

    es = ExitStack()
    sbuf = lambda n, s, dt: es.enter_context(nc.sbuf_tensor(n, s, dt))
    S = lambda n: Sem(es, nc, n)
    r3 = lambda ap: ap.rearrange("(t p) d -> p t d", p=128)
    tok = lambda c: slice(c * CT, (c + 1) * CT)

    ca1_b = sbuf("ca1_b", [128, D], BF16)
    sa_b = sbuf("sa_b", [128, D], BF16)
    bw_a = sbuf("bw_a", [96, 512], BF16)
    ones96 = sbuf("ones96", [96, 128], BF16)
    bw_b = sbuf("bw_b", [96, 512], BF16)
    A_b = sbuf("A_b", [128, D], BF16)
    B_b = sbuf("B_b", [128, D], BF16)
    cb1_b = sbuf("cb1_b", [128, D], BF16)
    sbb_b = sbuf("sbb_b", [128, D], BF16)
    ga_b = sbuf("ga_b", [128, D], F32)
    gb_b = sbuf("gb_b", [128, D], F32)
    gateb_b = sbuf("gateb_b", [128, NE], F32)
    ident = sbuf("ident", [128, 128], BF16)
    onesrow = sbuf("onesrow", [1, 128], BF16)
    onescol = sbuf("onescol", [128, 1], BF16)
    eexp = sbuf("eexp", [NH, NPAIR * 128], BF16)
    erow = sbuf("erow", [NH, NE * 128], BF16)
    rs1 = sbuf("rs1", [128, NT], F32)
    nmurs1 = sbuf("nmurs1", [128, NT], F32)
    rs3 = sbuf("rs3", [128, NT], F32)
    nmurs3 = sbuf("nmurs3", [128, NT], F32)
    ksum_sb = sbuf("ksum_sb", [128, NPAIR], F32)
    kv_sb = sbuf("kv_sb", [128, NPAIR * 128], BF16)
    ksbd = sbuf("ksbd", [128, NPAIR * NH], BF16)
    junk4 = sbuf("junk4", [128, TPC, D], BF16)
    epsc = sbuf("epsc", [128, 1], F32)
    gatew_sb = sbuf("gatew_sb", [128, 8, NE], BF16)
    es_wo = ExitStack()
    wout_sb = es_wo.enter_context(nc.sbuf_tensor("wout_sb", [128, 8, D], BF16))
    es_wq = ExitStack()
    wqkv_sb = es_wq.enter_context(nc.sbuf_tensor("wqkv_sb", [128, 8, DQKV], BF16))

    wcast = S("wcast")

    with ExitStack() as es0:
        sb0 = lambda n, s, dt: es0.enter_context(nc.sbuf_tensor(n, s, dt))
        ps0 = lambda n, s: es0.enter_context(nc.psum_tensor(n, s, F32))
        c_sb = sb0("c_sb", [1, D], F32)
        sc_bf = sb0("sc_bf", [1, D], BF16)
        sc_f = sb0("sc_f", [128, 8], BF16)
        mod_bf = sb0("mod_bf", [1, 6 * D], BF16)
        gb_row = sb0("gb_row", [1, NE], BF16)
        wmod_bf = sb0("wmod_bf", [128, 8, 3 * D], BF16)
        sgn_t = sb0("sgn_t", [128, D], F32)
        amx_t = sb0("amx_t", [128, D], F32)
        gammaT = sb0("gammaT", [128, 8], F32)
        betaT = sb0("betaT", [128, 8], BF16)
        mod_ps = ps0("mod_ps", [1, 3 * D])
        bc_ps = ps0("bc_ps", [128, D])
        bw_ps = mod_ps[0:1, 0:512]
        ch = S("p0ch")
        chd = S("p0chd")
        chs = S("p0chs")

        with nc.Block("p0") as blk:
            @blk.gpsimd
            def _(gp):
                chd.inc(gp.dma_start(out=wqkv_sb[:], in_=wqkv_in.rearrange("(k p) n -> p k n", p=128)), 16)
                chd.inc(gp.dma_start(out=wout_sb[:], in_=wout_in.rearrange("(k p) n -> p k n", p=128)), 16)
                chd.inc(gp.dma_start(out=gatew_sb[:], in_=gatewT_in.rearrange("(k p) n -> p k n", p=128)), 16)
                chd.inc(gp.dma_start(out=gb_row[:], in_=gateb_in), 16)
                bTv = beta_in.rearrange("o (kc p) -> kc p o", kc=8)
                for kc in range(8):
                    chd.inc(gp.dma_start(out=betaT[:, kc:kc + 1], in_=bTv[kc]), 16)

            @blk.sync
            def _(sy):
                chs.inc(sy.dma_start(out=c_sb[:], in_=c_row), 16)
                gTv = gamma_in.rearrange("o (kc p) -> kc p o", kc=8)
                for kc in range(8):
                    chs.inc(sy.dma_start(out=gammaT[:, kc:kc + 1], in_=gTv[kc]), 16)
                chs.inc(sy.dma_start(out=ident[:], in_=ident_in), 16)
                chs.inc(sy.dma_start(out=onesrow[:], in_=onesrow_in), 16)
                chs.inc(sy.dma_start(out=eexp[:], in_=eexp_in), 16)
                chs.inc(sy.dma_start(out=erow[:], in_=erow_in), 16)

            def step(eng_attr, fn):
                def body(e):
                    ch.wait(e)
                    chd.wait(e)
                    chs.wait(e)
                    fn(e)
                getattr(blk, eng_attr)(body)

            step("scalar", lambda e: ch.inc(e.activation(c_sb[:], c_sb[:], AF.Silu)))

            def _v0(v):
                ch.inc(v.memset(epsc[:], LN_EPS))
                ch.wait(v)
                ch.inc(v.memset(onescol[:], 1.0))
                ch.wait(v)
                ch.inc(v.memset(ones96[:], 1.0))
                ch.wait(v)
                ch.inc(v.memset(ksum_sb[:], 0.0))
                ch.wait(v)
                ch.inc(v.tensor_copy(out=sc_bf[:], in_=c_sb[:]))
            step("vector", _v0)

            def _t0(tn):
                for kc in range(8):
                    m = tn.matmul(bc_ps[:, kc:kc + 1], sc_bf[0:1, kc * 128:(kc + 1) * 128],
                                  onescol[0:1, 0:1], start=(kc == 0), stop=(kc == 7))
                ch.inc(m)
            step("tensor", _t0)
            step("vector", lambda v: ch.inc(v.tensor_copy(out=sc_f[:], in_=bc_ps[:, 0:8])))

            for half in range(2):
                def _gw(gp, half=half):
                    chd.inc(gp.dma_start(out=wmod_bf[:],
                                        in_=wmod_in[:, half * 3072:(half + 1) * 3072]
                                        .rearrange("(k p) n -> p k n", p=128)), 16)
                step("gpsimd", _gw)

                def _tm(tn, half=half):
                    for ns in range(6):
                        for kc in range(8):
                            m = tn.matmul(mod_ps[0:1, ns * 512:(ns + 1) * 512],
                                          sc_f[:, kc:kc + 1],
                                          wmod_bf[:, kc, ns * 512:(ns + 1) * 512],
                                          start=(kc == 0), stop=(kc == 7))
                    ch.inc(m)
                step("tensor", _tm)
                step("vector", lambda v, half=half: ch.inc(
                    v.tensor_copy(out=mod_bf[0:1, half * 3072:(half + 1) * 3072], in_=mod_ps[:])))

            items = [("sa", mod_bf[0:1, 0:D]), ("ca", mod_bf[0:1, D:2 * D]),
                     ("ga", mod_bf[0:1, 2 * D:3 * D]), ("sb", mod_bf[0:1, 3 * D:4 * D]),
                     ("cb", mod_bf[0:1, 4 * D:5 * D]), ("gb", mod_bf[0:1, 5 * D:6 * D]),
                     ("gateb", gb_row[0:1, 0:NE])]
            for name, src in items:
                def _tb(tn, src=src):
                    n = src.shape[-1]
                    for ns in range((n + 511) // 512):
                        w = min(512, n - ns * 512)
                        m = tn.matmul(bc_ps[:, ns * 512:ns * 512 + w], onesrow[0:1, :],
                                      src[0:1, ns * 512:ns * 512 + w], start=True, stop=True)
                    ch.inc(m)
                step("tensor", _tb)
                if name in ("ga", "gb"):
                    step("scalar", lambda s: ch.inc(s.activation(amx_t[:], bc_ps[:], AF.Abs)))

                def _vb(v, name=name):
                    if name == "sa":
                        ch.inc(v.tensor_copy(out=sa_b[:], in_=bc_ps[:]))
                    elif name == "ca":
                        ch.inc(v.tensor_scalar(out=ca1_b[:], in0=bc_ps[:], scalar1=1.0,
                                               scalar2=None, op0=ALU.add))
                    elif name in ("ga", "gb"):
                        dst = ga_b if name == "ga" else gb_b
                        ch.inc(v.tensor_scalar(out=sgn_t[:], in0=bc_ps[:], scalar1=0.0,
                                               scalar2=2.0, op0=ALU.is_ge, op1=ALU.mult))
                        ch.wait(v)
                        ch.inc(v.tensor_scalar(out=sgn_t[:], in0=sgn_t[:], scalar1=-1.0,
                                               scalar2=None, op0=ALU.add))
                        ch.wait(v)
                        ch.inc(v.tensor_scalar(out=amx_t[:], in0=amx_t[:], scalar1=0.25,
                                               scalar2=None, op0=ALU.max))
                        ch.wait(v)
                        ch.inc(v.tensor_mul(out=dst[:], in0=sgn_t[:], in1=amx_t[:]))
                        if name == "ga":
                            ch.wait(v)
                            ch.inc(v.tensor_mul(out=A_b[:], in0=ga_b[:], in1=ca1_b[:]))
                            ch.wait(v)
                            ch.inc(v.tensor_mul(out=B_b[:], in0=ga_b[:], in1=sa_b[:]))
                    elif name == "sb":
                        ch.inc(v.tensor_copy(out=sbb_b[:], in_=bc_ps[:]))
                    elif name == "cb":
                        ch.inc(v.tensor_scalar(out=cb1_b[:], in0=bc_ps[:], scalar1=1.0,
                                               scalar2=None, op0=ALU.add))
                    elif name == "gateb":
                        ch.inc(v.tensor_copy(out=gateb_b[:], in_=bc_ps[:, 0:NE]))
                step("vector", _vb)

            for seg in range(6):
                def _tbw(tn, seg=seg):
                    for kc in range(8):
                        m = tn.matmul(bw_ps[0:1, :], betaT[:, kc:kc + 1],
                                      wqkv_sb[:, kc, seg * 512:(seg + 1) * 512],
                                      start=(kc == 0), stop=(kc == 7))
                    ch.inc(m)
                step("tensor", _tbw)
                dst = (bw_a if seg < 3 else bw_b)
                row = (seg % 3) * 32
                step("scalar", lambda s, dst=dst, row=row: ch.inc(
                    s.copy(out=dst[row:row + 1, 0:512], in_=bw_ps[0:1, :])))
            for kc in range(8):
                def _sgw(s, kc=kc):
                    ch.inc(s.activation(wqkv_sb[:, kc, :], wqkv_sb[:, kc, :], AF.Identity,
                                        scale=gammaT[:, kc:kc + 1]))
                step("scalar", _sgw)
            for kc in range(8):
                def _vgw(v, kc=kc):
                    ch.inc(v.tensor_mul(out=wout_sb[:, kc, :], in0=wout_sb[:, kc, :],
                                        in1=ga_b[:]))
                step("vector", _vgw)


    with ExitStack() as es1:
        sb1 = lambda n, s, dt: es1.enter_context(nc.sbuf_tensor(n, s, dt))
        ps1 = lambda n, s: es1.enter_context(nc.psum_tensor(n, s, F32))
        x_sb = sb1("x_sb", [128, 2, TPC, D], BF16)
        xn_t = sb1("xn_t", [128, 2, TPC, D], BF16)
        xn_f = sb1("xn_f", [128, 2, 8, CT], BF16)
        q_sb = sb1("q_sb", [128, 2, TPC, D], BF16)
        k_sb = sb1("k_sb", [128, D], BF16)
        v_sb = sb1("v_sb", [128, D], BF16)
        e1k = sb1("e1k", [128, D], BF16)
        t1b = sb1("t1b", [128, 4, D], BF16)
        h_t = sb1("h_t", [128, TPC, D], BF16)
        st_s = sb1("st_s", [128, 32], F32)
        bnx = sb1("bnx", [128, 2, TPC, 12], F32)
        mvx = sb1("mvx", [128, 2, TPC, 2], F32)
        bnh = sb1("bnh", [128, TPC, 12], F32)
        mvh = sb1("mvh", [128, TPC, 2], F32)
        rsh = sb1("rsh", [128, TPC], F32)
        nmh = sb1("nmh", [128, TPC], F32)
        K_ps = ps1("K_ps", [128, D])
        V_ps = ps1("V_ps", [128, D])
        Q_ps = ps1("Q_ps", [128, D])
        KV_ps = ps1("KV_ps", [128, NPAIR * 128])

        xld = [S("xld0"), S("xld1")]
        xnst = [S("xnst0"), S("xnst1")]
        xnfl = [S("xnfl0"), S("xnfl1")]
        qspl = [S("qspl0"), S("qspl1")]
        v1 = S("v1")
        s1 = S("s1")
        g1 = S("g1")
        t1 = S("t1")
        kmm = [0] * NT; vmm = [0] * NT; qmm = [0] * NT; gmm = [0] * NT
        gsum = [0] * NT
        kfin = [0] * NT; vcp = [0] * NT; qfin = [0] * NT; ksev = [0] * NT
        ek_mk = [0] * NT; eq_mk = [0] * NT
        xnf_ok = [0] * NCH
        xld_mk = [0] * (NCH + 2)
        xnst_mk = [0] * (NCH + 2)
        qspl_mk = [0] * (NCH + 2)
        t1last = [0] * (NCH + 1)
        hmkA = [None] * (NCH + 1)
        t2last = [0] * (NCH + 1)
        t2mks_h = [0] * (NCH + 1)
        aggx_mk = [0] * (NCH + 1)
        rsx_mark = [0] * (NCH + 1)
        nmx_mark = [0] * (NCH + 1)
        aggh_mk = [0] * (NCH + 1)
        rsh_mark = [0] * (NCH + 1)
        nmh_mark = [0] * (NCH + 1)

        with nc.Block("p1") as blk:
            def _gx0(gp):
                xld_mk[0] = xld[0].inc(gp.dma_start(out=x_sb[:, 0], in_=r3(x_in[tok(0)])), 16)
                if NCH > 1:
                    xld_mk[1] = xld[1].inc(gp.dma_start(out=x_sb[:, 1], in_=r3(x_in[tok(1)])), 16)
            blk.gpsimd(_gx0)
            castq = [(w13bf[e], w13T_in[e]) for e in range(NEX)] + \
                    [(w2bf[e], w2T_in[e]) for e in range(NEX)]

            def emit_mm_kv(c, t, sq):
                g = c * TPC + t
                par = c % 2

                def _t(tn):
                    if t == 0:
                        tn.wait_ge(xnfl[par].h, xnf_ok[c])
                    if g > 0:
                        tn.wait_ge(v1.h, ksev[g - 1])
                    for kc in range(8):
                        for hh in range(2):
                            tn.matmul(K_ps[:, hh * 512:(hh + 1) * 512],
                                      xn_f[:, par, kc, t * 128:(t + 1) * 128],
                                      wqkv_sb[:, kc, D + hh * 512:D + (hh + 1) * 512],
                                      start=(kc == 0), stop=False)
                    for hh in range(2):
                        m = tn.matmul(K_ps[:, hh * 512:(hh + 1) * 512],
                                      ones96[64 * (1 - hh):64 * (1 - hh) + 1, 0:128],
                                      (bw_a[64:65, 0:512] if hh == 0 else bw_b[0:1, 0:512]),
                                      start=False, stop=True)
                    kmm[g] = t1.inc(m)
                    for kc in range(8):
                        for hh in range(2):
                            tn.matmul(V_ps[:, hh * 512:(hh + 1) * 512],
                                      xn_f[:, par, kc, t * 128:(t + 1) * 128],
                                      wqkv_sb[:, kc, 2 * D + hh * 512:2 * D + (hh + 1) * 512],
                                      start=(kc == 0), stop=False)
                    for hh in range(2):
                        m = tn.matmul(V_ps[:, hh * 512:(hh + 1) * 512],
                                      ones96[32 * (1 + hh):32 * (1 + hh) + 1, 0:128],
                                      bw_b[32 * (1 + hh):32 * (1 + hh) + 1, 0:512],
                                      start=False, stop=True)
                    vmm[g] = t1.inc(m)
                blk.tensor(_t)

                def _sk(s, g=g):
                    s.wait_ge(t1.h, kmm[g])
                    if g > 0:
                        s.wait_ge(v1.h, qfin[g - 1])
                    return s.activation(e1k[:], K_ps[:], AF.Exp)
                ek_mk[g] = sq.add(_sk)

            def emit_v_elukv(c, t):
                g = c * TPC + t

                def _v(v):
                    v.wait_ge(s1.h, ek_mk[g])
                    v1.inc(v.tensor_scalar(out=e1k[:], in0=e1k[:], scalar1=1.0,
                                           scalar2=None, op0=ALU.min))
                    v.wait_ge(v1.h, v1.n)
                    if g > 0:
                        v.wait_ge(t1.h, gmm[g - 1])
                    kfin[g] = v1.inc(v.scalar_tensor_tensor(out=k_sb[:], in0=K_ps[:], scalar=1.0,
                                                            in1=e1k[:], op0=ALU.add, op1=ALU.max))
                    v.wait_ge(t1.h, vmm[g])
                    vcp[g] = v1.inc(v.tensor_copy(out=v_sb[:], in_=V_ps[:]))
                blk.vector(_v)

            def emit_v_ksev(c, t):
                g = c * TPC + t

                def _v2(v):
                    v.wait_ge(t1.h, gsum[g])
                    ksev[g] = v1.inc(v.tensor_add(out=ksum_sb[:], in0=ksum_sb[:],
                                                  in1=K_ps[:, 0:NPAIR]))
                blk.vector(_v2)

            def emit_t_gram(c, t):
                g = c * TPC + t

                def _t2(tn):
                    tn.wait_ge(v1.h, vcp[g])
                    tn.wait_ge(v1.h, kfin[g])
                    for p in range(NPAIR):
                        m = tn.matmul(K_ps[:, p:p + 1],
                                      k_sb[:, p * 128:(p + 1) * 128], onescol[:],
                                      start=(p == 0), stop=(p == NPAIR - 1))
                    gsum[g] = t1.inc(m)
                    for p in range(NPAIR):
                        m = tn.matmul(KV_ps[:, p * 128:(p + 1) * 128],
                                      k_sb[:, p * 128:(p + 1) * 128],
                                      v_sb[:, p * 128:(p + 1) * 128],
                                      start=(g == 0 and p % 4 == 0), stop=(g == NT - 1 and p % 4 == 3))
                    gmm[g] = t1.inc(m)
                blk.tensor(_t2)

            def emit_t_q(c, t, sq):
                g = c * TPC + t
                par = c % 2

                def _t3(tn):
                    if g > 0:
                        tn.wait_ge(v1.h, qfin[g - 1])
                    for kc in range(8):
                        for hh in range(2):
                            tn.matmul(Q_ps[:, hh * 512:(hh + 1) * 512],
                                      xn_f[:, par, kc, t * 128:(t + 1) * 128],
                                      wqkv_sb[:, kc, hh * 512:(hh + 1) * 512],
                                      start=(kc == 0), stop=False)
                    for hh in range(2):
                        m = tn.matmul(Q_ps[:, hh * 512:(hh + 1) * 512],
                                      ones96[32 * hh:32 * hh + 1, 0:128],
                                      bw_a[32 * hh:32 * hh + 1, 0:512],
                                      start=False, stop=True)
                    qmm[g] = t1.inc(m)
                blk.tensor(_t3)

                def _sq_(s, g=g):
                    s.wait_ge(t1.h, qmm[g])
                    s.wait_ge(v1.h, kfin[g])
                    return s.activation(e1k[:], Q_ps[:], AF.Exp)
                eq_mk[g] = sq.add(_sq_)

            def emit_v_eluq(c, t):
                g = c * TPC + t

                def _v(v):
                    v.wait_ge(s1.h, eq_mk[g])
                    v1.inc(v.tensor_scalar(out=e1k[:], in0=e1k[:], scalar1=1.0,
                                           scalar2=None, op0=ALU.min))
                    v.wait_ge(v1.h, v1.n)
                    if c >= 2:
                        v.wait_ge(qspl[c % 2].h, qspl_mk[c - 2])
                    qfin[g] = v1.inc(v.scalar_tensor_tensor(out=q_sb[:, c % 2, t, :], in0=Q_ps[:],
                                                            scalar=1.0, in1=e1k[:],
                                                            op0=ALU.add, op1=ALU.max))
                blk.vector(_v)

            def emit_lnA(ca, sq):
                parA = ca % 2

                def _vbn(v, ca=ca, parA=parA):
                    v.wait_ge(xld[parA].h, xld_mk[ca])
                    if ca >= 2:
                        v.wait_ge(s1.h, rsx_mark[ca - 2])
                    for t in range(TPC):
                        v1.inc(v.bn_stats(out=bnx[:, parA, t, 0:6],
                                          in_=x_sb[:, parA, t, 0:512]))
                        v1.inc(v.bn_stats(out=bnx[:, parA, t, 6:12],
                                          in_=x_sb[:, parA, t, 512:1024]))
                    v.wait_ge(v1.h, v1.n)
                    for t in range(TPC):
                        v1.inc(v.bn_aggr(out=mvx[:, parA, t, :], in_=bnx[:, parA, t, :]))
                    aggx_mk[ca] = v1.n
                blk.vector(_vbn)

                def _srs(s, ca=ca, parA=parA):
                    s.wait_ge(v1.h, aggx_mk[ca])
                    for t in range(TPC):
                        m = s.activation(mvx[:, parA, t, 1:2], mvx[:, parA, t, 1:2],
                                         AF.Ln, bias=epsc[:, 0:1])
                    return m
                lnx_mk = sq.add(_srs, inc=4)

                def _sex(s, ca=ca, parA=parA, _lm=lnx_mk):
                    s.wait_ge(s1.h, _lm)
                    for t in range(TPC):
                        g = ca * TPC + t
                        m = s.activation(rs1[:, g:g + 1], mvx[:, parA, t, 1:2],
                                         AF.Exp, scale=-0.5)
                    return m
                rsx_mark[ca] = sq.add(_sex, inc=4)

                def _vnm(v, ca=ca, parA=parA):
                    v.wait_ge(s1.h, rsx_mark[ca])
                    for t in range(TPC):
                        g = ca * TPC + t
                        v1.inc(v.tensor_scalar(out=nmurs1[:, g:g + 1], in0=mvx[:, parA, t, 0:1],
                                               scalar1=rs1[:, g:g + 1], scalar2=-1.0,
                                               op0=ALU.mult, op1=ALU.mult))
                    nmx_mark[ca] = v1.n
                blk.vector(_vnm)

            for it in range(NCH + 1):
                cc, c = it, it - 1
                par = cc % 2
                B0 = 16 * par
                sq = SQueue(s1)
                if it == 0:
                    emit_lnA(0, sq)

                if cc < NCH:
                    t1mk = {}
                    hmk = {}
                    hmkA[cc] = hmk
                    for t in range(TPC):
                        def _t1(s, cc=cc, t=t):
                            s.wait_ge(v1.h, nmx_mark[cc])
                            if cc >= 1:
                                s.wait_ge(v1.h, hmkA[cc - 1][t])
                            g = cc * TPC + t
                            return s.activation(t1b[:, t, :], x_sb[:, cc % 2, t, :], AF.Identity,
                                                bias=nmurs1[:, g:g + 1], scale=rs1[:, g:g + 1])
                        t1mk[t] = sq.add(_t1)
                    t1last[cc] = t1mk[TPC - 1]

                    def _vh01(v, cc=cc, t1mk=t1mk, hmk=hmk):
                        for t in range(2):
                            v.wait_ge(s1.h, t1mk[t])
                            if cc >= 2 and t == 0:
                                v.wait_ge(s1.h, t2last[cc - 2])
                            v1.inc(v.tensor_mul(out=h_t[:, t, :], in0=t1b[:, t, :],
                                                in1=ca1_b[:]))
                            v.wait_ge(v1.h, v1.n)
                            hmk[t] = v1.inc(v.tensor_add(out=h_t[:, t, :], in0=h_t[:, t, :],
                                                         in1=sa_b[:]))
                            v.wait_ge(v1.h, v1.n)
                            v1.inc(v.bn_stats(out=bnh[:, t, 0:6], in_=h_t[:, t, 0:512]))
                            v1.inc(v.bn_stats(out=bnh[:, t, 6:12], in_=h_t[:, t, 512:1024]))
                    blk.vector(_vh01)

                if c >= 0:
                    emit_mm_kv(c, 0, sq)
                    emit_v_elukv(c, 0)
                    emit_t_gram(c, 0)
                    emit_v_ksev(c, 0)
                    emit_t_q(c, 0, sq)
                    emit_v_eluq(c, 0)

                if cc < NCH:
                    def _vh23(v, cc=cc, t1mk=t1mk, hmk=hmk):
                        for t in range(2, TPC):
                            v.wait_ge(s1.h, t1mk[t])
                            v1.inc(v.tensor_mul(out=h_t[:, t, :], in0=t1b[:, t, :],
                                                in1=ca1_b[:]))
                            v.wait_ge(v1.h, v1.n)
                            hmk[t] = v1.inc(v.tensor_add(out=h_t[:, t, :], in0=h_t[:, t, :],
                                                         in1=sa_b[:]))
                            v.wait_ge(v1.h, v1.n)
                            v1.inc(v.bn_stats(out=bnh[:, t, 0:6], in_=h_t[:, t, 0:512]))
                            v1.inc(v.bn_stats(out=bnh[:, t, 6:12], in_=h_t[:, t, 512:1024]))
                        v.wait_ge(v1.h, v1.n)
                        if cc >= 1:
                            v.wait_ge(s1.h, rsh_mark[cc - 1])
                        for t in range(TPC):
                            v1.inc(v.bn_aggr(out=mvh[:, t, :], in_=bnh[:, t, :]))
                        aggh_mk[cc] = v1.n
                    blk.vector(_vh23)

                if cc < NCH:
                    def _srs2(s, cc=cc):
                        s.wait_ge(v1.h, aggh_mk[cc])
                        for t in range(TPC):
                            m = s.activation(mvh[:, t, 1:2], mvh[:, t, 1:2],
                                             AF.Ln, bias=epsc[:, 0:1])
                        return m
                    lnh_mk = sq.add(_srs2, inc=4)

                    def _seh(s, cc=cc, _lm=lnh_mk):
                        s.wait_ge(s1.h, _lm)
                        if cc >= 1:
                            s.wait_ge(s1.h, t2last[cc - 1])
                        for t in range(TPC):
                            m = s.activation(rsh[:, t:t + 1], mvh[:, t, 1:2],
                                             AF.Exp, scale=-0.5)
                        return m
                    rsh_mark[cc] = sq.add(_seh, inc=4)

                    def _vnmh(v, cc=cc):
                        v.wait_ge(s1.h, rsh_mark[cc])
                        for t in range(TPC):
                            v1.inc(v.tensor_scalar(out=nmh[:, t:t + 1], in0=mvh[:, t, 0:1],
                                                   scalar1=rsh[:, t:t + 1], scalar2=-1.0,
                                                   op0=ALU.mult, op1=ALU.mult))
                        nmh_mark[cc] = v1.n
                    blk.vector(_vnmh)

                if c >= 0:
                    emit_mm_kv(c, 1, sq)
                    emit_v_elukv(c, 1)
                    emit_t_gram(c, 1)
                    emit_v_ksev(c, 1)
                    emit_t_q(c, 1, sq)
                    emit_v_eluq(c, 1)

                if cc < NCH:
                    t2mks = {}
                    for t in range(TPC):
                        def _t2(s, cc=cc, t=t, par=par):
                            s.wait_ge(v1.h, nmh_mark[cc])
                            if t == 0 and cc >= 2:
                                s.wait_ge(xnst[par].h, xnst_mk[cc - 2])
                            return s.activation(xn_t[:, par, t, :], h_t[:, t, :], AF.Identity,
                                                bias=nmh[:, t:t + 1],
                                                scale=rsh[:, t:t + 1])
                        t2mks[t] = sq.add(_t2)
                    t2last[cc] = t2mks[TPC - 1]
                    t2mks_h[cc] = t2mks[TPC - 1]

                    def _sspq(s, cc=cc, par=par):
                        s.wait_ge(s1.h, t2mks_h[cc])
                        s.wait_ge(xnst[par].h, xnst[par].n)
                        xnst_mk[cc] = xnst[par].inc(
                            s.dma_start(out=r3(xn_scr[tok(cc)]), in_=xn_t[:, par]), 16)
                    sq.add(_sspq, inc=0)

                if c >= 0:
                    emit_mm_kv(c, 2, sq)
                    emit_v_elukv(c, 2)
                    emit_t_gram(c, 2)
                    emit_v_ksev(c, 2)
                    emit_t_q(c, 2, sq)
                    emit_v_eluq(c, 2)

                if cc + 1 < NCH:
                    emit_lnA(cc + 1, sq)

                if c >= 0:
                    emit_mm_kv(c, 3, sq)
                    emit_v_elukv(c, 3)
                    emit_t_gram(c, 3)
                    emit_v_ksev(c, 3)
                    emit_t_q(c, 3, sq)
                    emit_v_eluq(c, 3)

                sq.flush(blk)

                if cc < NCH:
                    def _syt(sy, cc=cc, par=par):
                        sy.wait_ge(xnst[par].h, xnst_mk[cc])
                        if cc >= 2:
                            sy.wait_ge(t1.h, qmm[(cc - 2) * TPC + TPC - 1])
                        sy.wait_ge(xnfl[par].h, xnfl[par].n)
                        for kc in range(8):
                            m = sy.dma_start_transpose(out=xn_f[:, par, kc, :],
                                                       in_=xn_scr[tok(cc), kc * 128:(kc + 1) * 128])
                            xnfl[par].inc(m, 16)
                        xnf_ok[cc] = xnfl[par].n
                    blk.sync(_syt)

                    def _gxl(gp, cc=cc, par=par):
                        if cc + 2 < NCH:
                            gp.wait_ge(s1.h, t1last[cc])
                            gp.wait_ge(xld[par].h, xld[par].n)
                            xld_mk[cc + 2] = xld[par].inc(
                                gp.dma_start(out=x_sb[:, par], in_=r3(x_in[tok(cc + 2)])), 16)
                        for _ in range(3):
                            if castq:
                                o, i = castq.pop(0)
                                wcast.inc(gp.dma_start(out=o, in_=i), 16)
                    blk.gpsimd(_gxl)

                if c >= 0:
                    def _sq2(sc, c=c):
                        sc.wait_ge(v1.h, qfin[c * TPC + TPC - 1])
                        sc.wait_ge(qspl[c % 2].h, qspl[c % 2].n)
                        qspl_mk[c] = qspl[c % 2].inc(
                            sc.dma_start(out=r3(q_scr[tok(c)]), in_=q_sb[:, c % 2]), 16)
                    blk.scalar(_sq2)

            if castq:
                def _gcrest(gp):
                    for o, i in castq:
                        wcast.inc(gp.dma_start(out=o, in_=i), 16)
                    castq.clear()
                blk.gpsimd(_gcrest)

            def _vprep(v):
                v.wait_ge(t1.h, gmm[NT - 1])
                v.wait_ge(v1.h, ksev[NT - 1])
                v1.inc(v.memset(kv_sb[:], 0.0))
                v1.inc(v.memset(ksbd[:], 0.0))
                v.wait_ge(v1.h, v1.n)
                for p in range(NPAIR):
                    v1.inc(v.tensor_copy(out=kv_sb[0:64, p * 128:p * 128 + 64],
                                         in_=KV_ps[0:64, p * 128:p * 128 + 64]))
                    v1.inc(v.tensor_copy(out=kv_sb[64:128, p * 128 + 64:p * 128 + 128],
                                         in_=KV_ps[64:128, p * 128 + 64:p * 128 + 128]))
                    v1.inc(v.tensor_copy(out=ksbd[0:64, p * NH + 2 * p:p * NH + 2 * p + 1],
                                         in_=ksum_sb[0:64, p:p + 1]))
                    v1.inc(v.tensor_copy(out=ksbd[64:128, p * NH + 2 * p + 1:p * NH + 2 * p + 2],
                                         in_=ksum_sb[64:128, p:p + 1]))
            blk.vector(_vprep)

    es_wq.close()

    with ExitStack() as es2:
        sb2 = lambda n, s, dt: es2.enter_context(nc.sbuf_tensor(n, s, dt))
        ps2 = lambda n, s: es2.enter_context(nc.psum_tensor(n, s, F32))
        x_sb = sb2("x_sb2", [128, 2, TPC, D], F32)
        qT = sb2("qT", [128, 2, 8, CT], BF16)
        attn_f = sb2("attn_f", [128, NPAIR, CT], BF16)
        rb_sb = sb2("rb_sb", [128, 2, CT], F32)
        de_sb = sb2("de_sb", [NH, CT], F32)
        rcp_bf = sb2("rcp_bf", [NH, CT], BF16)
        t1b2 = sb2("t1b2", [128, 2, D], BF16)
        t3b2 = sb2("t3b2", [128, 2, D], BF16)
        u_b = sb2("u_b", [128, D], BF16)
        u2_b = sb2("u2_b", [128, 2, D], BF16)
        r_b = sb2("r_b", [128, D], F32)
        vv_b = sb2("vv_b", [128, D], F32)
        x2_sb = sb2("x2_sb", [128, TPC, D], F32)
        h2A = sb2("h2A", [128, D], BF16)
        h2_t = sb2("h2_t", [128, TPC, D], BF16)
        st2 = sb2("st2", [128, 16], F32)
        bn3 = sb2("bn3", [128, TPC, 12], F32)
        mv3 = sb2("mv3", [128, TPC, 2], F32)
        DEN_ps = ps2("DEN_ps", [NH, CT])
        RB_ps = ps2("RB_ps", [128, 2 * CT])
        QO_ps = ps2("QO_ps", [128, 2 * CT])
        WO_ps = ps2("WO_ps", [128, D])

        xld2 = [S("xld2a"), S("xld2b")]
        qTl = [S("qTla"), S("qTlb")]
        xld2_mk = [0] * (NCH + 2)
        qTl_mk = [0] * (NCH + 2)
        v2 = S("v2")
        s2 = S("s2")
        g2 = S("g2")
        t2 = S("t2")
        x2st = S("x2st")
        h2st = S("h2st")
        den_free = [0] * NCH
        qT_free = [0] * NCH
        x_free_v = [0] * NCH
        x_free_s = [0] * NCH
        x_free_g = [0] * NCH
        gusA = {}
        vh2mA = {}
        wo_free = [0] * NT
        r_done = [0] * NT
        h2a_done = [0] * NT
        wo_last_ldw = [0] * NCH
        agg3_mk = [0] * NCH
        rs3q_mark = [0] * NCH
        nm3_mark = [0] * NCH

        with nc.Block("p2a") as blk:
            def _sy0(sy):
                qspl[0].wait(sy)
                qspl[1].wait(sy)
                xld2_mk[0] = xld2[0].inc(sy.dma_start(out=x_sb[:, 0], in_=r3(x_in[tok(0)])), 16)
                for kc in range(8):
                    qTl[0].inc(sy.dma_start_transpose(out=qT[:, 0, kc, :],
                                                      in_=q_scr[tok(0), kc * 128:(kc + 1) * 128]), 16)
                qTl_mk[0] = qTl[0].n
            blk.sync(_sy0)

            for c in range(NCH):
                par = c % 2

                def _syc(sy, c=c):
                    if c + 1 < NCH:
                        if c >= 1:
                            sy.wait_ge(v2.h, x_free_v[c - 1])
                            sy.wait_ge(s2.h, x_free_s[c - 1])
                            sy.wait_ge(t2.h, qT_free[c - 1])
                        sy.wait_ge(xld2[(c + 1) % 2].h, xld2[(c + 1) % 2].n)
                        sy.wait_ge(qTl[(c + 1) % 2].h, qTl[(c + 1) % 2].n)
                        xld2_mk[c + 1] = xld2[(c + 1) % 2].inc(
                            sy.dma_start(out=x_sb[:, (c + 1) % 2], in_=r3(x_in[tok(c + 1)])), 16)
                        for kc in range(8):
                            qTl[(c + 1) % 2].inc(
                                sy.dma_start_transpose(out=qT[:, (c + 1) % 2, kc, :],
                                                       in_=q_scr[tok(c + 1), kc * 128:(kc + 1) * 128]), 16)
                        qTl_mk[c + 1] = qTl[(c + 1) % 2].n
                blk.sync(_syc)

                def _td(tn, c=c):
                    tn.wait_ge(qTl[c % 2].h, qTl_mk[c])
                    if c >= 1:
                        tn.wait_ge(v2.h, den_free[c - 1])
                    for p in range(NPAIR):
                        m = tn.matmul(DEN_ps[:], ksbd[:, p * NH:(p + 1) * NH],
                                      qT[:, par, p, :], start=(p == 0), stop=(p == NPAIR - 1))
                    t2.inc(m)
                blk.tensor(_td)
                tden = t2.n

                def _vd(v, c=c, tden=tden):
                    v.wait_ge(t2.h, tden)
                    den_free[c] = v2.inc(v.tensor_scalar(out=de_sb[:], in0=DEN_ps[:],
                                                         scalar1=DEN_EPS, scalar2=None, op0=ALU.add))
                    v.wait_ge(v2.h, v2.n)
                    v2.inc(v.reciprocal(de_sb[:], de_sb[:]))
                    v.wait_ge(v2.h, v2.n)
                    v2.inc(v.tensor_copy(out=rcp_bf[:], in_=de_sb[:]))
                blk.vector(_vd)
                vrcp = v2.n

                sq2 = SQueue(s2)
                rbev = [0] * NPAIR
                aev = [0] * NPAIR
                for p in range(NPAIR):
                    def _tp(tn, p=p, c=c):
                        if p == 0:
                            tn.wait_ge(v2.h, vrcp)
                        if p >= 2:
                            tn.wait_ge(v2.h, rbev[p - 2])
                        m = tn.matmul(RB_ps[:, (p % 2) * CT:(p % 2 + 1) * CT],
                                      eexp[:, p * 128:(p + 1) * 128], rcp_bf[:],
                                      start=True, stop=True)
                        t2.inc(m)
                        if p >= 2:
                            tn.wait_ge(v2.h, aev[p - 2])
                        if c >= 1 and p == 0:
                            tn.wait_ge(v2.h, wo_free[c * TPC - 1])
                        m = tn.matmul(QO_ps[:, (p % 2) * CT:(p % 2 + 1) * CT],
                                      kv_sb[:, p * 128:(p + 1) * 128],
                                      qT[:, par, p, :], start=True, stop=True)
                        t2.inc(m)
                    blk.tensor(_tp)
                    tpp = t2.n

                    def _vp(v, p=p, tpp=tpp):
                        v.wait_ge(t2.h, tpp - 1)
                        rbev[p] = v2.inc(v.tensor_copy(out=rb_sb[:, p % 2, :],
                                                       in_=RB_ps[:, (p % 2) * CT:(p % 2 + 1) * CT]))
                        v.wait_ge(t2.h, tpp)
                        v.wait_ge(v2.h, rbev[p])
                        aev[p] = v2.inc(v.tensor_mul(out=attn_f[:, p, :],
                                                     in0=QO_ps[:, (p % 2) * CT:(p % 2 + 1) * CT],
                                                     in1=rb_sb[:, p % 2, :]))
                    blk.vector(_vp)
                qT_free[c] = t2.n

                twms = {}
                vx2 = {}
                gus = {}
                t3mks = {}
                vh2m = {}
                for t in range(TPC):
                    gt = c * TPC + t

                    def _tw(tn, t=t, gt=gt):
                        if t == 0:
                            tn.wait_ge(v2.h, aev[NPAIR - 1])
                        if gt > 0:
                            tn.wait_ge(v2.h, wo_free[gt - 1])
                        for kc in range(8):
                            for hh in range(2):
                                m = tn.matmul(WO_ps[:, hh * 512:(hh + 1) * 512],
                                              attn_f[:, kc, t * 128:(t + 1) * 128],
                                              wout_sb[:, kc, hh * 512:(hh + 1) * 512],
                                              start=(kc == 0), stop=(kc == 7))
                        t2.inc(m)
                    blk.tensor(_tw)
                    twms[t] = t2.n

                    def _sep(s, c=c, t=t, gt=gt):
                        s.wait_ge(xld2[c % 2].h, xld2_mk[c])
                        if gt > 1:
                            s.wait_ge(g2.h, gusA[gt - 2])
                        return s.activation(t1b2[:, gt % 2, :], x_sb[:, c % 2, t, :], AF.Identity,
                                            bias=nmurs1[:, gt:gt + 1], scale=rs1[:, gt:gt + 1])
                    st1 = sq2.add(_sep)

                    def _gep(gp, gt=gt, st1=st1, t=t):
                        gp.wait_ge(s2.h, st1)
                        if gt > 1:
                            gp.wait_ge(v2.h, r_done[gt - 2])
                        g2.inc(gp.tensor_mul(out=u_b[:], in0=t1b2[:, gt % 2, :], in1=A_b[:]))
                        gp.wait_ge(g2.h, g2.n)
                        gus[t] = gusA[gt] = g2.inc(gp.tensor_add(out=u2_b[:, gt % 2, :],
                                                                 in0=u_b[:], in1=B_b[:]))
                    blk.gpsimd(_gep)

                    def _vep(v, c=c, t=t, gt=gt):
                        v.wait_ge(g2.h, gus[t])
                        r_done[gt] = v2.inc(v.tensor_add(out=r_b[:], in0=u2_b[:, gt % 2, :],
                                                         in1=x_sb[:, c % 2, t, :]))
                        v.wait_ge(t2.h, twms[t])
                        v.wait_ge(v2.h, v2.n)
                        if c >= 1 and t == 0:
                            v.wait_ge(x2st.h, 16 * c)
                        wo_free[gt] = v2.inc(v.tensor_add(out=x2_sb[:, t, :], in0=r_b[:],
                                                          in1=WO_ps[:]))
                        v.wait_ge(v2.h, v2.n)
                        v2.inc(v.bn_stats(out=bn3[:, t, 0:6], in_=x2_sb[:, t, 0:512]))
                        vx2[t] = v2.inc(v.bn_stats(out=bn3[:, t, 6:12], in_=x2_sb[:, t, 512:1024]))
                    blk.vector(_vep)

                def _vagg3(v, c=c):
                    v.wait_ge(v2.h, vx2[TPC - 1])
                    if c >= 1:
                        v.wait_ge(s2.h, rs3q_mark[c - 1])
                    for t in range(TPC):
                        v2.inc(v.bn_aggr(out=mv3[:, t, :], in_=bn3[:, t, :]))
                    agg3_mk[c] = v2.n
                blk.vector(_vagg3)

                def _srs3(s, c=c):
                    s.wait_ge(v2.h, agg3_mk[c])
                    for t in range(TPC):
                        m = s.activation(mv3[:, t, 1:2], mv3[:, t, 1:2],
                                         AF.Ln, bias=epsc[:, 0:1])
                    return m
                ln3q_mk = sq2.add(_srs3, inc=4)

                def _se3(s, c=c, _lm=ln3q_mk):
                    s.wait_ge(s2.h, _lm)
                    for t in range(TPC):
                        gt = c * TPC + t
                        m = s.activation(rs3[:, gt:gt + 1], mv3[:, t, 1:2],
                                         AF.Exp, scale=-0.5)
                    return m
                rs3q_mark[c] = sq2.add(_se3, inc=4)

                def _vnm3(v, c=c):
                    v.wait_ge(s2.h, rs3q_mark[c])
                    for t in range(TPC):
                        gt = c * TPC + t
                        v2.inc(v.tensor_scalar(out=nmurs3[:, gt:gt + 1], in0=mv3[:, t, 0:1],
                                               scalar1=rs3[:, gt:gt + 1], scalar2=-1.0,
                                               op0=ALU.mult, op1=ALU.mult))
                    nm3_mark[c] = v2.n
                blk.vector(_vnm3)

                for t in range(TPC):
                    gt = c * TPC + t

                    def _sh2(s, c=c, t=t, gt=gt):
                        s.wait_ge(v2.h, nm3_mark[c])
                        if gt > 1:
                            s.wait_ge(v2.h, vh2mA[gt - 2])
                        return s.activation(t3b2[:, gt % 2, :], x2_sb[:, t, :], AF.Identity,
                                            bias=nmurs3[:, gt:gt + 1], scale=rs3[:, gt:gt + 1])
                    t3mks[t] = sq2.add(_sh2)

                    def _vh2(v, c=c, t=t, gt=gt):
                        if t > 0:
                            v.wait_ge(v2.h, v2.n)
                        v.wait_ge(s2.h, t3mks[t])
                        vh2m[t] = vh2mA[gt] = v2.inc(v.tensor_mul(out=h2A[:], in0=t3b2[:, gt % 2, :],
                                                                  in1=cb1_b[:]))
                        v.wait_ge(v2.h, v2.n)
                        if c >= 1 and t == 0:
                            v.wait_ge(h2st.h, 16 * c)
                        v2.inc(v.tensor_add(out=h2_t[:, t, :], in0=h2A[:], in1=sbb_b[:]))
                    blk.vector(_vh2)

                sq2.flush(blk)

                x_free_v[c] = v2.n
                x_free_s[c] = s2.n
                x_free_g[c] = g2.n
                vln = v2.n
                gln = g2.n

                def _gspill(sc, c=c, vln=vln, gln=gln):
                    sc.wait_ge(v2.h, vln)
                    sc.wait_ge(x2st.h, x2st.n)
                    x2st.inc(sc.dma_start(out=r3(x2_scr[tok(c)]), in_=x2_sb[:]), 16)
                    sc.wait_ge(h2st.h, h2st.n)
                    h2st.inc(sc.dma_start(out=r3(h2_scr[tok(c)]), in_=h2_t[:]), 16)
                blk.scalar(_gspill)

            def _gfin(sc):
                x2st.wait(sc)
                h2st.wait(sc)
            blk.scalar(_gfin)

    es_wo.close()

    with ExitStack() as es3:
        sb3 = lambda n, s, dt: es3.enter_context(nc.sbuf_tensor(n, s, dt))
        ps3 = lambda n, s: es3.enter_context(nc.psum_tensor(n, s, F32))
        h2f = sb3("h2f", [128, 2, 8, CT], BF16)
        w13_sb = sb3("w13_sb", [128, 2, 8, D], BF16)
        w2_sb = sb3("w2_sb", [128, 2, 2, 4, D], BF16)
        act_sl = sb3("act_sl", [128, NSLOT, 4, CT], BF16)
        gbuf = sb3("gbuf", [128, 4, CT], BF16)
        cE = sb3("cE", [128, 2, CT], BF16)
        sc_all = sb3("sc_all", [128, TPC * NE], F32)
        bb_all = sb3("bb_all", [128, TPC * NE], F32)
        oh_all = sb3("oh_all", [128, TPC * NE], F32)
        pk_t = sb3("pk_t", [128, TPC * NE], F32)
        sel_pk = sb3("sel_pk", [128, TPC * NE], F32)
        mx_s = sb3("mx_s", [128, TPC], F32)
        ws_s = sb3("ws_s", [128, TPC], F32)
        comb_bf = sb3("comb_bf", [128, TPC * NE], BF16)
        comb_f = sb3("comb_f", [NH, CT], BF16)
        x2_sb3 = sb3("x2_sb3", [128, 2, TPC, D], F32)
        yf_b = sb3("yf_b", [128, TPC, D], F32)
        ob_b = sb3("ob_b", [128, TPC, D], F32)
        I1_ps = ps3("I1_ps", [128, 2 * 512])
        Y_ps = ps3("Y_ps", [128, 2 * D])
        CE_ps = ps3("CE_ps", [128, 2 * 512])

        h2l = [S("h2la"), S("h2lb")]
        w13l = [S("w13l0"), S("w13l1")]
        w2l = {(p, q): S(f"w2l{p}{q}") for p in range(2) for q in range(2)}
        x2l = [S("x2la"), S("x2lb")]
        h2l_mk = [0] * (NCH + 2)
        x2l_mk = [0] * (NCH + 2)
        w13_mk = {}
        w2_mk = {}
        v3 = S("v3")
        s3 = S("s3")
        g3 = S("g3")
        t3 = S("t3")
        outst = S("outst")
        i1done = {}
        tvec = {}
        actv = {}
        ymm = {}
        yev = {}
        ceev = {}
        ce_mm = {}
        gate_done = [0] * NCH
        gate_sig = [0] * NCH
        comb_done = [0] * NCH
        ct_free = [0] * NCH
        h2f_free = [0] * NCH
        x2_free = [0] * NCH
        vzmul = [0] * NCH
        w2slot_user = {}
        w13_user = {}

        eidx = lambda c, e: c * NEX + e

        with nc.Block("p2b") as blk:
            def _sy00(sy):
                wcast.wait(sy)
                h2st.wait(sy)
                for kc in range(8):
                    h2l[0].inc(sy.dma_start_transpose(out=h2f[:, 0, kc, :],
                                                      in_=h2_scr[tok(0), kc * 128:(kc + 1) * 128]), 16)
                h2l_mk[0] = h2l[0].n
            blk.sync(_sy00)

            def _gx0(gp):
                x2st.wait(gp)
                x2l_mk[0] = x2l[0].inc(gp.dma_start(out=x2_sb3[:, 0], in_=r3(x2_scr[tok(0)])), 16)
            blk.gpsimd(_gx0)

            tct_mm = [0] * NCH
            vct_mark = [0] * NCH

            def em_gate(c):
                def _tg(tn, c=c):
                    tn.wait_ge(h2l[c % 2].h, h2l_mk[c])
                    if c >= 1:
                        tn.wait_ge(v3.h, vct_mark[c - 1])
                        tn.wait_ge(s3.h, gate_sig[c - 1])
                    for t in range(TPC):
                        for kc in range(8):
                            m = tn.matmul(CE_ps[:, 512 + t * NE:512 + (t + 1) * NE],
                                          h2f[:, c % 2, kc, t * 128:(t + 1) * 128],
                                          gatew_sb[:, kc, :],
                                          start=(kc == 0), stop=(kc == 7))
                    gate_done[c] = t3.inc(m)
                blk.tensor(_tg)

            def em_sig(c, sq):
                def _sg(s, c=c):
                    s.wait_ge(t3.h, gate_done[c])
                    if c >= 1:
                        s.wait_ge(v3.h, comb_done[c - 1])
                    return s.activation(sc_all[:], CE_ps[:, 512:512 + TPC * NE], AF.Sigmoid)
                gate_sig[c] = sq.add(_sg)

            def em_topk(c, phase):
                def _v(v, c=c, phase=phase):
                    if phase == 0:
                        v.wait_ge(s3.h, gate_sig[c])
                        for t in range(TPC):
                            v3.inc(v.tensor_add(out=bb_all[:, t * NE:(t + 1) * NE],
                                                in0=sc_all[:, t * NE:(t + 1) * NE], in1=gateb_b[:]))
                        v.wait_ge(v3.h, v3.n)
                        its = [0]
                    elif phase == 1:
                        its = [1, 2]
                    else:
                        its = [3]
                    for it4 in its:
                        for t in range(TPC):
                            v3.inc(v.reduce_max(mx_s[:, t:t + 1], bb_all[:, t * NE:(t + 1) * NE],
                                                axis=AX.X))
                        v.wait_ge(v3.h, v3.n)
                        for t in range(TPC):
                            v3.inc(v.tensor_scalar(out=oh_all[:, t * NE:(t + 1) * NE],
                                                   in0=bb_all[:, t * NE:(t + 1) * NE],
                                                   scalar1=mx_s[:, t:t + 1], scalar2=0.0,
                                                   op0=ALU.subtract, op1=ALU.is_ge))
                        v.wait_ge(v3.h, v3.n)
                        for t in range(TPC):
                            sl = slice(t * NE, (t + 1) * NE)
                            if it4 == 0:
                                v3.inc(v.tensor_mul(out=sel_pk[:, sl], in0=oh_all[:, sl],
                                                    in1=sc_all[:, sl]))
                            else:
                                v3.inc(v.tensor_mul(out=pk_t[:, sl], in0=oh_all[:, sl],
                                                    in1=sc_all[:, sl]))
                        if it4 < 3:
                            for t in range(TPC):
                                sl = slice(t * NE, (t + 1) * NE)
                                v3.inc(v.scalar_tensor_tensor(out=bb_all[:, sl], in0=oh_all[:, sl],
                                                              scalar=-1e30, in1=bb_all[:, sl],
                                                              op0=ALU.mult, op1=ALU.add))
                        v.wait_ge(v3.h, v3.n)
                        if it4 > 0:
                            for t in range(TPC):
                                sl = slice(t * NE, (t + 1) * NE)
                                v3.inc(v.tensor_add(out=sel_pk[:, sl], in0=sel_pk[:, sl],
                                                    in1=pk_t[:, sl]))
                            v.wait_ge(v3.h, v3.n)
                    if phase == 2:
                        for t in range(TPC):
                            v3.inc(v.reduce_sum(ws_s[:, t:t + 1], sel_pk[:, t * NE:(t + 1) * NE],
                                                axis=AX.X))
                        v.wait_ge(v3.h, v3.n)
                        v3.inc(v.reciprocal(ws_s[:], ws_s[:]))
                        v.wait_ge(v3.h, v3.n)
                        if c >= 1:
                            v.wait_ge(t3.h, tct_mm[c - 1])
                        for t in range(TPC):
                            comb_done[c] = v3.inc(v.tensor_scalar(out=comb_bf[:, t * NE:(t + 1) * NE],
                                                                  in0=sel_pk[:, t * NE:(t + 1) * NE],
                                                                  scalar1=ws_s[:, t:t + 1],
                                                                  scalar2=ROUTE_SCALE,
                                                                  op0=ALU.mult, op1=ALU.mult))
                blk.vector(_v)

            def em_tct(c):
                def _t(tn, c=c):
                    tn.wait_ge(v3.h, comb_done[c])
                    for t in range(TPC):
                        m = tn.matmul(CE_ps[0:NH, 512 + t * 128:512 + (t + 1) * 128],
                                      comb_bf[:, t * NE:(t + 1) * NE], ident[:],
                                      start=True, stop=True)
                    tct_mm[c] = t3.inc(m)
                blk.tensor(_t)

            def em_vct(c):
                def _v(v, c=c):
                    v.wait_ge(t3.h, tct_mm[c])
                    if c >= 1:
                        v.wait_ge(t3.h, ce_mm.get(eidx(c - 1, NE - 1), 0))
                    vct_mark[c] = v3.inc(v.tensor_copy(out=comb_f[:], in_=CE_ps[0:NH, 512:1024]))
                blk.vector(_v)

            for c in range(NCH):
                par = c % 2

                def _gxp(gp, c=c):
                    if c + 1 < NCH:
                        if c >= 1:
                            gp.wait_ge(v3.h, x2_free[c - 1])
                        gp.wait_ge(x2l[(c + 1) % 2].h, x2l[(c + 1) % 2].n)
                        x2l_mk[c + 1] = x2l[(c + 1) % 2].inc(
                            gp.dma_start(out=x2_sb3[:, (c + 1) % 2], in_=r3(x2_scr[tok(c + 1)])), 16)
                blk.gpsimd(_gxp)

                sq3 = SQueue(s3)

                if c == 0:
                    em_gate(0)
                    em_sig(0, sq3)
                    em_topk(0, 0)
                    em_topk(0, 1)
                    em_topk(0, 2)
                    em_tct(0)
                    em_vct(0)

                for gi, grp in enumerate(EGROUPS):
                    for e in grp:
                        ei = eidx(c, e)
                        slot = ei % NSLOT
                        pos = grp.index(e)
                        _mc = {}

                        def _syw(sy, c=c, e=e, ei=ei, gi=gi, pos=pos):
                            s13 = ei % 2
                            if s13 in w13_user:
                                sy.wait_ge(t3.h, i1done[w13_user[s13]])
                            w13_user[s13] = ei
                            if c == 0 and e < 2:
                                wcast.wait(sy)
                            sy.wait_ge(w13l[s13].h, w13l[s13].n)
                            w13_mk[ei] = w13l[s13].inc(
                                sy.dma_start(out=w13_sb[:, s13],
                                             in_=w13bf[e].rearrange("(k p) n -> p k n", p=128)), 16)
                            key = (gi % 2, pos)
                            if key in w2slot_user:
                                pc, pgi = w2slot_user[key]
                                sy.wait_ge(t3.h, ymm.get((pc, pgi, TPC - 1), 0))
                            w2slot_user[key] = (c, gi)
                            sy.wait_ge(w2l[key].h, w2l[key].n)
                            w2_mk[ei] = w2l[key].inc(
                                sy.dma_start(out=w2_sb[:, gi % 2, pos],
                                             in_=w2bf[e].rearrange("(k p) n -> p k n", p=128)), 16)
                        blk.sync(_syw)

                        if e < NE:
                            def _tce(tn, c=c, e=e, ei=ei):
                                tn.wait_ge(v3.h, vct_mark[c])
                                if e >= 1:
                                    tn.wait_ge(v3.h, ceev.get(eidx(c, e - 1), 0))
                                elif c >= 1:
                                    tn.wait_ge(v3.h, ceev.get(eidx(c - 1, NE - 1), 0))
                                m = tn.matmul(CE_ps[:, 0:512],
                                              erow[:, e * 128:(e + 1) * 128], comb_f[:],
                                              start=True, stop=True)
                                ce_mm[ei] = t3.inc(m)
                                _mc["ce"] = ce_mm[ei]
                            blk.tensor(_tce)

                            def _vce(v, e=e, ei=ei):
                                v.wait_ge(t3.h, _mc["ce"])
                                ceev[ei] = v3.inc(v.tensor_copy(out=cE[:, e % 2, :],
                                                                in_=CE_ps[:, 0:512]))
                            blk.vector(_vce)

                        s3_base = s3.n
                        v3_base = v3.n

                        def _tm1(tn, c=c, e=e, ei=ei, s3_base=s3_base, v3_base=v3_base):
                            tn.wait_ge(w13l[ei % 2].h, w13_mk[ei])
                            tn.wait_ge(h2l[c % 2].h, h2l_mk[c])
                            if ei >= 1:
                                tn.wait_ge(s3.h, s3_base)
                                tn.wait_ge(v3.h, tvec.get(ei - 1, 0))
                            for oc in range(8):
                                if oc in (2, 3, 4, 5):
                                    tn.wait_ge(s3.h, s3_base + (oc - 1))
                                elif oc in (6, 7):
                                    tn.wait_ge(v3.h, v3_base + (oc - 5))
                                for kc in range(8):
                                    m = tn.matmul(I1_ps[:, (oc % 2) * 512:(oc % 2 + 1) * 512],
                                                  w13_sb[:, ei % 2, kc, oc * 128:(oc + 1) * 128],
                                                  h2f[:, par, kc, :],
                                                  start=(kc == 0), stop=(kc == 7))
                                _mc[("mm", oc)] = t3.inc(m)
                        blk.tensor(_tm1)
                        i1done[ei] = t3.n
                        h2f_free[c] = t3.n

                        for j in range(4):
                            def _sm1(s, ei=ei, j=j, _mc=_mc):
                                s.wait_ge(t3.h, _mc[("mm", j)])
                                if ei >= 1:
                                    s.wait_ge(v3.h, tvec.get(ei - 1, 0))
                                return s.activation(gbuf[:, j, :],
                                                    I1_ps[:, (j % 2) * 512:(j % 2 + 1) * 512],
                                                    AF.Silu)
                            sq3.add(_sm1)
                        assert s3.n == s3_base + 4

                        def _vm1(v, c=c, e=e, ei=ei, slot=slot, s3_base=s3_base):
                            if ei >= 1:
                                v.wait_ge(v3.h, actv[ei - 1])
                            if ei >= NSLOT:
                                pe = ei - NSLOT
                                pc, pee = divmod(pe, NEX)
                                v.wait_ge(t3.h, ymm.get((pc, EGROUPS_OF[pee], TPC - 1), 0))
                            for j in range(4):
                                v.wait_ge(t3.h, _mc[("mm", 4 + j)])
                                v.wait_ge(s3.h, s3_base + j + 1)
                                tvec[ei] = v3.inc(v.tensor_mul(out=act_sl[:, slot, j, :],
                                                               in0=gbuf[:, j, :],
                                                               in1=I1_ps[:, (j % 2) * 512:(j % 2 + 1) * 512]))
                            v.wait_ge(v3.h, v3.n)
                            if e < NE:
                                v.wait_ge(v3.h, ceev[ei])
                                for j in range(4):
                                    actv[ei] = v3.inc(v.tensor_mul(out=act_sl[:, slot, j, :],
                                                                   in0=act_sl[:, slot, j, :],
                                                                   in1=cE[:, e % 2, :]))
                            else:
                                actv[ei] = tvec[ei]
                        blk.vector(_vm1)
                        assert v3.n == v3_base + (1 if e < NE else 0) * 0 + 8 or True

                    if gi == 0:
                        def _syp(sy, c=c):
                            if c + 1 < NCH:
                                if c >= 1:
                                    sy.wait_ge(t3.h, h2f_free[c - 1])
                                sy.wait_ge(h2l[(c + 1) % 2].h, h2l[(c + 1) % 2].n)
                                for kc in range(8):
                                    h2l[(c + 1) % 2].inc(
                                        sy.dma_start_transpose(out=h2f[:, (c + 1) % 2, kc, :],
                                                               in_=h2_scr[tok(c + 1), kc * 128:(kc + 1) * 128]), 16)
                                h2l_mk[c + 1] = h2l[(c + 1) % 2].n
                        blk.sync(_syp)

                    if gi == len(EGROUPS) - 1 and c + 1 < NCH:
                        em_tct(c + 1)
                        em_vct(c + 1)

                    for t in range(TPC):
                        def _tm2(tn, c=c, gi=gi, grp=grp, t=t):
                            for e2 in grp:
                                tn.wait_ge(v3.h, actv[eidx(c, e2)])
                                tn.wait_ge(w2l[(gi % 2, grp.index(e2))].h, w2_mk[eidx(c, e2)])
                            prev = None
                            if t >= 2:
                                prev = (c, gi, t - 2)
                            elif gi >= 1:
                                prev = (c, gi - 1, t)
                            elif c >= 1:
                                prev = (c - 1, len(EGROUPS) - 1, t)
                            if prev is not None and prev in yev:
                                tn.wait_ge(v3.h, yev[prev])
                            for e2 in grp:
                                slot2 = eidx(c, e2) % NSLOT
                                for kc in range(4):
                                    for hh in range(2):
                                        m = tn.matmul(
                                            Y_ps[:, (t % 2) * D + hh * 512:(t % 2) * D + (hh + 1) * 512],
                                            act_sl[:, slot2, kc, t * 128:(t + 1) * 128],
                                            w2_sb[:, gi % 2, grp.index(e2), kc, hh * 512:(hh + 1) * 512],
                                            start=(e2 == grp[0] and kc == 0),
                                            stop=(e2 == grp[-1] and kc == 3))
                            ymm[(c, gi, t)] = t3.inc(m)
                        blk.tensor(_tm2)

                        def _vy(v, c=c, gi=gi, t=t):
                            v.wait_ge(t3.h, ymm[(c, gi, t)])
                            if gi == 0:
                                yev[(c, gi, t)] = v3.inc(v.tensor_copy(out=yf_b[:, t, :],
                                                                       in_=Y_ps[:, (t % 2) * D:(t % 2 + 1) * D]))
                            else:
                                yev[(c, gi, t)] = v3.inc(v.tensor_add(out=yf_b[:, t, :],
                                                                      in0=yf_b[:, t, :],
                                                                      in1=Y_ps[:, (t % 2) * D:(t % 2 + 1) * D]))
                        blk.vector(_vy)

                    if c + 1 < NCH:
                        if gi == 0:
                            em_gate(c + 1)
                            em_sig(c + 1, sq3)
                        elif gi == 2:
                            em_topk(c + 1, 0)
                        elif gi == 4:
                            em_topk(c + 1, 1)
                        elif gi == 6:
                            em_topk(c + 1, 2)

                sq3.flush(blk)

                def _vz(v, c=c, par=par):
                    v.wait_ge(v3.h, yev[(c, len(EGROUPS) - 1, TPC - 1)])
                    if c >= 1:
                        v.wait_ge(outst.h, 16 * c)
                    for t in range(TPC):
                        v3.inc(v.tensor_mul(out=ob_b[:, t, :], in0=yf_b[:, t, :], in1=gb_b[:]))
                    vzmul[c] = v3.n
                    v.wait_ge(v3.h, v3.n)
                    v.wait_ge(x2l[c % 2].h, x2l_mk[c])
                    for t in range(TPC):
                        v3.inc(v.tensor_add(out=ob_b[:, t, :], in0=ob_b[:, t, :],
                                            in1=x2_sb3[:, par, t, :]))
                    x2_free[c] = v3.n
                blk.vector(_vz)
                vzm = v3.n

                def _gout(gp, c=c, vzm=vzm):
                    gp.wait_ge(v3.h, vzm)
                    gp.wait_ge(outst.h, outst.n)
                    outst.inc(gp.dma_start(out=r3(out[tok(c)]), in_=ob_b[:]), 16)
                blk.gpsimd(_gout)

            def _gend(gp):
                outst.wait(gp)
            blk.gpsimd(_gend)

    es.close()
    return nc


def _host_prep(inputs):
    f32 = np.float32
    asn = np.ascontiguousarray
    w13 = np.concatenate([np.asarray(inputs["L0_ffn_w13"], f32),
                          np.asarray(inputs["L0_ffn_w13_s"], f32)[None]], axis=0)
    w2 = np.concatenate([np.asarray(inputs["L0_ffn_w2"], f32),
                         np.asarray(inputs["L0_ffn_w2_s"], f32)[None]], axis=0)
    w13T = asn(np.transpose(w13, (0, 2, 1)))
    w2T = asn(np.transpose(w2, (0, 2, 1)))
    gatewT = asn(np.asarray(inputs["L0_ffn_gate_w"], f32).T)
    ident = np.eye(128, dtype=ml_dtypes.bfloat16)
    onesrow = np.ones((1, 128), dtype=ml_dtypes.bfloat16)
    eexp = np.zeros((NH, NPAIR * 128), dtype=ml_dtypes.bfloat16)
    for p in range(NPAIR):
        for j in range(128):
            eexp[2 * p + j // 64, p * 128 + j] = 1
    erow = np.zeros((NH, NE * 128), dtype=ml_dtypes.bfloat16)
    for e in range(NE):
        erow[e, e * 128:(e + 1) * 128] = 1
    return {
        "wmod_in": asn(np.asarray(inputs["L0_wmod"], f32)),
        "gamma_in": asn(np.asarray(inputs["L0_attn_gamma"], f32).reshape(1, D)),
        "beta_in": asn(np.asarray(inputs["L0_attn_beta"], f32).reshape(1, D)),
        "wqkv_in": asn(np.asarray(inputs["L0_attn_Wqkv"], f32)),
        "wout_in": asn(np.asarray(inputs["L0_attn_Wout"], f32)),
        "gatewT_in": gatewT,
        "gateb_in": asn(np.asarray(inputs["L0_ffn_gate_b"], f32).reshape(1, NE)),
        "w13T_in": w13T,
        "w2T_in": w2T,
        "ident_in": ident,
        "onesrow_in": onesrow,
        "eexp_in": eexp,
        "erow_in": erow,
    }


def kernel(**inputs):
    x = np.asarray(inputs["L0_attn_x"], np.float32)
    cmat = np.asarray(inputs["L0_c"], np.float32)
    B, L, _ = x.shape
    com = _host_prep(inputs)
    nc = build_program(L)
    in_maps = []
    for b in range(B):
        m = dict(com)
        m["x_in"] = np.ascontiguousarray(x[b])
        m["c_row"] = np.ascontiguousarray(cmat[b:b + 1])
        in_maps.append(m)
    res = run_bass_kernel_spmd(nc, in_maps, core_ids=list(range(B)))
    return np.stack([res.results[b]["out"] for b in range(B)], axis=0)


# version 30
# speedup vs baseline: 1.1273x; 1.0030x over previous
import numpy as np
import ml_dtypes
from contextlib import ExitStack

import concourse.bass as bass
import concourse.mybir as mybir
from concourse.bass_utils import run_bass_kernel_spmd

F32 = mybir.dt.float32
BF16 = mybir.dt.bfloat16
AF = mybir.ActivationFunctionType
ALU = mybir.AluOpType
AX = mybir.AxisListType

D = 1024
DQKV = 3 * D
NH = 16
NPAIR = 8
NE = 16
NEX = 17
INTER = 512
ROUTE_SCALE = 2.5
LN_EPS = 1e-5
DEN_EPS = 1e-6
CT = 512
TPC = 4
RCPD = float(1.0 / D)
RSQD = float((1.0 / D) ** 0.5)
EGROUPS = [(0, 1), (2, 3), (4, 5), (6, 7), (8, 9), (10, 11), (12, 13), (14, 15), (16,)]
EGROUPS_OF = {e: gi for gi, grp in enumerate(EGROUPS) for e in grp}
NSLOT = 3


def _preload_tab(nc, s, idx):
    s.add_instruction(mybir.InstLoadActFuncSet(
        name=nc.get_next_instruction_name(), act_func_set_id=idx, ins=[], outs=[]))


class Sem:
    def __init__(self, es, nc, name):
        self.h = es.enter_context(nc.semaphore(name))
        self.n = 0

    def inc(self, inst, k=1):
        inst.then_inc(self.h, k)
        self.n += k
        return self.n

    def wait(self, eng, n=None):
        eng.wait_ge(self.h, self.n if n is None else n)


class SQueue:

    def __init__(self, sem):
        self.sem = sem
        self.q = []

    def add(self, emitfn, inc=1):
        mark = None
        if inc:
            self.sem.n += inc
            mark = self.sem.n
        self.q.append((emitfn, mark, inc))
        return mark

    def flush(self, blk):
        q, self.q = self.q, []
        if not q:
            return

        def body(s):
            for fn, mark, inc in q:
                inst = fn(s)
                if mark is not None:
                    inst.then_inc(self.sem.h, inc)
        blk.scalar(body)


def build_program(L):
    assert L % CT == 0
    NCH = L // CT
    NT = L // 128
    nc = bass.Bass("TRN2", target_bir_lowering=False)

    dr = lambda n, s, dt, k="Internal": nc.dram_tensor(n, s, dt, kind=k).ap()
    x_in = dr("x_in", [L, D], F32, "ExternalInput")
    c_row = dr("c_row", [1, D], F32, "ExternalInput")
    wmod_in = dr("wmod_in", [D, 6 * D], F32, "ExternalInput")
    gamma_in = dr("gamma_in", [1, D], F32, "ExternalInput")
    beta_in = dr("beta_in", [1, D], F32, "ExternalInput")
    wqkv_in = dr("wqkv_in", [D, DQKV], F32, "ExternalInput")
    wout_in = dr("wout_in", [D, D], F32, "ExternalInput")
    gatewT_in = dr("gatewT_in", [D, NE], F32, "ExternalInput")
    gateb_in = dr("gateb_in", [1, NE], F32, "ExternalInput")
    w13T_in = dr("w13T_in", [NEX, D, D], F32, "ExternalInput")
    w2T_in = dr("w2T_in", [NEX, INTER, D], F32, "ExternalInput")
    ident_in = dr("ident_in", [128, 128], BF16, "ExternalInput")
    onesrow_in = dr("onesrow_in", [1, 128], BF16, "ExternalInput")
    eexp_in = dr("eexp_in", [NH, NPAIR * 128], BF16, "ExternalInput")
    erow_in = dr("erow_in", [NH, NE * 128], BF16, "ExternalInput")
    out = dr("out", [L, D], F32, "ExternalOutput")
    w13bf = dr("w13bf", [NEX, D, D], BF16)
    w2bf = dr("w2bf", [NEX, INTER, D], BF16)
    xn_scr = dr("xn_scr", [L, D], BF16)
    q_scr = dr("q_scr", [L, D], BF16)
    x2_scr = dr("x2_scr", [L, D], F32)
    h2_scr = dr("h2_scr", [L, D], BF16)

    es = ExitStack()
    sbuf = lambda n, s, dt: es.enter_context(nc.sbuf_tensor(n, s, dt))
    S = lambda n: Sem(es, nc, n)
    r3 = lambda ap: ap.rearrange("(t p) d -> p t d", p=128)
    tok = lambda c: slice(c * CT, (c + 1) * CT)

    ca1_b = sbuf("ca1_b", [128, D], BF16)
    sa_b = sbuf("sa_b", [128, D], BF16)
    bw_a = sbuf("bw_a", [96, 512], BF16)
    ones96 = sbuf("ones96", [96, 128], BF16)
    bw_b = sbuf("bw_b", [96, 512], BF16)
    A_b = sbuf("A_b", [128, D], BF16)
    B_b = sbuf("B_b", [128, D], BF16)
    cb1_b = sbuf("cb1_b", [128, D], BF16)
    sbb_b = sbuf("sbb_b", [128, D], BF16)
    ga_b = sbuf("ga_b", [128, D], F32)
    gb_b = sbuf("gb_b", [128, D], F32)
    gateb_b = sbuf("gateb_b", [128, NE], F32)
    ident = sbuf("ident", [128, 128], BF16)
    onesrow = sbuf("onesrow", [1, 128], BF16)
    onescol = sbuf("onescol", [128, 1], BF16)
    eexp = sbuf("eexp", [NH, NPAIR * 128], BF16)
    erow = sbuf("erow", [NH, NE * 128], BF16)
    rs1 = sbuf("rs1", [128, NT], F32)
    nmurs1 = sbuf("nmurs1", [128, NT], F32)
    rs3 = sbuf("rs3", [128, NT], F32)
    nmurs3 = sbuf("nmurs3", [128, NT], F32)
    ksum_sb = sbuf("ksum_sb", [128, NPAIR], F32)
    kv_sb = sbuf("kv_sb", [128, NPAIR * 128], BF16)
    ksbd = sbuf("ksbd", [128, NPAIR * NH], BF16)
    junk4 = sbuf("junk4", [128, TPC, D], BF16)
    epsc = sbuf("epsc", [128, 1], F32)
    gatew_sb = sbuf("gatew_sb", [128, 8, NE], BF16)
    es_wo = ExitStack()
    wout_sb = es_wo.enter_context(nc.sbuf_tensor("wout_sb", [128, 8, D], BF16))
    es_wq = ExitStack()
    wqkv_sb = es_wq.enter_context(nc.sbuf_tensor("wqkv_sb", [128, 8, DQKV], BF16))

    wcast = S("wcast")

    with ExitStack() as es0:
        sb0 = lambda n, s, dt: es0.enter_context(nc.sbuf_tensor(n, s, dt))
        ps0 = lambda n, s: es0.enter_context(nc.psum_tensor(n, s, F32))
        c_sb = sb0("c_sb", [1, D], F32)
        sc_bf = sb0("sc_bf", [1, D], BF16)
        sc_f = sb0("sc_f", [128, 8], BF16)
        mod_bf = sb0("mod_bf", [1, 6 * D], BF16)
        gb_row = sb0("gb_row", [1, NE], BF16)
        wmod_bf = sb0("wmod_bf", [128, 8, 3 * D], BF16)
        sgn_t = sb0("sgn_t", [128, D], F32)
        amx_t = sb0("amx_t", [128, D], F32)
        gammaT = sb0("gammaT", [128, 8], F32)
        betaT = sb0("betaT", [128, 8], BF16)
        mod_ps = ps0("mod_ps", [1, 3 * D])
        bc_ps = ps0("bc_ps", [128, D])
        bw_ps = mod_ps[0:1, 0:512]
        ch = S("p0ch")
        chd = S("p0chd")
        chs = S("p0chs")

        with nc.Block("p0") as blk:
            @blk.gpsimd
            def _(gp):
                chd.inc(gp.dma_start(out=wqkv_sb[:], in_=wqkv_in.rearrange("(k p) n -> p k n", p=128)), 16)
                chd.inc(gp.dma_start(out=wout_sb[:], in_=wout_in.rearrange("(k p) n -> p k n", p=128)), 16)
                chd.inc(gp.dma_start(out=gatew_sb[:], in_=gatewT_in.rearrange("(k p) n -> p k n", p=128)), 16)
                chd.inc(gp.dma_start(out=gb_row[:], in_=gateb_in), 16)
                bTv = beta_in.rearrange("o (kc p) -> kc p o", kc=8)
                for kc in range(8):
                    chd.inc(gp.dma_start(out=betaT[:, kc:kc + 1], in_=bTv[kc]), 16)

            @blk.sync
            def _(sy):
                chs.inc(sy.dma_start(out=c_sb[:], in_=c_row), 16)
                gTv = gamma_in.rearrange("o (kc p) -> kc p o", kc=8)
                for kc in range(8):
                    chs.inc(sy.dma_start(out=gammaT[:, kc:kc + 1], in_=gTv[kc]), 16)
                chs.inc(sy.dma_start(out=ident[:], in_=ident_in), 16)
                chs.inc(sy.dma_start(out=onesrow[:], in_=onesrow_in), 16)
                chs.inc(sy.dma_start(out=eexp[:], in_=eexp_in), 16)
                chs.inc(sy.dma_start(out=erow[:], in_=erow_in), 16)

            def step(eng_attr, fn):
                def body(e):
                    ch.wait(e)
                    chd.wait(e)
                    chs.wait(e)
                    fn(e)
                getattr(blk, eng_attr)(body)

            step("scalar", lambda e: ch.inc(e.activation(c_sb[:], c_sb[:], AF.Silu)))

            def _v0(v):
                ch.inc(v.memset(epsc[:], LN_EPS))
                ch.wait(v)
                ch.inc(v.memset(onescol[:], 1.0))
                ch.wait(v)
                ch.inc(v.memset(ones96[:], 1.0))
                ch.wait(v)
                ch.inc(v.memset(ksum_sb[:], 0.0))
                ch.wait(v)
                ch.inc(v.tensor_copy(out=sc_bf[:], in_=c_sb[:]))
            step("vector", _v0)

            def _t0(tn):
                for kc in range(8):
                    m = tn.matmul(bc_ps[:, kc:kc + 1], sc_bf[0:1, kc * 128:(kc + 1) * 128],
                                  onescol[0:1, 0:1], start=(kc == 0), stop=(kc == 7))
                ch.inc(m)
            step("tensor", _t0)
            step("vector", lambda v: ch.inc(v.tensor_copy(out=sc_f[:], in_=bc_ps[:, 0:8])))

            for half in range(2):
                def _gw(gp, half=half):
                    chd.inc(gp.dma_start(out=wmod_bf[:],
                                        in_=wmod_in[:, half * 3072:(half + 1) * 3072]
                                        .rearrange("(k p) n -> p k n", p=128)), 16)
                step("gpsimd", _gw)

                def _tm(tn, half=half):
                    for ns in range(6):
                        for kc in range(8):
                            m = tn.matmul(mod_ps[0:1, ns * 512:(ns + 1) * 512],
                                          sc_f[:, kc:kc + 1],
                                          wmod_bf[:, kc, ns * 512:(ns + 1) * 512],
                                          start=(kc == 0), stop=(kc == 7))
                    ch.inc(m)
                step("tensor", _tm)
                step("vector", lambda v, half=half: ch.inc(
                    v.tensor_copy(out=mod_bf[0:1, half * 3072:(half + 1) * 3072], in_=mod_ps[:])))

            items = [("sa", mod_bf[0:1, 0:D]), ("ca", mod_bf[0:1, D:2 * D]),
                     ("ga", mod_bf[0:1, 2 * D:3 * D]), ("sb", mod_bf[0:1, 3 * D:4 * D]),
                     ("cb", mod_bf[0:1, 4 * D:5 * D]), ("gb", mod_bf[0:1, 5 * D:6 * D]),
                     ("gateb", gb_row[0:1, 0:NE])]
            for name, src in items:
                def _tb(tn, src=src):
                    n = src.shape[-1]
                    for ns in range((n + 511) // 512):
                        w = min(512, n - ns * 512)
                        m = tn.matmul(bc_ps[:, ns * 512:ns * 512 + w], onesrow[0:1, :],
                                      src[0:1, ns * 512:ns * 512 + w], start=True, stop=True)
                    ch.inc(m)
                step("tensor", _tb)
                if name in ("ga", "gb"):
                    step("scalar", lambda s: ch.inc(s.activation(amx_t[:], bc_ps[:], AF.Abs)))

                def _vb(v, name=name):
                    if name == "sa":
                        ch.inc(v.tensor_copy(out=sa_b[:], in_=bc_ps[:]))
                    elif name == "ca":
                        ch.inc(v.tensor_scalar(out=ca1_b[:], in0=bc_ps[:], scalar1=1.0,
                                               scalar2=None, op0=ALU.add))
                    elif name in ("ga", "gb"):
                        dst = ga_b if name == "ga" else gb_b
                        ch.inc(v.tensor_scalar(out=sgn_t[:], in0=bc_ps[:], scalar1=0.0,
                                               scalar2=2.0, op0=ALU.is_ge, op1=ALU.mult))
                        ch.wait(v)
                        ch.inc(v.tensor_scalar(out=sgn_t[:], in0=sgn_t[:], scalar1=-1.0,
                                               scalar2=None, op0=ALU.add))
                        ch.wait(v)
                        ch.inc(v.tensor_scalar(out=amx_t[:], in0=amx_t[:], scalar1=0.25,
                                               scalar2=None, op0=ALU.max))
                        ch.wait(v)
                        ch.inc(v.tensor_mul(out=dst[:], in0=sgn_t[:], in1=amx_t[:]))
                        if name == "ga":
                            ch.wait(v)
                            ch.inc(v.tensor_mul(out=A_b[:], in0=ga_b[:], in1=ca1_b[:]))
                            ch.wait(v)
                            ch.inc(v.tensor_mul(out=B_b[:], in0=ga_b[:], in1=sa_b[:]))
                    elif name == "sb":
                        ch.inc(v.tensor_copy(out=sbb_b[:], in_=bc_ps[:]))
                    elif name == "cb":
                        ch.inc(v.tensor_scalar(out=cb1_b[:], in0=bc_ps[:], scalar1=1.0,
                                               scalar2=None, op0=ALU.add))
                    elif name == "gateb":
                        ch.inc(v.tensor_copy(out=gateb_b[:], in_=bc_ps[:, 0:NE]))
                step("vector", _vb)

            for seg in range(6):
                def _tbw(tn, seg=seg):
                    for kc in range(8):
                        m = tn.matmul(bw_ps[0:1, :], betaT[:, kc:kc + 1],
                                      wqkv_sb[:, kc, seg * 512:(seg + 1) * 512],
                                      start=(kc == 0), stop=(kc == 7))
                    ch.inc(m)
                step("tensor", _tbw)
                dst = (bw_a if seg < 3 else bw_b)
                row = (seg % 3) * 32
                step("scalar", lambda s, dst=dst, row=row: ch.inc(
                    s.copy(out=dst[row:row + 1, 0:512], in_=bw_ps[0:1, :])))
            for kc in range(8):
                def _sgw(s, kc=kc):
                    ch.inc(s.activation(wqkv_sb[:, kc, :], wqkv_sb[:, kc, :], AF.Identity,
                                        scale=gammaT[:, kc:kc + 1]))
                step("scalar", _sgw)
            for kc in range(8):
                def _vgw(v, kc=kc):
                    ch.inc(v.tensor_mul(out=wout_sb[:, kc, :], in0=wout_sb[:, kc, :],
                                        in1=ga_b[:]))
                step("vector", _vgw)


    with ExitStack() as es1:
        sb1 = lambda n, s, dt: es1.enter_context(nc.sbuf_tensor(n, s, dt))
        ps1 = lambda n, s: es1.enter_context(nc.psum_tensor(n, s, F32))
        x_sb = sb1("x_sb", [128, 2, TPC, D], BF16)
        xn_t = sb1("xn_t", [128, 2, TPC, D], BF16)
        xn_f = sb1("xn_f", [128, 2, 8, CT], BF16)
        q_sb = sb1("q_sb", [128, 2, TPC, D], BF16)
        k_sb = sb1("k_sb", [128, D], BF16)
        v_sb = sb1("v_sb", [128, D], BF16)
        e1k = sb1("e1k", [128, D], BF16)
        t1b = sb1("t1b", [128, 4, D], BF16)
        h_t = sb1("h_t", [128, TPC, D], BF16)
        st_s = sb1("st_s", [128, 32], F32)
        bnx = sb1("bnx", [128, 2, TPC, 12], F32)
        mvx = sb1("mvx", [128, 2, TPC, 2], F32)
        bnh = sb1("bnh", [128, TPC, 12], F32)
        mvh = sb1("mvh", [128, TPC, 2], F32)
        rsh = sb1("rsh", [128, TPC], F32)
        nmh = sb1("nmh", [128, TPC], F32)
        K_ps = ps1("K_ps", [128, D])
        V_ps = ps1("V_ps", [128, D])
        Q_ps = ps1("Q_ps", [128, D])
        KV_ps = ps1("KV_ps", [128, NPAIR * 128])

        xld = [S("xld0"), S("xld1")]
        xnst = [S("xnst0"), S("xnst1")]
        xnfl = [S("xnfl0"), S("xnfl1")]
        qspl = [S("qspl0"), S("qspl1")]
        v1 = S("v1")
        s1 = S("s1")
        g1 = S("g1")
        t1 = S("t1")
        kmm = [0] * NT; vmm = [0] * NT; qmm = [0] * NT; gmm = [0] * NT
        gsum = [0] * NT
        kfin = [0] * NT; vcp = [0] * NT; qfin = [0] * NT; ksev = [0] * NT
        ek_mk = [0] * NT; eq_mk = [0] * NT
        xnf_ok = [0] * NCH
        xld_mk = [0] * (NCH + 2)
        xnst_mk = [0] * (NCH + 2)
        qspl_mk = [0] * (NCH + 2)
        t1last = [0] * (NCH + 1)
        hmkA = [None] * (NCH + 1)
        t2last = [0] * (NCH + 1)
        t2mks_h = [0] * (NCH + 1)
        aggx_mk = [0] * (NCH + 1)
        rsx_mark = [0] * (NCH + 1)
        nmx_mark = [0] * (NCH + 1)
        aggh_mk = [0] * (NCH + 1)
        rsh_mark = [0] * (NCH + 1)
        nmh_mark = [0] * (NCH + 1)

        with nc.Block("p1") as blk:
            def _gx0(gp):
                xld_mk[0] = xld[0].inc(gp.dma_start(out=x_sb[:, 0], in_=r3(x_in[tok(0)])), 16)
                if NCH > 1:
                    xld_mk[1] = xld[1].inc(gp.dma_start(out=x_sb[:, 1], in_=r3(x_in[tok(1)])), 16)
            blk.gpsimd(_gx0)
            castq = [(w13bf[e], w13T_in[e]) for e in range(NEX)] + \
                    [(w2bf[e], w2T_in[e]) for e in range(NEX)]

            def emit_mm_kv(c, t, sq):
                g = c * TPC + t
                par = c % 2

                def _t(tn):
                    if t == 0:
                        tn.wait_ge(xnfl[par].h, xnf_ok[c])
                    if g > 0:
                        tn.wait_ge(v1.h, ksev[g - 1])
                    for kc in range(8):
                        for hh in range(2):
                            tn.matmul(K_ps[:, hh * 512:(hh + 1) * 512],
                                      xn_f[:, par, kc, t * 128:(t + 1) * 128],
                                      wqkv_sb[:, kc, D + hh * 512:D + (hh + 1) * 512],
                                      start=(kc == 0), stop=False)
                    for hh in range(2):
                        m = tn.matmul(K_ps[:, hh * 512:(hh + 1) * 512],
                                      ones96[64 * (1 - hh):64 * (1 - hh) + 1, 0:128],
                                      (bw_a[64:65, 0:512] if hh == 0 else bw_b[0:1, 0:512]),
                                      start=False, stop=True)
                    kmm[g] = t1.inc(m)
                    for kc in range(8):
                        for hh in range(2):
                            tn.matmul(V_ps[:, hh * 512:(hh + 1) * 512],
                                      xn_f[:, par, kc, t * 128:(t + 1) * 128],
                                      wqkv_sb[:, kc, 2 * D + hh * 512:2 * D + (hh + 1) * 512],
                                      start=(kc == 0), stop=False)
                    for hh in range(2):
                        m = tn.matmul(V_ps[:, hh * 512:(hh + 1) * 512],
                                      ones96[32 * (1 + hh):32 * (1 + hh) + 1, 0:128],
                                      bw_b[32 * (1 + hh):32 * (1 + hh) + 1, 0:512],
                                      start=False, stop=True)
                    vmm[g] = t1.inc(m)
                blk.tensor(_t)

                def _sk(s, g=g):
                    s.wait_ge(t1.h, kmm[g])
                    if g > 0:
                        s.wait_ge(v1.h, qfin[g - 1])
                    return s.activation(e1k[:], K_ps[:], AF.Exp)
                ek_mk[g] = sq.add(_sk)

            def emit_v_elukv(c, t):
                g = c * TPC + t

                def _v(v):
                    v.wait_ge(s1.h, ek_mk[g])
                    v1.inc(v.tensor_scalar(out=e1k[:], in0=e1k[:], scalar1=1.0,
                                           scalar2=None, op0=ALU.min))
                    v.wait_ge(v1.h, v1.n)
                    if g > 0:
                        v.wait_ge(t1.h, gmm[g - 1])
                    kfin[g] = v1.inc(v.scalar_tensor_tensor(out=k_sb[:], in0=K_ps[:], scalar=1.0,
                                                            in1=e1k[:], op0=ALU.add, op1=ALU.max))
                    v.wait_ge(t1.h, vmm[g])
                    vcp[g] = v1.inc(v.tensor_copy(out=v_sb[:], in_=V_ps[:]))
                blk.vector(_v)

            def emit_v_ksev(c, t):
                g = c * TPC + t

                def _v2(v):
                    v.wait_ge(t1.h, gsum[g])
                    ksev[g] = v1.inc(v.tensor_add(out=ksum_sb[:], in0=ksum_sb[:],
                                                  in1=K_ps[:, 0:NPAIR]))
                blk.vector(_v2)

            def emit_t_gram(c, t):
                g = c * TPC + t

                def _t2(tn):
                    tn.wait_ge(v1.h, vcp[g])
                    tn.wait_ge(v1.h, kfin[g])
                    for p in range(NPAIR):
                        m = tn.matmul(K_ps[:, p:p + 1],
                                      k_sb[:, p * 128:(p + 1) * 128], onescol[:],
                                      start=(p == 0), stop=(p == NPAIR - 1))
                    gsum[g] = t1.inc(m)
                    for p in range(NPAIR):
                        m = tn.matmul(KV_ps[:, p * 128:(p + 1) * 128],
                                      k_sb[:, p * 128:(p + 1) * 128],
                                      v_sb[:, p * 128:(p + 1) * 128],
                                      start=(g == 0 and p % 4 == 0), stop=(g == NT - 1 and p % 4 == 3))
                    gmm[g] = t1.inc(m)
                blk.tensor(_t2)

            def emit_t_q(c, t, sq):
                g = c * TPC + t
                par = c % 2

                def _t3(tn):
                    if g > 0:
                        tn.wait_ge(v1.h, qfin[g - 1])
                    for kc in range(8):
                        for hh in range(2):
                            tn.matmul(Q_ps[:, hh * 512:(hh + 1) * 512],
                                      xn_f[:, par, kc, t * 128:(t + 1) * 128],
                                      wqkv_sb[:, kc, hh * 512:(hh + 1) * 512],
                                      start=(kc == 0), stop=False)
                    for hh in range(2):
                        m = tn.matmul(Q_ps[:, hh * 512:(hh + 1) * 512],
                                      ones96[32 * hh:32 * hh + 1, 0:128],
                                      bw_a[32 * hh:32 * hh + 1, 0:512],
                                      start=False, stop=True)
                    qmm[g] = t1.inc(m)
                blk.tensor(_t3)

                def _sq_(s, g=g):
                    s.wait_ge(t1.h, qmm[g])
                    s.wait_ge(v1.h, kfin[g])
                    return s.activation(e1k[:], Q_ps[:], AF.Exp)
                eq_mk[g] = sq.add(_sq_)

            def emit_v_eluq(c, t):
                g = c * TPC + t

                def _v(v):
                    v.wait_ge(s1.h, eq_mk[g])
                    v1.inc(v.tensor_scalar(out=e1k[:], in0=e1k[:], scalar1=1.0,
                                           scalar2=None, op0=ALU.min))
                    v.wait_ge(v1.h, v1.n)
                    if c >= 2:
                        v.wait_ge(qspl[c % 2].h, qspl_mk[c - 2])
                    qfin[g] = v1.inc(v.scalar_tensor_tensor(out=q_sb[:, c % 2, t, :], in0=Q_ps[:],
                                                            scalar=1.0, in1=e1k[:],
                                                            op0=ALU.add, op1=ALU.max))
                blk.vector(_v)

            def emit_lnA(ca, sq):
                parA = ca % 2

                def _vbn(v, ca=ca, parA=parA):
                    v.wait_ge(xld[parA].h, xld_mk[ca])
                    if ca >= 2:
                        v.wait_ge(s1.h, rsx_mark[ca - 2])
                    for t in range(TPC):
                        v1.inc(v.bn_stats(out=bnx[:, parA, t, 0:6],
                                          in_=x_sb[:, parA, t, 0:512]))
                        v1.inc(v.bn_stats(out=bnx[:, parA, t, 6:12],
                                          in_=x_sb[:, parA, t, 512:1024]))
                    v.wait_ge(v1.h, v1.n)
                    for t in range(TPC):
                        v1.inc(v.bn_aggr(out=mvx[:, parA, t, :], in_=bnx[:, parA, t, :]))
                    aggx_mk[ca] = v1.n
                blk.vector(_vbn)

                def _srs(s, ca=ca, parA=parA):
                    s.wait_ge(v1.h, aggx_mk[ca])
                    for t in range(TPC):
                        m = s.activation(mvx[:, parA, t, 1:2], mvx[:, parA, t, 1:2],
                                         AF.Ln, bias=epsc[:, 0:1])
                    return m
                lnx_mk = sq.add(_srs, inc=4)

                def _sex(s, ca=ca, parA=parA, _lm=lnx_mk):
                    s.wait_ge(s1.h, _lm)
                    for t in range(TPC):
                        g = ca * TPC + t
                        m = s.activation(rs1[:, g:g + 1], mvx[:, parA, t, 1:2],
                                         AF.Exp, scale=-0.5)
                    return m
                rsx_mark[ca] = sq.add(_sex, inc=4)

                def _vnm(v, ca=ca, parA=parA):
                    v.wait_ge(s1.h, rsx_mark[ca])
                    for t in range(TPC):
                        g = ca * TPC + t
                        v1.inc(v.tensor_scalar(out=nmurs1[:, g:g + 1], in0=mvx[:, parA, t, 0:1],
                                               scalar1=rs1[:, g:g + 1], scalar2=-1.0,
                                               op0=ALU.mult, op1=ALU.mult))
                    nmx_mark[ca] = v1.n
                blk.vector(_vnm)

            for it in range(NCH + 1):
                cc, c = it, it - 1
                par = cc % 2
                B0 = 16 * par
                sq = SQueue(s1)
                if it == 0:
                    emit_lnA(0, sq)

                if cc < NCH:
                    t1mk = {}
                    hmk = {}
                    hmkA[cc] = hmk
                    for t in range(TPC):
                        def _t1(s, cc=cc, t=t):
                            s.wait_ge(v1.h, nmx_mark[cc])
                            if cc >= 1:
                                s.wait_ge(v1.h, hmkA[cc - 1][t])
                            g = cc * TPC + t
                            return s.activation(t1b[:, t, :], x_sb[:, cc % 2, t, :], AF.Identity,
                                                bias=nmurs1[:, g:g + 1], scale=rs1[:, g:g + 1])
                        t1mk[t] = sq.add(_t1)
                    t1last[cc] = t1mk[TPC - 1]

                    def _vh01(v, cc=cc, t1mk=t1mk, hmk=hmk):
                        for t in range(2):
                            v.wait_ge(s1.h, t1mk[t])
                            if cc >= 2 and t == 0:
                                v.wait_ge(s1.h, t2last[cc - 2])
                            v1.inc(v.tensor_mul(out=h_t[:, t, :], in0=t1b[:, t, :],
                                                in1=ca1_b[:]))
                            v.wait_ge(v1.h, v1.n)
                            hmk[t] = v1.inc(v.tensor_add(out=h_t[:, t, :], in0=h_t[:, t, :],
                                                         in1=sa_b[:]))
                            v.wait_ge(v1.h, v1.n)
                            v1.inc(v.bn_stats(out=bnh[:, t, 0:6], in_=h_t[:, t, 0:512]))
                            v1.inc(v.bn_stats(out=bnh[:, t, 6:12], in_=h_t[:, t, 512:1024]))
                    blk.vector(_vh01)

                if c >= 0:
                    emit_mm_kv(c, 0, sq)
                    emit_v_elukv(c, 0)
                    emit_t_gram(c, 0)
                    emit_v_ksev(c, 0)
                    emit_t_q(c, 0, sq)
                    emit_v_eluq(c, 0)

                if cc < NCH:
                    def _vh23(v, cc=cc, t1mk=t1mk, hmk=hmk):
                        for t in range(2, TPC):
                            v.wait_ge(s1.h, t1mk[t])
                            v1.inc(v.tensor_mul(out=h_t[:, t, :], in0=t1b[:, t, :],
                                                in1=ca1_b[:]))
                            v.wait_ge(v1.h, v1.n)
                            hmk[t] = v1.inc(v.tensor_add(out=h_t[:, t, :], in0=h_t[:, t, :],
                                                         in1=sa_b[:]))
                            v.wait_ge(v1.h, v1.n)
                            v1.inc(v.bn_stats(out=bnh[:, t, 0:6], in_=h_t[:, t, 0:512]))
                            v1.inc(v.bn_stats(out=bnh[:, t, 6:12], in_=h_t[:, t, 512:1024]))
                        v.wait_ge(v1.h, v1.n)
                        if cc >= 1:
                            v.wait_ge(s1.h, rsh_mark[cc - 1])
                        for t in range(TPC):
                            v1.inc(v.bn_aggr(out=mvh[:, t, :], in_=bnh[:, t, :]))
                        aggh_mk[cc] = v1.n
                    blk.vector(_vh23)

                if cc < NCH:
                    def _srs2(s, cc=cc):
                        s.wait_ge(v1.h, aggh_mk[cc])
                        for t in range(TPC):
                            m = s.activation(mvh[:, t, 1:2], mvh[:, t, 1:2],
                                             AF.Ln, bias=epsc[:, 0:1])
                        return m
                    lnh_mk = sq.add(_srs2, inc=4)

                    def _seh(s, cc=cc, _lm=lnh_mk):
                        s.wait_ge(s1.h, _lm)
                        if cc >= 1:
                            s.wait_ge(s1.h, t2last[cc - 1])
                        for t in range(TPC):
                            m = s.activation(rsh[:, t:t + 1], mvh[:, t, 1:2],
                                             AF.Exp, scale=-0.5)
                        return m
                    rsh_mark[cc] = sq.add(_seh, inc=4)

                    def _vnmh(v, cc=cc):
                        v.wait_ge(s1.h, rsh_mark[cc])
                        for t in range(TPC):
                            v1.inc(v.tensor_scalar(out=nmh[:, t:t + 1], in0=mvh[:, t, 0:1],
                                                   scalar1=rsh[:, t:t + 1], scalar2=-1.0,
                                                   op0=ALU.mult, op1=ALU.mult))
                        nmh_mark[cc] = v1.n
                    blk.vector(_vnmh)

                if c >= 0:
                    emit_mm_kv(c, 1, sq)
                    emit_v_elukv(c, 1)
                    emit_t_gram(c, 1)
                    emit_v_ksev(c, 1)
                    emit_t_q(c, 1, sq)
                    emit_v_eluq(c, 1)

                if cc < NCH:
                    t2mks = {}
                    for t in range(TPC):
                        def _t2(s, cc=cc, t=t, par=par):
                            s.wait_ge(v1.h, nmh_mark[cc])
                            if t == 0 and cc >= 2:
                                s.wait_ge(xnst[par].h, xnst_mk[cc - 2])
                            return s.activation(xn_t[:, par, t, :], h_t[:, t, :], AF.Identity,
                                                bias=nmh[:, t:t + 1],
                                                scale=rsh[:, t:t + 1])
                        t2mks[t] = sq.add(_t2)
                    t2last[cc] = t2mks[TPC - 1]
                    t2mks_h[cc] = t2mks[TPC - 1]

                    def _sspq(s, cc=cc, par=par):
                        s.wait_ge(s1.h, t2mks_h[cc])
                        s.wait_ge(xnst[par].h, xnst[par].n)
                        xnst_mk[cc] = xnst[par].inc(
                            s.dma_start(out=r3(xn_scr[tok(cc)]), in_=xn_t[:, par]), 16)
                    sq.add(_sspq, inc=0)

                if c >= 0:
                    emit_mm_kv(c, 2, sq)
                    emit_v_elukv(c, 2)
                    emit_t_gram(c, 2)
                    emit_v_ksev(c, 2)
                    emit_t_q(c, 2, sq)
                    emit_v_eluq(c, 2)

                if cc + 1 < NCH:
                    emit_lnA(cc + 1, sq)

                if c >= 0:
                    emit_mm_kv(c, 3, sq)
                    emit_v_elukv(c, 3)
                    emit_t_gram(c, 3)
                    emit_v_ksev(c, 3)
                    emit_t_q(c, 3, sq)
                    emit_v_eluq(c, 3)

                sq.flush(blk)

                if cc < NCH:
                    def _syt(sy, cc=cc, par=par):
                        sy.wait_ge(xnst[par].h, xnst_mk[cc])
                        if cc >= 2:
                            sy.wait_ge(t1.h, qmm[(cc - 2) * TPC + TPC - 1])
                        sy.wait_ge(xnfl[par].h, xnfl[par].n)
                        for kc in range(8):
                            m = sy.dma_start_transpose(out=xn_f[:, par, kc, :],
                                                       in_=xn_scr[tok(cc), kc * 128:(kc + 1) * 128])
                            xnfl[par].inc(m, 16)
                        xnf_ok[cc] = xnfl[par].n
                    blk.sync(_syt)

                    def _gxl(gp, cc=cc, par=par):
                        if cc + 2 < NCH:
                            gp.wait_ge(s1.h, t1last[cc])
                            gp.wait_ge(xld[par].h, xld[par].n)
                            xld_mk[cc + 2] = xld[par].inc(
                                gp.dma_start(out=x_sb[:, par], in_=r3(x_in[tok(cc + 2)])), 16)
                        for _ in range(3):
                            if castq:
                                o, i = castq.pop(0)
                                wcast.inc(gp.dma_start(out=o, in_=i), 16)
                    blk.gpsimd(_gxl)

                if c >= 0:
                    def _sq2(sc, c=c):
                        sc.wait_ge(v1.h, qfin[c * TPC + TPC - 1])
                        sc.wait_ge(qspl[c % 2].h, qspl[c % 2].n)
                        qspl_mk[c] = qspl[c % 2].inc(
                            sc.dma_start(out=r3(q_scr[tok(c)]), in_=q_sb[:, c % 2]), 16)
                    blk.scalar(_sq2)

            if castq:
                def _gcrest(gp):
                    for o, i in castq:
                        wcast.inc(gp.dma_start(out=o, in_=i), 16)
                    castq.clear()
                blk.gpsimd(_gcrest)

            def _vprep(v):
                v.wait_ge(t1.h, gmm[NT - 1])
                v.wait_ge(v1.h, ksev[NT - 1])
                v1.inc(v.memset(kv_sb[:], 0.0))
                v1.inc(v.memset(ksbd[:], 0.0))
                v.wait_ge(v1.h, v1.n)
                for p in range(NPAIR):
                    v1.inc(v.tensor_copy(out=kv_sb[0:64, p * 128:p * 128 + 64],
                                         in_=KV_ps[0:64, p * 128:p * 128 + 64]))
                    v1.inc(v.tensor_copy(out=kv_sb[64:128, p * 128 + 64:p * 128 + 128],
                                         in_=KV_ps[64:128, p * 128 + 64:p * 128 + 128]))
                    v1.inc(v.tensor_copy(out=ksbd[0:64, p * NH + 2 * p:p * NH + 2 * p + 1],
                                         in_=ksum_sb[0:64, p:p + 1]))
                    v1.inc(v.tensor_copy(out=ksbd[64:128, p * NH + 2 * p + 1:p * NH + 2 * p + 2],
                                         in_=ksum_sb[64:128, p:p + 1]))
            blk.vector(_vprep)

    es_wq.close()

    with ExitStack() as es2:
        sb2 = lambda n, s, dt: es2.enter_context(nc.sbuf_tensor(n, s, dt))
        ps2 = lambda n, s: es2.enter_context(nc.psum_tensor(n, s, F32))
        x_sb = sb2("x_sb2", [128, 2, TPC, D], F32)
        qT = sb2("qT", [128, 2, 8, CT], BF16)
        attn_f = sb2("attn_f", [128, NPAIR, CT], BF16)
        rb_sb = sb2("rb_sb", [128, 2, CT], F32)
        de_sb = sb2("de_sb", [NH, CT], F32)
        rcp_bf = sb2("rcp_bf", [NH, CT], BF16)
        t1b2 = sb2("t1b2", [128, 2, D], BF16)
        t3b2 = sb2("t3b2", [128, 2, D], BF16)
        u_b = sb2("u_b", [128, D], BF16)
        u2_b = sb2("u2_b", [128, 2, D], BF16)
        r_b = sb2("r_b", [128, D], F32)
        vv_b = sb2("vv_b", [128, D], F32)
        x2_sb = sb2("x2_sb", [128, TPC, D], F32)
        h2A = sb2("h2A", [128, D], BF16)
        h2_t = sb2("h2_t", [128, TPC, D], BF16)
        st2 = sb2("st2", [128, 16], F32)
        bn3 = sb2("bn3", [128, TPC, 12], F32)
        mv3 = sb2("mv3", [128, TPC, 2], F32)
        DEN_ps = ps2("DEN_ps", [NH, CT])
        RB_ps = ps2("RB_ps", [128, 2 * CT])
        QO_ps = ps2("QO_ps", [128, 2 * CT])
        WO_ps = ps2("WO_ps", [128, D])

        xld2 = [S("xld2a"), S("xld2b")]
        qTl = [S("qTla"), S("qTlb")]
        xld2_mk = [0] * (NCH + 2)
        qTl_mk = [0] * (NCH + 2)
        v2 = S("v2")
        s2 = S("s2")
        g2 = S("g2")
        t2 = S("t2")
        x2st = S("x2st")
        h2st = S("h2st")
        den_free = [0] * NCH
        qT_free = [0] * NCH
        x_free_v = [0] * NCH
        x_free_s = [0] * NCH
        x_free_g = [0] * NCH
        gusA = {}
        vh2mA = {}
        wo_free = [0] * NT
        r_done = [0] * NT
        h2a_done = [0] * NT
        wo_last_ldw = [0] * NCH
        agg3_mk = [0] * NCH
        rs3q_mark = [0] * NCH
        nm3_mark = [0] * NCH

        with nc.Block("p2a") as blk:
            def _sy0(sy):
                qspl[0].wait(sy)
                qspl[1].wait(sy)
                xld2_mk[0] = xld2[0].inc(sy.dma_start(out=x_sb[:, 0], in_=r3(x_in[tok(0)])), 16)
                for kc in range(8):
                    qTl[0].inc(sy.dma_start_transpose(out=qT[:, 0, kc, :],
                                                      in_=q_scr[tok(0), kc * 128:(kc + 1) * 128]), 16)
                qTl_mk[0] = qTl[0].n
            blk.sync(_sy0)

            tden_mk = [0] * NCH
            vrcp_mk = [0] * NCH

            def em_td(cn):
                def _td(tn, cn=cn):
                    tn.wait_ge(qTl[cn % 2].h, qTl_mk[cn])
                    if cn >= 1:
                        tn.wait_ge(v2.h, den_free[cn - 1])
                    for p in range(NPAIR):
                        m = tn.matmul(DEN_ps[:], ksbd[:, p * NH:(p + 1) * NH],
                                      qT[:, cn % 2, p, :], start=(p == 0), stop=(p == NPAIR - 1))
                    tden_mk[cn] = t2.inc(m)
                blk.tensor(_td)

            def em_vd(cn):
                def _vd(v, cn=cn):
                    v.wait_ge(t2.h, tden_mk[cn])
                    den_free[cn] = v2.inc(v.tensor_scalar(out=de_sb[:], in0=DEN_ps[:],
                                                          scalar1=DEN_EPS, scalar2=None, op0=ALU.add))
                    v.wait_ge(v2.h, v2.n)
                    v2.inc(v.reciprocal(de_sb[:], de_sb[:]))
                    v.wait_ge(v2.h, v2.n)
                    vrcp_mk[cn] = v2.inc(v.tensor_copy(out=rcp_bf[:], in_=de_sb[:]))
                blk.vector(_vd)

            for c in range(NCH):
                par = c % 2

                def _syc(sy, c=c):
                    if c + 1 < NCH:
                        if c >= 1:
                            sy.wait_ge(v2.h, x_free_v[c - 1])
                            sy.wait_ge(s2.h, x_free_s[c - 1])
                            sy.wait_ge(t2.h, qT_free[c - 1])
                        sy.wait_ge(xld2[(c + 1) % 2].h, xld2[(c + 1) % 2].n)
                        sy.wait_ge(qTl[(c + 1) % 2].h, qTl[(c + 1) % 2].n)
                        xld2_mk[c + 1] = xld2[(c + 1) % 2].inc(
                            sy.dma_start(out=x_sb[:, (c + 1) % 2], in_=r3(x_in[tok(c + 1)])), 16)
                        for kc in range(8):
                            qTl[(c + 1) % 2].inc(
                                sy.dma_start_transpose(out=qT[:, (c + 1) % 2, kc, :],
                                                       in_=q_scr[tok(c + 1), kc * 128:(kc + 1) * 128]), 16)
                        qTl_mk[c + 1] = qTl[(c + 1) % 2].n
                blk.sync(_syc)

                if c == 0:
                    em_td(0)
                    em_vd(0)
                vrcp = vrcp_mk[c]

                sq2 = SQueue(s2)
                rbev = [0] * NPAIR
                aev = [0] * NPAIR
                for p in range(NPAIR):
                    def _tp(tn, p=p, c=c):
                        if p == 0:
                            tn.wait_ge(v2.h, vrcp)
                        if p >= 2:
                            tn.wait_ge(v2.h, rbev[p - 2])
                        m = tn.matmul(RB_ps[:, (p % 2) * CT:(p % 2 + 1) * CT],
                                      eexp[:, p * 128:(p + 1) * 128], rcp_bf[:],
                                      start=True, stop=True)
                        t2.inc(m)
                        if p >= 2:
                            tn.wait_ge(v2.h, aev[p - 2])
                        if c >= 1 and p == 0:
                            tn.wait_ge(v2.h, wo_free[c * TPC - 1])
                        m = tn.matmul(QO_ps[:, (p % 2) * CT:(p % 2 + 1) * CT],
                                      kv_sb[:, p * 128:(p + 1) * 128],
                                      qT[:, par, p, :], start=True, stop=True)
                        t2.inc(m)
                    blk.tensor(_tp)
                    tpp = t2.n

                    def _vp(v, p=p, tpp=tpp):
                        v.wait_ge(t2.h, tpp - 1)
                        rbev[p] = v2.inc(v.tensor_copy(out=rb_sb[:, p % 2, :],
                                                       in_=RB_ps[:, (p % 2) * CT:(p % 2 + 1) * CT]))
                        v.wait_ge(t2.h, tpp)
                        v.wait_ge(v2.h, rbev[p])
                        aev[p] = v2.inc(v.tensor_mul(out=attn_f[:, p, :],
                                                     in0=QO_ps[:, (p % 2) * CT:(p % 2 + 1) * CT],
                                                     in1=rb_sb[:, p % 2, :]))
                    blk.vector(_vp)
                qT_free[c] = t2.n

                twms = {}
                vx2 = {}
                gus = {}
                t3mks = {}
                vh2m = {}
                for t in range(TPC):
                    gt = c * TPC + t

                    def _tw(tn, t=t, gt=gt):
                        if t == 0:
                            tn.wait_ge(v2.h, aev[NPAIR - 1])
                        if gt > 0:
                            tn.wait_ge(v2.h, wo_free[gt - 1])
                        for kc in range(8):
                            for hh in range(2):
                                m = tn.matmul(WO_ps[:, hh * 512:(hh + 1) * 512],
                                              attn_f[:, kc, t * 128:(t + 1) * 128],
                                              wout_sb[:, kc, hh * 512:(hh + 1) * 512],
                                              start=(kc == 0), stop=(kc == 7))
                        t2.inc(m)
                    blk.tensor(_tw)
                    twms[t] = t2.n

                    def _sep(s, c=c, t=t, gt=gt):
                        s.wait_ge(xld2[c % 2].h, xld2_mk[c])
                        if gt > 1:
                            s.wait_ge(g2.h, gusA[gt - 2])
                        return s.activation(t1b2[:, gt % 2, :], x_sb[:, c % 2, t, :], AF.Identity,
                                            bias=nmurs1[:, gt:gt + 1], scale=rs1[:, gt:gt + 1])
                    st1 = sq2.add(_sep)

                    def _gep(gp, gt=gt, st1=st1, t=t):
                        gp.wait_ge(s2.h, st1)
                        if gt > 1:
                            gp.wait_ge(v2.h, r_done[gt - 2])
                        g2.inc(gp.tensor_mul(out=u_b[:], in0=t1b2[:, gt % 2, :], in1=A_b[:]))
                        gp.wait_ge(g2.h, g2.n)
                        gus[t] = gusA[gt] = g2.inc(gp.tensor_add(out=u2_b[:, gt % 2, :],
                                                                 in0=u_b[:], in1=B_b[:]))
                    blk.gpsimd(_gep)

                    def _vep(v, c=c, t=t, gt=gt):
                        v.wait_ge(g2.h, gus[t])
                        r_done[gt] = v2.inc(v.tensor_add(out=r_b[:], in0=u2_b[:, gt % 2, :],
                                                         in1=x_sb[:, c % 2, t, :]))
                        v.wait_ge(t2.h, twms[t])
                        v.wait_ge(v2.h, v2.n)
                        if c >= 1 and t == 0:
                            v.wait_ge(x2st.h, 16 * c)
                        wo_free[gt] = v2.inc(v.tensor_add(out=x2_sb[:, t, :], in0=r_b[:],
                                                          in1=WO_ps[:]))
                        v.wait_ge(v2.h, v2.n)
                        v2.inc(v.bn_stats(out=bn3[:, t, 0:6], in_=x2_sb[:, t, 0:512]))
                        vx2[t] = v2.inc(v.bn_stats(out=bn3[:, t, 6:12], in_=x2_sb[:, t, 512:1024]))
                    blk.vector(_vep)

                    if t == 2 and c + 1 < NCH:
                        em_td(c + 1)
                        em_vd(c + 1)

                def _vagg3(v, c=c):
                    v.wait_ge(v2.h, vx2[TPC - 1])
                    if c >= 1:
                        v.wait_ge(s2.h, rs3q_mark[c - 1])
                    for t in range(TPC):
                        v2.inc(v.bn_aggr(out=mv3[:, t, :], in_=bn3[:, t, :]))
                    agg3_mk[c] = v2.n
                blk.vector(_vagg3)

                def _srs3(s, c=c):
                    s.wait_ge(v2.h, agg3_mk[c])
                    for t in range(TPC):
                        m = s.activation(mv3[:, t, 1:2], mv3[:, t, 1:2],
                                         AF.Ln, bias=epsc[:, 0:1])
                    return m
                ln3q_mk = sq2.add(_srs3, inc=4)

                def _se3(s, c=c, _lm=ln3q_mk):
                    s.wait_ge(s2.h, _lm)
                    for t in range(TPC):
                        gt = c * TPC + t
                        m = s.activation(rs3[:, gt:gt + 1], mv3[:, t, 1:2],
                                         AF.Exp, scale=-0.5)
                    return m
                rs3q_mark[c] = sq2.add(_se3, inc=4)

                def _vnm3(v, c=c):
                    v.wait_ge(s2.h, rs3q_mark[c])
                    for t in range(TPC):
                        gt = c * TPC + t
                        v2.inc(v.tensor_scalar(out=nmurs3[:, gt:gt + 1], in0=mv3[:, t, 0:1],
                                               scalar1=rs3[:, gt:gt + 1], scalar2=-1.0,
                                               op0=ALU.mult, op1=ALU.mult))
                    nm3_mark[c] = v2.n
                blk.vector(_vnm3)

                for t in range(TPC):
                    gt = c * TPC + t

                    def _sh2(s, c=c, t=t, gt=gt):
                        s.wait_ge(v2.h, nm3_mark[c])
                        if gt > 1:
                            s.wait_ge(v2.h, vh2mA[gt - 2])
                        return s.activation(t3b2[:, gt % 2, :], x2_sb[:, t, :], AF.Identity,
                                            bias=nmurs3[:, gt:gt + 1], scale=rs3[:, gt:gt + 1])
                    t3mks[t] = sq2.add(_sh2)

                    def _vh2(v, c=c, t=t, gt=gt):
                        if t > 0:
                            v.wait_ge(v2.h, v2.n)
                        v.wait_ge(s2.h, t3mks[t])
                        vh2m[t] = vh2mA[gt] = v2.inc(v.tensor_mul(out=h2A[:], in0=t3b2[:, gt % 2, :],
                                                                  in1=cb1_b[:]))
                        v.wait_ge(v2.h, v2.n)
                        if c >= 1 and t == 0:
                            v.wait_ge(h2st.h, 16 * c)
                        v2.inc(v.tensor_add(out=h2_t[:, t, :], in0=h2A[:], in1=sbb_b[:]))
                    blk.vector(_vh2)

                sq2.flush(blk)

                x_free_v[c] = v2.n
                x_free_s[c] = s2.n
                x_free_g[c] = g2.n
                vln = v2.n
                gln = g2.n

                def _gspill(sc, c=c, vln=vln, gln=gln):
                    sc.wait_ge(v2.h, vln)
                    sc.wait_ge(x2st.h, x2st.n)
                    x2st.inc(sc.dma_start(out=r3(x2_scr[tok(c)]), in_=x2_sb[:]), 16)
                    sc.wait_ge(h2st.h, h2st.n)
                    h2st.inc(sc.dma_start(out=r3(h2_scr[tok(c)]), in_=h2_t[:]), 16)
                blk.scalar(_gspill)

            def _gfin(sc):
                x2st.wait(sc)
                h2st.wait(sc)
            blk.scalar(_gfin)

    es_wo.close()

    with ExitStack() as es3:
        sb3 = lambda n, s, dt: es3.enter_context(nc.sbuf_tensor(n, s, dt))
        ps3 = lambda n, s: es3.enter_context(nc.psum_tensor(n, s, F32))
        h2f = sb3("h2f", [128, 2, 8, CT], BF16)
        w13_sb = sb3("w13_sb", [128, 2, 8, D], BF16)
        w2_sb = sb3("w2_sb", [128, 2, 2, 4, D], BF16)
        act_sl = sb3("act_sl", [128, NSLOT, 4, CT], BF16)
        gbuf = sb3("gbuf", [128, 4, CT], BF16)
        cE = sb3("cE", [128, 2, CT], BF16)
        sc_all = sb3("sc_all", [128, TPC * NE], F32)
        bb_all = sb3("bb_all", [128, TPC * NE], F32)
        oh_all = sb3("oh_all", [128, TPC * NE], F32)
        pk_t = sb3("pk_t", [128, TPC * NE], F32)
        sel_pk = sb3("sel_pk", [128, TPC * NE], F32)
        mx_s = sb3("mx_s", [128, TPC], F32)
        ws_s = sb3("ws_s", [128, TPC], F32)
        comb_bf = sb3("comb_bf", [128, TPC * NE], BF16)
        comb_f = sb3("comb_f", [NH, CT], BF16)
        x2_sb3 = sb3("x2_sb3", [128, 2, TPC, D], F32)
        yf_b = sb3("yf_b", [128, TPC, D], F32)
        ob_b = sb3("ob_b", [128, TPC, D], F32)
        I1_ps = ps3("I1_ps", [128, 2 * 512])
        Y_ps = ps3("Y_ps", [128, 2 * D])
        CE_ps = ps3("CE_ps", [128, 2 * 512])

        h2l = [S("h2la"), S("h2lb")]
        w13l = [S("w13l0"), S("w13l1")]
        w2l = {(p, q): S(f"w2l{p}{q}") for p in range(2) for q in range(2)}
        x2l = [S("x2la"), S("x2lb")]
        h2l_mk = [0] * (NCH + 2)
        x2l_mk = [0] * (NCH + 2)
        w13_mk = {}
        w2_mk = {}
        v3 = S("v3")
        s3 = S("s3")
        g3 = S("g3")
        t3 = S("t3")
        outst = S("outst")
        i1done = {}
        tvec = {}
        actv = {}
        ymm = {}
        yev = {}
        ceev = {}
        ce_mm = {}
        gate_done = [0] * NCH
        gate_sig = [0] * NCH
        comb_done = [0] * NCH
        ct_free = [0] * NCH
        h2f_free = [0] * NCH
        x2_free = [0] * NCH
        vzmul = [0] * NCH
        w2slot_user = {}
        w13_user = {}

        eidx = lambda c, e: c * NEX + e

        with nc.Block("p2b") as blk:
            def _sy00(sy):
                wcast.wait(sy)
                h2st.wait(sy)
                for kc in range(8):
                    h2l[0].inc(sy.dma_start_transpose(out=h2f[:, 0, kc, :],
                                                      in_=h2_scr[tok(0), kc * 128:(kc + 1) * 128]), 16)
                h2l_mk[0] = h2l[0].n
            blk.sync(_sy00)

            def _gx0(gp):
                x2st.wait(gp)
                x2l_mk[0] = x2l[0].inc(gp.dma_start(out=x2_sb3[:, 0], in_=r3(x2_scr[tok(0)])), 16)
            blk.gpsimd(_gx0)

            tct_mm = [0] * NCH
            vct_mark = [0] * NCH

            def em_gate(c):
                def _tg(tn, c=c):
                    tn.wait_ge(h2l[c % 2].h, h2l_mk[c])
                    if c >= 1:
                        tn.wait_ge(v3.h, vct_mark[c - 1])
                        tn.wait_ge(s3.h, gate_sig[c - 1])
                    for t in range(TPC):
                        for kc in range(8):
                            m = tn.matmul(CE_ps[:, 512 + t * NE:512 + (t + 1) * NE],
                                          h2f[:, c % 2, kc, t * 128:(t + 1) * 128],
                                          gatew_sb[:, kc, :],
                                          start=(kc == 0), stop=(kc == 7))
                    gate_done[c] = t3.inc(m)
                blk.tensor(_tg)

            def em_sig(c, sq):
                def _sg(s, c=c):
                    s.wait_ge(t3.h, gate_done[c])
                    if c >= 1:
                        s.wait_ge(v3.h, comb_done[c - 1])
                    return s.activation(sc_all[:], CE_ps[:, 512:512 + TPC * NE], AF.Sigmoid)
                gate_sig[c] = sq.add(_sg)

            def em_topk(c, phase):
                def _v(v, c=c, phase=phase):
                    if phase == 0:
                        v.wait_ge(s3.h, gate_sig[c])
                        for t in range(TPC):
                            v3.inc(v.tensor_add(out=bb_all[:, t * NE:(t + 1) * NE],
                                                in0=sc_all[:, t * NE:(t + 1) * NE], in1=gateb_b[:]))
                        v.wait_ge(v3.h, v3.n)
                        its = [0]
                    elif phase == 1:
                        its = [1, 2]
                    else:
                        its = [3]
                    for it4 in its:
                        for t in range(TPC):
                            v3.inc(v.reduce_max(mx_s[:, t:t + 1], bb_all[:, t * NE:(t + 1) * NE],
                                                axis=AX.X))
                        v.wait_ge(v3.h, v3.n)
                        for t in range(TPC):
                            v3.inc(v.tensor_scalar(out=oh_all[:, t * NE:(t + 1) * NE],
                                                   in0=bb_all[:, t * NE:(t + 1) * NE],
                                                   scalar1=mx_s[:, t:t + 1], scalar2=0.0,
                                                   op0=ALU.subtract, op1=ALU.is_ge))
                        v.wait_ge(v3.h, v3.n)
                        for t in range(TPC):
                            sl = slice(t * NE, (t + 1) * NE)
                            if it4 == 0:
                                v3.inc(v.tensor_mul(out=sel_pk[:, sl], in0=oh_all[:, sl],
                                                    in1=sc_all[:, sl]))
                            else:
                                v3.inc(v.tensor_mul(out=pk_t[:, sl], in0=oh_all[:, sl],
                                                    in1=sc_all[:, sl]))
                        if it4 < 3:
                            for t in range(TPC):
                                sl = slice(t * NE, (t + 1) * NE)
                                v3.inc(v.scalar_tensor_tensor(out=bb_all[:, sl], in0=oh_all[:, sl],
                                                              scalar=-1e30, in1=bb_all[:, sl],
                                                              op0=ALU.mult, op1=ALU.add))
                        v.wait_ge(v3.h, v3.n)
                        if it4 > 0:
                            for t in range(TPC):
                                sl = slice(t * NE, (t + 1) * NE)
                                v3.inc(v.tensor_add(out=sel_pk[:, sl], in0=sel_pk[:, sl],
                                                    in1=pk_t[:, sl]))
                            v.wait_ge(v3.h, v3.n)
                    if phase == 2:
                        for t in range(TPC):
                            v3.inc(v.reduce_sum(ws_s[:, t:t + 1], sel_pk[:, t * NE:(t + 1) * NE],
                                                axis=AX.X))
                        v.wait_ge(v3.h, v3.n)
                        v3.inc(v.reciprocal(ws_s[:], ws_s[:]))
                        v.wait_ge(v3.h, v3.n)
                        if c >= 1:
                            v.wait_ge(t3.h, tct_mm[c - 1])
                        for t in range(TPC):
                            comb_done[c] = v3.inc(v.tensor_scalar(out=comb_bf[:, t * NE:(t + 1) * NE],
                                                                  in0=sel_pk[:, t * NE:(t + 1) * NE],
                                                                  scalar1=ws_s[:, t:t + 1],
                                                                  scalar2=ROUTE_SCALE,
                                                                  op0=ALU.mult, op1=ALU.mult))
                blk.vector(_v)

            def em_tct(c):
                def _t(tn, c=c):
                    tn.wait_ge(v3.h, comb_done[c])
                    for t in range(TPC):
                        m = tn.matmul(CE_ps[0:NH, 512 + t * 128:512 + (t + 1) * 128],
                                      comb_bf[:, t * NE:(t + 1) * NE], ident[:],
                                      start=True, stop=True)
                    tct_mm[c] = t3.inc(m)
                blk.tensor(_t)

            def em_vct(c):
                def _v(v, c=c):
                    v.wait_ge(t3.h, tct_mm[c])
                    if c >= 1:
                        v.wait_ge(t3.h, ce_mm.get(eidx(c - 1, NE - 1), 0))
                    vct_mark[c] = v3.inc(v.tensor_copy(out=comb_f[:], in_=CE_ps[0:NH, 512:1024]))
                blk.vector(_v)

            for c in range(NCH):
                par = c % 2

                def _gxp(gp, c=c):
                    if c + 1 < NCH:
                        if c >= 1:
                            gp.wait_ge(v3.h, x2_free[c - 1])
                        gp.wait_ge(x2l[(c + 1) % 2].h, x2l[(c + 1) % 2].n)
                        x2l_mk[c + 1] = x2l[(c + 1) % 2].inc(
                            gp.dma_start(out=x2_sb3[:, (c + 1) % 2], in_=r3(x2_scr[tok(c + 1)])), 16)
                blk.gpsimd(_gxp)

                sq3 = SQueue(s3)

                if c == 0:
                    em_gate(0)
                    em_sig(0, sq3)
                    em_topk(0, 0)
                    em_topk(0, 1)
                    em_topk(0, 2)
                    em_tct(0)
                    em_vct(0)

                for gi, grp in enumerate(EGROUPS):
                    for e in grp:
                        ei = eidx(c, e)
                        slot = ei % NSLOT
                        pos = grp.index(e)
                        _mc = {}

                        def _syw(sy, c=c, e=e, ei=ei, gi=gi, pos=pos):
                            s13 = ei % 2
                            if s13 in w13_user:
                                sy.wait_ge(t3.h, i1done[w13_user[s13]])
                            w13_user[s13] = ei
                            if c == 0 and e < 2:
                                wcast.wait(sy)
                            sy.wait_ge(w13l[s13].h, w13l[s13].n)
                            w13_mk[ei] = w13l[s13].inc(
                                sy.dma_start(out=w13_sb[:, s13],
                                             in_=w13bf[e].rearrange("(k p) n -> p k n", p=128)), 16)
                            key = (gi % 2, pos)
                            if key in w2slot_user:
                                pc, pgi = w2slot_user[key]
                                sy.wait_ge(t3.h, ymm.get((pc, pgi, TPC - 1), 0))
                            w2slot_user[key] = (c, gi)
                            sy.wait_ge(w2l[key].h, w2l[key].n)
                            w2_mk[ei] = w2l[key].inc(
                                sy.dma_start(out=w2_sb[:, gi % 2, pos],
                                             in_=w2bf[e].rearrange("(k p) n -> p k n", p=128)), 16)
                        blk.sync(_syw)

                        if e < NE:
                            def _tce(tn, c=c, e=e, ei=ei):
                                tn.wait_ge(v3.h, vct_mark[c])
                                if e >= 1:
                                    tn.wait_ge(v3.h, ceev.get(eidx(c, e - 1), 0))
                                elif c >= 1:
                                    tn.wait_ge(v3.h, ceev.get(eidx(c - 1, NE - 1), 0))
                                m = tn.matmul(CE_ps[:, 0:512],
                                              erow[:, e * 128:(e + 1) * 128], comb_f[:],
                                              start=True, stop=True)
                                ce_mm[ei] = t3.inc(m)
                                _mc["ce"] = ce_mm[ei]
                            blk.tensor(_tce)

                            def _vce(v, e=e, ei=ei):
                                v.wait_ge(t3.h, _mc["ce"])
                                ceev[ei] = v3.inc(v.tensor_copy(out=cE[:, e % 2, :],
                                                                in_=CE_ps[:, 0:512]))
                            blk.vector(_vce)

                        s3_base = s3.n
                        v3_base = v3.n

                        def _tm1(tn, c=c, e=e, ei=ei, s3_base=s3_base, v3_base=v3_base):
                            tn.wait_ge(w13l[ei % 2].h, w13_mk[ei])
                            tn.wait_ge(h2l[c % 2].h, h2l_mk[c])
                            if ei >= 1:
                                tn.wait_ge(s3.h, s3_base)
                                tn.wait_ge(v3.h, tvec.get(ei - 1, 0))
                            for oc in range(8):
                                if oc in (2, 3, 4, 5):
                                    tn.wait_ge(s3.h, s3_base + (oc - 1))
                                elif oc in (6, 7):
                                    tn.wait_ge(v3.h, v3_base + (oc - 5))
                                for kc in range(8):
                                    m = tn.matmul(I1_ps[:, (oc % 2) * 512:(oc % 2 + 1) * 512],
                                                  w13_sb[:, ei % 2, kc, oc * 128:(oc + 1) * 128],
                                                  h2f[:, par, kc, :],
                                                  start=(kc == 0), stop=(kc == 7))
                                _mc[("mm", oc)] = t3.inc(m)
                        blk.tensor(_tm1)
                        i1done[ei] = t3.n
                        h2f_free[c] = t3.n

                        for j in range(4):
                            def _sm1(s, ei=ei, j=j, _mc=_mc):
                                s.wait_ge(t3.h, _mc[("mm", j)])
                                if ei >= 1:
                                    s.wait_ge(v3.h, tvec.get(ei - 1, 0))
                                return s.activation(gbuf[:, j, :],
                                                    I1_ps[:, (j % 2) * 512:(j % 2 + 1) * 512],
                                                    AF.Silu)
                            sq3.add(_sm1)
                        assert s3.n == s3_base + 4

                        def _vm1(v, c=c, e=e, ei=ei, slot=slot, s3_base=s3_base):
                            if ei >= 1:
                                v.wait_ge(v3.h, actv[ei - 1])
                            if ei >= NSLOT:
                                pe = ei - NSLOT
                                pc, pee = divmod(pe, NEX)
                                v.wait_ge(t3.h, ymm.get((pc, EGROUPS_OF[pee], TPC - 1), 0))
                            for j in range(4):
                                v.wait_ge(t3.h, _mc[("mm", 4 + j)])
                                v.wait_ge(s3.h, s3_base + j + 1)
                                tvec[ei] = v3.inc(v.tensor_mul(out=act_sl[:, slot, j, :],
                                                               in0=gbuf[:, j, :],
                                                               in1=I1_ps[:, (j % 2) * 512:(j % 2 + 1) * 512]))
                            v.wait_ge(v3.h, v3.n)
                            if e < NE:
                                v.wait_ge(v3.h, ceev[ei])
                                for j in range(4):
                                    actv[ei] = v3.inc(v.tensor_mul(out=act_sl[:, slot, j, :],
                                                                   in0=act_sl[:, slot, j, :],
                                                                   in1=cE[:, e % 2, :]))
                            else:
                                actv[ei] = tvec[ei]
                        blk.vector(_vm1)
                        assert v3.n == v3_base + (1 if e < NE else 0) * 0 + 8 or True

                    if gi == 0:
                        def _syp(sy, c=c):
                            if c + 1 < NCH:
                                if c >= 1:
                                    sy.wait_ge(t3.h, h2f_free[c - 1])
                                sy.wait_ge(h2l[(c + 1) % 2].h, h2l[(c + 1) % 2].n)
                                for kc in range(8):
                                    h2l[(c + 1) % 2].inc(
                                        sy.dma_start_transpose(out=h2f[:, (c + 1) % 2, kc, :],
                                                               in_=h2_scr[tok(c + 1), kc * 128:(kc + 1) * 128]), 16)
                                h2l_mk[c + 1] = h2l[(c + 1) % 2].n
                        blk.sync(_syp)

                    if gi == len(EGROUPS) - 1 and c + 1 < NCH:
                        em_tct(c + 1)
                        em_vct(c + 1)

                    for t in range(TPC):
                        def _tm2(tn, c=c, gi=gi, grp=grp, t=t):
                            for e2 in grp:
                                tn.wait_ge(v3.h, actv[eidx(c, e2)])
                                tn.wait_ge(w2l[(gi % 2, grp.index(e2))].h, w2_mk[eidx(c, e2)])
                            prev = None
                            if t >= 2:
                                prev = (c, gi, t - 2)
                            elif gi >= 1:
                                prev = (c, gi - 1, t)
                            elif c >= 1:
                                prev = (c - 1, len(EGROUPS) - 1, t)
                            if prev is not None and prev in yev:
                                tn.wait_ge(v3.h, yev[prev])
                            for e2 in grp:
                                slot2 = eidx(c, e2) % NSLOT
                                for kc in range(4):
                                    for hh in range(2):
                                        m = tn.matmul(
                                            Y_ps[:, (t % 2) * D + hh * 512:(t % 2) * D + (hh + 1) * 512],
                                            act_sl[:, slot2, kc, t * 128:(t + 1) * 128],
                                            w2_sb[:, gi % 2, grp.index(e2), kc, hh * 512:(hh + 1) * 512],
                                            start=(e2 == grp[0] and kc == 0),
                                            stop=(e2 == grp[-1] and kc == 3))
                            ymm[(c, gi, t)] = t3.inc(m)
                        blk.tensor(_tm2)

                        def _vy(v, c=c, gi=gi, t=t):
                            v.wait_ge(t3.h, ymm[(c, gi, t)])
                            if gi == 0:
                                yev[(c, gi, t)] = v3.inc(v.tensor_copy(out=yf_b[:, t, :],
                                                                       in_=Y_ps[:, (t % 2) * D:(t % 2 + 1) * D]))
                            else:
                                yev[(c, gi, t)] = v3.inc(v.tensor_add(out=yf_b[:, t, :],
                                                                      in0=yf_b[:, t, :],
                                                                      in1=Y_ps[:, (t % 2) * D:(t % 2 + 1) * D]))
                        blk.vector(_vy)

                    if c + 1 < NCH:
                        if gi == 0:
                            em_gate(c + 1)
                            em_sig(c + 1, sq3)
                        elif gi == 2:
                            em_topk(c + 1, 0)
                        elif gi == 4:
                            em_topk(c + 1, 1)
                        elif gi == 6:
                            em_topk(c + 1, 2)

                sq3.flush(blk)

                def _vz(v, c=c, par=par):
                    v.wait_ge(v3.h, yev[(c, len(EGROUPS) - 1, TPC - 1)])
                    if c >= 1:
                        v.wait_ge(outst.h, 16 * c)
                    for t in range(TPC):
                        v3.inc(v.tensor_mul(out=ob_b[:, t, :], in0=yf_b[:, t, :], in1=gb_b[:]))
                    vzmul[c] = v3.n
                    v.wait_ge(v3.h, v3.n)
                    v.wait_ge(x2l[c % 2].h, x2l_mk[c])
                    for t in range(TPC):
                        v3.inc(v.tensor_add(out=ob_b[:, t, :], in0=ob_b[:, t, :],
                                            in1=x2_sb3[:, par, t, :]))
                    x2_free[c] = v3.n
                blk.vector(_vz)
                vzm = v3.n

                def _gout(gp, c=c, vzm=vzm):
                    gp.wait_ge(v3.h, vzm)
                    gp.wait_ge(outst.h, outst.n)
                    outst.inc(gp.dma_start(out=r3(out[tok(c)]), in_=ob_b[:]), 16)
                blk.gpsimd(_gout)

            def _gend(gp):
                outst.wait(gp)
            blk.gpsimd(_gend)

    es.close()
    return nc


def _host_prep(inputs):
    f32 = np.float32
    asn = np.ascontiguousarray
    w13 = np.concatenate([np.asarray(inputs["L0_ffn_w13"], f32),
                          np.asarray(inputs["L0_ffn_w13_s"], f32)[None]], axis=0)
    w2 = np.concatenate([np.asarray(inputs["L0_ffn_w2"], f32),
                         np.asarray(inputs["L0_ffn_w2_s"], f32)[None]], axis=0)
    w13T = asn(np.transpose(w13, (0, 2, 1)))
    w2T = asn(np.transpose(w2, (0, 2, 1)))
    gatewT = asn(np.asarray(inputs["L0_ffn_gate_w"], f32).T)
    ident = np.eye(128, dtype=ml_dtypes.bfloat16)
    onesrow = np.ones((1, 128), dtype=ml_dtypes.bfloat16)
    eexp = np.zeros((NH, NPAIR * 128), dtype=ml_dtypes.bfloat16)
    for p in range(NPAIR):
        for j in range(128):
            eexp[2 * p + j // 64, p * 128 + j] = 1
    erow = np.zeros((NH, NE * 128), dtype=ml_dtypes.bfloat16)
    for e in range(NE):
        erow[e, e * 128:(e + 1) * 128] = 1
    return {
        "wmod_in": asn(np.asarray(inputs["L0_wmod"], f32)),
        "gamma_in": asn(np.asarray(inputs["L0_attn_gamma"], f32).reshape(1, D)),
        "beta_in": asn(np.asarray(inputs["L0_attn_beta"], f32).reshape(1, D)),
        "wqkv_in": asn(np.asarray(inputs["L0_attn_Wqkv"], f32)),
        "wout_in": asn(np.asarray(inputs["L0_attn_Wout"], f32)),
        "gatewT_in": gatewT,
        "gateb_in": asn(np.asarray(inputs["L0_ffn_gate_b"], f32).reshape(1, NE)),
        "w13T_in": w13T,
        "w2T_in": w2T,
        "ident_in": ident,
        "onesrow_in": onesrow,
        "eexp_in": eexp,
        "erow_in": erow,
    }


def kernel(**inputs):
    x = np.asarray(inputs["L0_attn_x"], np.float32)
    cmat = np.asarray(inputs["L0_c"], np.float32)
    B, L, _ = x.shape
    com = _host_prep(inputs)
    nc = build_program(L)
    in_maps = []
    for b in range(B):
        m = dict(com)
        m["x_in"] = np.ascontiguousarray(x[b])
        m["c_row"] = np.ascontiguousarray(cmat[b:b + 1])
        in_maps.append(m)
    res = run_bass_kernel_spmd(nc, in_maps, core_ids=list(range(B)))
    return np.stack([res.results[b]["out"] for b in range(B)], axis=0)


# version 31 (final)
# speedup vs baseline: 1.1644x; 1.0329x over previous
import numpy as np
import ml_dtypes
from contextlib import ExitStack

import concourse.bass as bass
import concourse.mybir as mybir
from concourse.bass_utils import run_bass_kernel_spmd

F32 = mybir.dt.float32
BF16 = mybir.dt.bfloat16
AF = mybir.ActivationFunctionType
ALU = mybir.AluOpType
AX = mybir.AxisListType

D = 1024
DQKV = 3 * D
NH = 16
NPAIR = 8
NE = 16
NEX = 17
INTER = 512
ROUTE_SCALE = 2.5
LN_EPS = 1e-5
DEN_EPS = 1e-6
CT = 512
TPC = 4
RCPD = float(1.0 / D)
RSQD = float((1.0 / D) ** 0.5)
EGROUPS = [(0, 1), (2, 3), (4, 5), (6, 7), (8, 9), (10, 11), (12, 13), (14, 15), (16,)]
EGROUPS_OF = {e: gi for gi, grp in enumerate(EGROUPS) for e in grp}
NSLOT = 3


def _preload_tab(nc, s, idx):
    s.add_instruction(mybir.InstLoadActFuncSet(
        name=nc.get_next_instruction_name(), act_func_set_id=idx, ins=[], outs=[]))


class Sem:
    def __init__(self, es, nc, name):
        self.h = es.enter_context(nc.semaphore(name))
        self.n = 0

    def inc(self, inst, k=1):
        inst.then_inc(self.h, k)
        self.n += k
        return self.n

    def wait(self, eng, n=None):
        eng.wait_ge(self.h, self.n if n is None else n)


class SQueue:

    def __init__(self, sem):
        self.sem = sem
        self.q = []

    def add(self, emitfn, inc=1):
        mark = None
        if inc:
            self.sem.n += inc
            mark = self.sem.n
        self.q.append((emitfn, mark, inc))
        return mark

    def flush(self, blk):
        q, self.q = self.q, []
        if not q:
            return

        def body(s):
            for fn, mark, inc in q:
                inst = fn(s)
                if mark is not None:
                    inst.then_inc(self.sem.h, inc)
        blk.scalar(body)


def build_program(L):
    assert L % CT == 0
    NCH = L // CT
    NT = L // 128
    nc = bass.Bass("TRN2", target_bir_lowering=False)

    dr = lambda n, s, dt, k="Internal": nc.dram_tensor(n, s, dt, kind=k).ap()
    x_in = dr("x_in", [L, D], F32, "ExternalInput")
    c_row = dr("c_row", [1, D], F32, "ExternalInput")
    wmod_in = dr("wmod_in", [D, 6 * D], F32, "ExternalInput")
    gamma_in = dr("gamma_in", [1, D], F32, "ExternalInput")
    beta_in = dr("beta_in", [1, D], F32, "ExternalInput")
    wqkv_in = dr("wqkv_in", [D, DQKV], F32, "ExternalInput")
    wout_in = dr("wout_in", [D, D], F32, "ExternalInput")
    gatewT_in = dr("gatewT_in", [D, NE], F32, "ExternalInput")
    gateb_in = dr("gateb_in", [1, NE], F32, "ExternalInput")
    w13T_in = dr("w13T_in", [NEX, D, D], F32, "ExternalInput")
    w2T_in = dr("w2T_in", [NEX, INTER, D], F32, "ExternalInput")
    ident_in = dr("ident_in", [128, 128], BF16, "ExternalInput")
    onesrow_in = dr("onesrow_in", [1, 128], BF16, "ExternalInput")
    eexp_in = dr("eexp_in", [NH, NPAIR * 128], BF16, "ExternalInput")
    erow_in = dr("erow_in", [NH, NE * 128], BF16, "ExternalInput")
    out = dr("out", [L, D], F32, "ExternalOutput")
    w13bf = dr("w13bf", [NEX, D, D], BF16)
    w2bf = dr("w2bf", [NEX, INTER, D], BF16)
    xn_scr = dr("xn_scr", [L, D], BF16)
    q_scr = dr("q_scr", [L, D], BF16)
    x2_scr = dr("x2_scr", [L, D], F32)
    h2_scr = dr("h2_scr", [L, D], BF16)

    es = ExitStack()
    sbuf = lambda n, s, dt: es.enter_context(nc.sbuf_tensor(n, s, dt))
    S = lambda n: Sem(es, nc, n)
    r3 = lambda ap: ap.rearrange("(t p) d -> p t d", p=128)
    tok = lambda c: slice(c * CT, (c + 1) * CT)

    ca1_b = sbuf("ca1_b", [128, D], BF16)
    sa_b = sbuf("sa_b", [128, D], BF16)
    bw_a = sbuf("bw_a", [96, 512], BF16)
    ones96 = sbuf("ones96", [96, 128], BF16)
    bw_b = sbuf("bw_b", [96, 512], BF16)
    A_b = sbuf("A_b", [128, D], BF16)
    B_b = sbuf("B_b", [128, D], BF16)
    cb1_b = sbuf("cb1_b", [128, D], BF16)
    sbb_b = sbuf("sbb_b", [128, D], BF16)
    ga_b = sbuf("ga_b", [128, D], F32)
    gb_b = sbuf("gb_b", [128, D], F32)
    gateb_b = sbuf("gateb_b", [128, NE], F32)
    ident = sbuf("ident", [128, 128], BF16)
    onesrow = sbuf("onesrow", [1, 128], BF16)
    onescol = sbuf("onescol", [128, 1], BF16)
    eexp = sbuf("eexp", [NH, NPAIR * 128], BF16)
    erow = sbuf("erow", [NH, NE * 128], BF16)
    rs1 = sbuf("rs1", [128, NT], F32)
    nmurs1 = sbuf("nmurs1", [128, NT], F32)
    rs3 = sbuf("rs3", [128, NT], F32)
    nmurs3 = sbuf("nmurs3", [128, NT], F32)
    ksum_sb = sbuf("ksum_sb", [128, NPAIR], F32)
    kv_sb = sbuf("kv_sb", [128, NPAIR * 128], BF16)
    ksbd = sbuf("ksbd", [128, NPAIR * NH], BF16)
    junk4 = sbuf("junk4", [128, TPC, D], BF16)
    epsc = sbuf("epsc", [128, 1], F32)
    gatew_sb = sbuf("gatew_sb", [128, 8, NE], BF16)
    es_wo = ExitStack()
    wout_sb = es_wo.enter_context(nc.sbuf_tensor("wout_sb", [128, 8, D], BF16))
    es_wq = ExitStack()
    wqkv_sb = es_wq.enter_context(nc.sbuf_tensor("wqkv_sb", [128, 8, DQKV], BF16))

    wcast = S("wcast")

    with ExitStack() as es0:
        sb0 = lambda n, s, dt: es0.enter_context(nc.sbuf_tensor(n, s, dt))
        ps0 = lambda n, s: es0.enter_context(nc.psum_tensor(n, s, F32))
        c_sb = sb0("c_sb", [1, D], F32)
        sc_bf = sb0("sc_bf", [1, D], BF16)
        sc_f = sb0("sc_f", [128, 8], BF16)
        mod_bf = sb0("mod_bf", [1, 6 * D], BF16)
        gb_row = sb0("gb_row", [1, NE], BF16)
        wmod_bf = sb0("wmod_bf", [128, 8, 3 * D], BF16)
        sgn_t = sb0("sgn_t", [128, D], F32)
        amx_t = sb0("amx_t", [128, D], F32)
        gammaT = sb0("gammaT", [128, 8], F32)
        betaT = sb0("betaT", [128, 8], BF16)
        mod_ps = ps0("mod_ps", [1, 3 * D])
        bc_ps = ps0("bc_ps", [128, D])
        bw_ps = mod_ps[0:1, 0:512]
        ch = S("p0ch")
        chd = S("p0chd")
        chs = S("p0chs")

        with nc.Block("p0") as blk:
            @blk.gpsimd
            def _(gp):
                chd.inc(gp.dma_start(out=wqkv_sb[:], in_=wqkv_in.rearrange("(k p) n -> p k n", p=128)), 16)
                chd.inc(gp.dma_start(out=wout_sb[:], in_=wout_in.rearrange("(k p) n -> p k n", p=128)), 16)
                chd.inc(gp.dma_start(out=gatew_sb[:], in_=gatewT_in.rearrange("(k p) n -> p k n", p=128)), 16)
                chd.inc(gp.dma_start(out=gb_row[:], in_=gateb_in), 16)
                bTv = beta_in.rearrange("o (kc p) -> kc p o", kc=8)
                for kc in range(8):
                    chd.inc(gp.dma_start(out=betaT[:, kc:kc + 1], in_=bTv[kc]), 16)

            @blk.sync
            def _(sy):
                chs.inc(sy.dma_start(out=c_sb[:], in_=c_row), 16)
                gTv = gamma_in.rearrange("o (kc p) -> kc p o", kc=8)
                for kc in range(8):
                    chs.inc(sy.dma_start(out=gammaT[:, kc:kc + 1], in_=gTv[kc]), 16)
                chs.inc(sy.dma_start(out=ident[:], in_=ident_in), 16)
                chs.inc(sy.dma_start(out=onesrow[:], in_=onesrow_in), 16)
                chs.inc(sy.dma_start(out=eexp[:], in_=eexp_in), 16)
                chs.inc(sy.dma_start(out=erow[:], in_=erow_in), 16)

            def step(eng_attr, fn):
                def body(e):
                    ch.wait(e)
                    chd.wait(e)
                    chs.wait(e)
                    fn(e)
                getattr(blk, eng_attr)(body)

            step("scalar", lambda e: ch.inc(e.activation(c_sb[:], c_sb[:], AF.Silu)))

            def _v0(v):
                ch.inc(v.memset(epsc[:], LN_EPS))
                ch.wait(v)
                ch.inc(v.memset(onescol[:], 1.0))
                ch.wait(v)
                ch.inc(v.memset(ones96[:], 1.0))
                ch.wait(v)
                ch.inc(v.memset(ksum_sb[:], 0.0))
                ch.wait(v)
                ch.inc(v.tensor_copy(out=sc_bf[:], in_=c_sb[:]))
            step("vector", _v0)

            def _t0(tn):
                for kc in range(8):
                    m = tn.matmul(bc_ps[:, kc:kc + 1], sc_bf[0:1, kc * 128:(kc + 1) * 128],
                                  onescol[0:1, 0:1], start=(kc == 0), stop=(kc == 7))
                ch.inc(m)
            step("tensor", _t0)
            step("vector", lambda v: ch.inc(v.tensor_copy(out=sc_f[:], in_=bc_ps[:, 0:8])))

            for half in range(2):
                def _gw(gp, half=half):
                    chd.inc(gp.dma_start(out=wmod_bf[:],
                                        in_=wmod_in[:, half * 3072:(half + 1) * 3072]
                                        .rearrange("(k p) n -> p k n", p=128)), 16)
                step("gpsimd", _gw)

                def _tm(tn, half=half):
                    for ns in range(6):
                        for kc in range(8):
                            m = tn.matmul(mod_ps[0:1, ns * 512:(ns + 1) * 512],
                                          sc_f[:, kc:kc + 1],
                                          wmod_bf[:, kc, ns * 512:(ns + 1) * 512],
                                          start=(kc == 0), stop=(kc == 7))
                    ch.inc(m)
                step("tensor", _tm)
                step("vector", lambda v, half=half: ch.inc(
                    v.tensor_copy(out=mod_bf[0:1, half * 3072:(half + 1) * 3072], in_=mod_ps[:])))

            items = [("sa", mod_bf[0:1, 0:D]), ("ca", mod_bf[0:1, D:2 * D]),
                     ("ga", mod_bf[0:1, 2 * D:3 * D]), ("sb", mod_bf[0:1, 3 * D:4 * D]),
                     ("cb", mod_bf[0:1, 4 * D:5 * D]), ("gb", mod_bf[0:1, 5 * D:6 * D]),
                     ("gateb", gb_row[0:1, 0:NE])]
            for name, src in items:
                def _tb(tn, src=src):
                    n = src.shape[-1]
                    for ns in range((n + 511) // 512):
                        w = min(512, n - ns * 512)
                        m = tn.matmul(bc_ps[:, ns * 512:ns * 512 + w], onesrow[0:1, :],
                                      src[0:1, ns * 512:ns * 512 + w], start=True, stop=True)
                    ch.inc(m)
                step("tensor", _tb)
                if name in ("ga", "gb"):
                    step("scalar", lambda s: ch.inc(s.activation(amx_t[:], bc_ps[:], AF.Abs)))

                def _vb(v, name=name):
                    if name == "sa":
                        ch.inc(v.tensor_copy(out=sa_b[:], in_=bc_ps[:]))
                    elif name == "ca":
                        ch.inc(v.tensor_scalar(out=ca1_b[:], in0=bc_ps[:], scalar1=1.0,
                                               scalar2=None, op0=ALU.add))
                    elif name in ("ga", "gb"):
                        dst = ga_b if name == "ga" else gb_b
                        ch.inc(v.tensor_scalar(out=sgn_t[:], in0=bc_ps[:], scalar1=0.0,
                                               scalar2=2.0, op0=ALU.is_ge, op1=ALU.mult))
                        ch.wait(v)
                        ch.inc(v.tensor_scalar(out=sgn_t[:], in0=sgn_t[:], scalar1=-1.0,
                                               scalar2=None, op0=ALU.add))
                        ch.wait(v)
                        ch.inc(v.tensor_scalar(out=amx_t[:], in0=amx_t[:], scalar1=0.25,
                                               scalar2=None, op0=ALU.max))
                        ch.wait(v)
                        ch.inc(v.tensor_mul(out=dst[:], in0=sgn_t[:], in1=amx_t[:]))
                        if name == "ga":
                            ch.wait(v)
                            ch.inc(v.tensor_mul(out=A_b[:], in0=ga_b[:], in1=ca1_b[:]))
                            ch.wait(v)
                            ch.inc(v.tensor_mul(out=B_b[:], in0=ga_b[:], in1=sa_b[:]))
                    elif name == "sb":
                        ch.inc(v.tensor_copy(out=sbb_b[:], in_=bc_ps[:]))
                    elif name == "cb":
                        ch.inc(v.tensor_scalar(out=cb1_b[:], in0=bc_ps[:], scalar1=1.0,
                                               scalar2=None, op0=ALU.add))
                    elif name == "gateb":
                        ch.inc(v.tensor_copy(out=gateb_b[:], in_=bc_ps[:, 0:NE]))
                step("vector", _vb)

            for seg in range(6):
                def _tbw(tn, seg=seg):
                    for kc in range(8):
                        m = tn.matmul(bw_ps[0:1, :], betaT[:, kc:kc + 1],
                                      wqkv_sb[:, kc, seg * 512:(seg + 1) * 512],
                                      start=(kc == 0), stop=(kc == 7))
                    ch.inc(m)
                step("tensor", _tbw)
                dst = (bw_a if seg < 3 else bw_b)
                row = (seg % 3) * 32
                step("scalar", lambda s, dst=dst, row=row: ch.inc(
                    s.copy(out=dst[row:row + 1, 0:512], in_=bw_ps[0:1, :])))
            for kc in range(8):
                def _sgw(s, kc=kc):
                    ch.inc(s.activation(wqkv_sb[:, kc, :], wqkv_sb[:, kc, :], AF.Identity,
                                        scale=gammaT[:, kc:kc + 1]))
                step("scalar", _sgw)
            for kc in range(8):
                def _vgw(v, kc=kc):
                    ch.inc(v.tensor_mul(out=wout_sb[:, kc, :], in0=wout_sb[:, kc, :],
                                        in1=ga_b[:]))
                step("vector", _vgw)


    with ExitStack() as es1:
        sb1 = lambda n, s, dt: es1.enter_context(nc.sbuf_tensor(n, s, dt))
        ps1 = lambda n, s: es1.enter_context(nc.psum_tensor(n, s, F32))
        x_sb = sb1("x_sb", [128, 2, TPC, D], BF16)
        xn_t = sb1("xn_t", [128, 2, TPC, D], BF16)
        xn_f = sb1("xn_f", [128, 2, 8, CT], BF16)
        q_sb = sb1("q_sb", [128, 2, TPC, D], BF16)
        k_sb = sb1("k_sb", [128, D], BF16)
        v_sb = sb1("v_sb", [128, D], BF16)
        e1k = sb1("e1k", [128, D], BF16)
        t1b = sb1("t1b", [128, 4, D], BF16)
        h_t = sb1("h_t", [128, TPC, D], BF16)
        st_s = sb1("st_s", [128, 32], F32)
        bnx = sb1("bnx", [128, 2, TPC, 12], F32)
        mvx = sb1("mvx", [128, 2, TPC, 2], F32)
        bnh = sb1("bnh", [128, TPC, 12], F32)
        mvh = sb1("mvh", [128, TPC, 2], F32)
        rsh = sb1("rsh", [128, TPC], F32)
        nmh = sb1("nmh", [128, TPC], F32)
        K_ps = ps1("K_ps", [128, D])
        V_ps = ps1("V_ps", [128, D])
        Q_ps = ps1("Q_ps", [128, D])
        KV_ps = ps1("KV_ps", [128, NPAIR * 128])

        xld = [S("xld0"), S("xld1")]
        xnst = [S("xnst0"), S("xnst1")]
        xnfl = [S("xnfl0"), S("xnfl1")]
        qspl = [S("qspl0"), S("qspl1")]
        v1 = S("v1")
        s1 = S("s1")
        g1 = S("g1")
        t1 = S("t1")
        kmm = [0] * NT; vmm = [0] * NT; qmm = [0] * NT; gmm = [0] * NT
        gsum = [0] * NT
        kfin = [0] * NT; vcp = [0] * NT; qfin = [0] * NT; ksev = [0] * NT
        ek_mk = [0] * NT; eq_mk = [0] * NT
        xnf_ok = [0] * NCH
        xld_mk = [0] * (NCH + 2)
        xnst_mk = [0] * (NCH + 2)
        qspl_mk = [0] * (NCH + 2)
        t1last = [0] * (NCH + 1)
        hmkA = [None] * (NCH + 1)
        t2last = [0] * (NCH + 1)
        t2mks_h = [0] * (NCH + 1)
        aggx_mk = [0] * (NCH + 1)
        rsx_mark = [0] * (NCH + 1)
        nmx_mark = [0] * (NCH + 1)
        aggh_mk = [0] * (NCH + 1)
        rsh_mark = [0] * (NCH + 1)
        nmh_mark = [0] * (NCH + 1)

        with nc.Block("p1") as blk:
            def _gx0(gp):
                xld_mk[0] = xld[0].inc(gp.dma_start(out=x_sb[:, 0], in_=r3(x_in[tok(0)])), 16)
                if NCH > 1:
                    xld_mk[1] = xld[1].inc(gp.dma_start(out=x_sb[:, 1], in_=r3(x_in[tok(1)])), 16)
            blk.gpsimd(_gx0)
            castq = [(w13bf[e], w13T_in[e]) for e in range(NEX)] + \
                    [(w2bf[e], w2T_in[e]) for e in range(NEX)]

            def emit_mm_kv(c, t, sq):
                g = c * TPC + t
                par = c % 2

                def _t(tn):
                    if t == 0:
                        tn.wait_ge(xnfl[par].h, xnf_ok[c])
                    if g > 0:
                        tn.wait_ge(v1.h, ksev[g - 1])
                    for kc in range(8):
                        for hh in range(2):
                            tn.matmul(K_ps[:, hh * 512:(hh + 1) * 512],
                                      xn_f[:, par, kc, t * 128:(t + 1) * 128],
                                      wqkv_sb[:, kc, D + hh * 512:D + (hh + 1) * 512],
                                      start=(kc == 0), stop=False)
                    for hh in range(2):
                        m = tn.matmul(K_ps[:, hh * 512:(hh + 1) * 512],
                                      ones96[64 * (1 - hh):64 * (1 - hh) + 1, 0:128],
                                      (bw_a[64:65, 0:512] if hh == 0 else bw_b[0:1, 0:512]),
                                      start=False, stop=True)
                    kmm[g] = t1.inc(m)
                    for kc in range(8):
                        for hh in range(2):
                            tn.matmul(V_ps[:, hh * 512:(hh + 1) * 512],
                                      xn_f[:, par, kc, t * 128:(t + 1) * 128],
                                      wqkv_sb[:, kc, 2 * D + hh * 512:2 * D + (hh + 1) * 512],
                                      start=(kc == 0), stop=False)
                    for hh in range(2):
                        m = tn.matmul(V_ps[:, hh * 512:(hh + 1) * 512],
                                      ones96[32 * (1 + hh):32 * (1 + hh) + 1, 0:128],
                                      bw_b[32 * (1 + hh):32 * (1 + hh) + 1, 0:512],
                                      start=False, stop=True)
                    vmm[g] = t1.inc(m)
                blk.tensor(_t)

                def _sk(s, g=g):
                    s.wait_ge(t1.h, kmm[g])
                    if g > 0:
                        s.wait_ge(v1.h, qfin[g - 1])
                    return s.activation(e1k[:], K_ps[:], AF.Exp)
                ek_mk[g] = sq.add(_sk)

            def emit_v_elukv(c, t):
                g = c * TPC + t

                def _v(v):
                    v.wait_ge(s1.h, ek_mk[g])
                    v1.inc(v.tensor_scalar(out=e1k[:], in0=e1k[:], scalar1=1.0,
                                           scalar2=None, op0=ALU.min))
                    v.wait_ge(v1.h, v1.n)
                    if g > 0:
                        v.wait_ge(t1.h, gmm[g - 1])
                    kfin[g] = v1.inc(v.scalar_tensor_tensor(out=k_sb[:], in0=K_ps[:], scalar=1.0,
                                                            in1=e1k[:], op0=ALU.add, op1=ALU.max))
                    v.wait_ge(t1.h, vmm[g])
                    vcp[g] = v1.inc(v.tensor_copy(out=v_sb[:], in_=V_ps[:]))
                blk.vector(_v)

            def emit_v_ksev(c, t):
                g = c * TPC + t

                def _v2(v):
                    v.wait_ge(t1.h, gsum[g])
                    ksev[g] = v1.inc(v.tensor_add(out=ksum_sb[:], in0=ksum_sb[:],
                                                  in1=K_ps[:, 0:NPAIR]))
                blk.vector(_v2)

            def emit_t_gram(c, t):
                g = c * TPC + t

                def _t2(tn):
                    tn.wait_ge(v1.h, vcp[g])
                    tn.wait_ge(v1.h, kfin[g])
                    for p in range(NPAIR):
                        m = tn.matmul(K_ps[:, p:p + 1],
                                      k_sb[:, p * 128:(p + 1) * 128], onescol[:],
                                      start=(p == 0), stop=(p == NPAIR - 1))
                    gsum[g] = t1.inc(m)
                    for p in range(NPAIR):
                        m = tn.matmul(KV_ps[:, p * 128:(p + 1) * 128],
                                      k_sb[:, p * 128:(p + 1) * 128],
                                      v_sb[:, p * 128:(p + 1) * 128],
                                      start=(g == 0 and p % 4 == 0), stop=(g == NT - 1 and p % 4 == 3))
                    gmm[g] = t1.inc(m)
                blk.tensor(_t2)

            def emit_t_q(c, t, sq):
                g = c * TPC + t
                par = c % 2

                def _t3(tn):
                    if g > 0:
                        tn.wait_ge(v1.h, qfin[g - 1])
                    for kc in range(8):
                        for hh in range(2):
                            tn.matmul(Q_ps[:, hh * 512:(hh + 1) * 512],
                                      xn_f[:, par, kc, t * 128:(t + 1) * 128],
                                      wqkv_sb[:, kc, hh * 512:(hh + 1) * 512],
                                      start=(kc == 0), stop=False)
                    for hh in range(2):
                        m = tn.matmul(Q_ps[:, hh * 512:(hh + 1) * 512],
                                      ones96[32 * hh:32 * hh + 1, 0:128],
                                      bw_a[32 * hh:32 * hh + 1, 0:512],
                                      start=False, stop=True)
                    qmm[g] = t1.inc(m)
                blk.tensor(_t3)

                def _sq_(s, g=g):
                    s.wait_ge(t1.h, qmm[g])
                    s.wait_ge(v1.h, kfin[g])
                    return s.activation(e1k[:], Q_ps[:], AF.Exp)
                eq_mk[g] = sq.add(_sq_)

            def emit_v_eluq(c, t):
                g = c * TPC + t

                def _v(v):
                    v.wait_ge(s1.h, eq_mk[g])
                    v1.inc(v.tensor_scalar(out=e1k[:], in0=e1k[:], scalar1=1.0,
                                           scalar2=None, op0=ALU.min))
                    v.wait_ge(v1.h, v1.n)
                    if c >= 2:
                        v.wait_ge(qspl[c % 2].h, qspl_mk[c - 2])
                    qfin[g] = v1.inc(v.scalar_tensor_tensor(out=q_sb[:, c % 2, t, :], in0=Q_ps[:],
                                                            scalar=1.0, in1=e1k[:],
                                                            op0=ALU.add, op1=ALU.max))
                blk.vector(_v)

            def emit_lnA(ca, sq):
                parA = ca % 2

                def _vbn(v, ca=ca, parA=parA):
                    v.wait_ge(xld[parA].h, xld_mk[ca])
                    if ca >= 2:
                        v.wait_ge(s1.h, rsx_mark[ca - 2])
                    for t in range(TPC):
                        v1.inc(v.bn_stats(out=bnx[:, parA, t, 0:6],
                                          in_=x_sb[:, parA, t, 0:512]))
                        v1.inc(v.bn_stats(out=bnx[:, parA, t, 6:12],
                                          in_=x_sb[:, parA, t, 512:1024]))
                    v.wait_ge(v1.h, v1.n)
                    for t in range(TPC):
                        v1.inc(v.bn_aggr(out=mvx[:, parA, t, :], in_=bnx[:, parA, t, :]))
                    aggx_mk[ca] = v1.n
                blk.vector(_vbn)

                def _srs(s, ca=ca, parA=parA):
                    s.wait_ge(v1.h, aggx_mk[ca])
                    for t in range(TPC):
                        m = s.activation(mvx[:, parA, t, 1:2], mvx[:, parA, t, 1:2],
                                         AF.Ln, bias=epsc[:, 0:1])
                    return m
                lnx_mk = sq.add(_srs, inc=4)

                def _sex(s, ca=ca, parA=parA, _lm=lnx_mk):
                    s.wait_ge(s1.h, _lm)
                    for t in range(TPC):
                        g = ca * TPC + t
                        m = s.activation(rs1[:, g:g + 1], mvx[:, parA, t, 1:2],
                                         AF.Exp, scale=-0.5)
                    return m
                rsx_mark[ca] = sq.add(_sex, inc=4)

                def _vnm(v, ca=ca, parA=parA):
                    v.wait_ge(s1.h, rsx_mark[ca])
                    for t in range(TPC):
                        g = ca * TPC + t
                        v1.inc(v.tensor_scalar(out=nmurs1[:, g:g + 1], in0=mvx[:, parA, t, 0:1],
                                               scalar1=rs1[:, g:g + 1], scalar2=-1.0,
                                               op0=ALU.mult, op1=ALU.mult))
                    nmx_mark[ca] = v1.n
                blk.vector(_vnm)

            for it in range(NCH + 1):
                cc, c = it, it - 1
                par = cc % 2
                B0 = 16 * par
                sq = SQueue(s1)
                if it == 0:
                    emit_lnA(0, sq)

                if cc < NCH:
                    t1mk = {}
                    hmk = {}
                    hmkA[cc] = hmk
                    for t in range(TPC):
                        def _t1(s, cc=cc, t=t):
                            s.wait_ge(v1.h, nmx_mark[cc])
                            if cc >= 1:
                                s.wait_ge(v1.h, hmkA[cc - 1][t])
                            g = cc * TPC + t
                            return s.activation(t1b[:, t, :], x_sb[:, cc % 2, t, :], AF.Identity,
                                                bias=nmurs1[:, g:g + 1], scale=rs1[:, g:g + 1])
                        t1mk[t] = sq.add(_t1)
                    t1last[cc] = t1mk[TPC - 1]

                    def _vh01(v, cc=cc, t1mk=t1mk, hmk=hmk):
                        for t in range(2):
                            v.wait_ge(s1.h, t1mk[t])
                            if cc >= 2 and t == 0:
                                v.wait_ge(s1.h, t2last[cc - 2])
                            v1.inc(v.tensor_mul(out=h_t[:, t, :], in0=t1b[:, t, :],
                                                in1=ca1_b[:]))
                            v.wait_ge(v1.h, v1.n)
                            hmk[t] = v1.inc(v.tensor_add(out=h_t[:, t, :], in0=h_t[:, t, :],
                                                         in1=sa_b[:]))
                            v.wait_ge(v1.h, v1.n)
                            v1.inc(v.bn_stats(out=bnh[:, t, 0:6], in_=h_t[:, t, 0:512]))
                            v1.inc(v.bn_stats(out=bnh[:, t, 6:12], in_=h_t[:, t, 512:1024]))
                    blk.vector(_vh01)

                if c >= 0:
                    emit_mm_kv(c, 0, sq)
                    emit_v_elukv(c, 0)
                    emit_t_gram(c, 0)
                    emit_v_ksev(c, 0)
                    emit_t_q(c, 0, sq)
                    emit_v_eluq(c, 0)

                if cc < NCH:
                    def _vh23(v, cc=cc, t1mk=t1mk, hmk=hmk):
                        for t in range(2, TPC):
                            v.wait_ge(s1.h, t1mk[t])
                            v1.inc(v.tensor_mul(out=h_t[:, t, :], in0=t1b[:, t, :],
                                                in1=ca1_b[:]))
                            v.wait_ge(v1.h, v1.n)
                            hmk[t] = v1.inc(v.tensor_add(out=h_t[:, t, :], in0=h_t[:, t, :],
                                                         in1=sa_b[:]))
                            v.wait_ge(v1.h, v1.n)
                            v1.inc(v.bn_stats(out=bnh[:, t, 0:6], in_=h_t[:, t, 0:512]))
                            v1.inc(v.bn_stats(out=bnh[:, t, 6:12], in_=h_t[:, t, 512:1024]))
                        v.wait_ge(v1.h, v1.n)
                        if cc >= 1:
                            v.wait_ge(s1.h, rsh_mark[cc - 1])
                        for t in range(TPC):
                            v1.inc(v.bn_aggr(out=mvh[:, t, :], in_=bnh[:, t, :]))
                        aggh_mk[cc] = v1.n
                    blk.vector(_vh23)

                if cc < NCH:
                    def _srs2(s, cc=cc):
                        s.wait_ge(v1.h, aggh_mk[cc])
                        for t in range(TPC):
                            m = s.activation(mvh[:, t, 1:2], mvh[:, t, 1:2],
                                             AF.Ln, bias=epsc[:, 0:1])
                        return m
                    lnh_mk = sq.add(_srs2, inc=4)

                    def _seh(s, cc=cc, _lm=lnh_mk):
                        s.wait_ge(s1.h, _lm)
                        if cc >= 1:
                            s.wait_ge(s1.h, t2last[cc - 1])
                        for t in range(TPC):
                            m = s.activation(rsh[:, t:t + 1], mvh[:, t, 1:2],
                                             AF.Exp, scale=-0.5)
                        return m
                    rsh_mark[cc] = sq.add(_seh, inc=4)

                    def _vnmh(v, cc=cc):
                        v.wait_ge(s1.h, rsh_mark[cc])
                        for t in range(TPC):
                            v1.inc(v.tensor_scalar(out=nmh[:, t:t + 1], in0=mvh[:, t, 0:1],
                                                   scalar1=rsh[:, t:t + 1], scalar2=-1.0,
                                                   op0=ALU.mult, op1=ALU.mult))
                        nmh_mark[cc] = v1.n
                    blk.vector(_vnmh)

                if c >= 0:
                    emit_mm_kv(c, 1, sq)
                    emit_v_elukv(c, 1)
                    emit_t_gram(c, 1)
                    emit_v_ksev(c, 1)
                    emit_t_q(c, 1, sq)
                    emit_v_eluq(c, 1)

                if cc < NCH:
                    t2mks = {}
                    for t in range(TPC):
                        def _t2(s, cc=cc, t=t, par=par):
                            s.wait_ge(v1.h, nmh_mark[cc])
                            if t == 0 and cc >= 2:
                                s.wait_ge(xnst[par].h, xnst_mk[cc - 2])
                            return s.activation(xn_t[:, par, t, :], h_t[:, t, :], AF.Identity,
                                                bias=nmh[:, t:t + 1],
                                                scale=rsh[:, t:t + 1])
                        t2mks[t] = sq.add(_t2)
                    t2last[cc] = t2mks[TPC - 1]
                    t2mks_h[cc] = t2mks[TPC - 1]

                    def _sspq(s, cc=cc, par=par):
                        s.wait_ge(s1.h, t2mks_h[cc])
                        s.wait_ge(xnst[par].h, xnst[par].n)
                        xnst_mk[cc] = xnst[par].inc(
                            s.dma_start(out=r3(xn_scr[tok(cc)]), in_=xn_t[:, par]), 16)
                    sq.add(_sspq, inc=0)

                if c >= 0:
                    emit_mm_kv(c, 2, sq)
                    emit_v_elukv(c, 2)
                    emit_t_gram(c, 2)
                    emit_v_ksev(c, 2)
                    emit_t_q(c, 2, sq)
                    emit_v_eluq(c, 2)

                if cc + 1 < NCH:
                    emit_lnA(cc + 1, sq)

                if c >= 0:
                    emit_mm_kv(c, 3, sq)
                    emit_v_elukv(c, 3)
                    emit_t_gram(c, 3)
                    emit_v_ksev(c, 3)
                    emit_t_q(c, 3, sq)
                    emit_v_eluq(c, 3)

                sq.flush(blk)

                if cc < NCH:
                    def _syt(sy, cc=cc, par=par):
                        sy.wait_ge(xnst[par].h, xnst_mk[cc])
                        if cc >= 2:
                            sy.wait_ge(t1.h, qmm[(cc - 2) * TPC + TPC - 1])
                        sy.wait_ge(xnfl[par].h, xnfl[par].n)
                        for kc in range(8):
                            m = sy.dma_start_transpose(out=xn_f[:, par, kc, :],
                                                       in_=xn_scr[tok(cc), kc * 128:(kc + 1) * 128])
                            xnfl[par].inc(m, 16)
                        xnf_ok[cc] = xnfl[par].n
                    blk.sync(_syt)

                    def _gxl(gp, cc=cc, par=par):
                        if cc + 2 < NCH:
                            gp.wait_ge(s1.h, t1last[cc])
                            gp.wait_ge(xld[par].h, xld[par].n)
                            xld_mk[cc + 2] = xld[par].inc(
                                gp.dma_start(out=x_sb[:, par], in_=r3(x_in[tok(cc + 2)])), 16)
                        for _ in range(3):
                            if castq:
                                o, i = castq.pop(0)
                                wcast.inc(gp.dma_start(out=o, in_=i), 16)
                    blk.gpsimd(_gxl)

                if c >= 0:
                    def _sq2(sc, c=c):
                        sc.wait_ge(v1.h, qfin[c * TPC + TPC - 1])
                        sc.wait_ge(qspl[c % 2].h, qspl[c % 2].n)
                        qspl_mk[c] = qspl[c % 2].inc(
                            sc.dma_start(out=r3(q_scr[tok(c)]), in_=q_sb[:, c % 2]), 16)
                    blk.scalar(_sq2)

            if castq:
                def _gcrest(gp):
                    for o, i in castq:
                        wcast.inc(gp.dma_start(out=o, in_=i), 16)
                    castq.clear()
                blk.gpsimd(_gcrest)

            def _vprep(v):
                v.wait_ge(t1.h, gmm[NT - 1])
                v.wait_ge(v1.h, ksev[NT - 1])
                v1.inc(v.memset(kv_sb[:], 0.0))
                v1.inc(v.memset(ksbd[:], 0.0))
                v.wait_ge(v1.h, v1.n)
                for p in range(NPAIR):
                    v1.inc(v.tensor_copy(out=kv_sb[0:64, p * 128:p * 128 + 64],
                                         in_=KV_ps[0:64, p * 128:p * 128 + 64]))
                    v1.inc(v.tensor_copy(out=kv_sb[64:128, p * 128 + 64:p * 128 + 128],
                                         in_=KV_ps[64:128, p * 128 + 64:p * 128 + 128]))
                    v1.inc(v.tensor_copy(out=ksbd[0:64, p * NH + 2 * p:p * NH + 2 * p + 1],
                                         in_=ksum_sb[0:64, p:p + 1]))
                    v1.inc(v.tensor_copy(out=ksbd[64:128, p * NH + 2 * p + 1:p * NH + 2 * p + 2],
                                         in_=ksum_sb[64:128, p:p + 1]))
            blk.vector(_vprep)

    es_wq.close()

    with ExitStack() as es2:
        sb2 = lambda n, s, dt: es2.enter_context(nc.sbuf_tensor(n, s, dt))
        ps2 = lambda n, s: es2.enter_context(nc.psum_tensor(n, s, F32))
        x_sb = sb2("x_sb2", [128, 2, TPC, D], F32)
        qT = sb2("qT", [128, 2, 8, CT], BF16)
        attn_f = sb2("attn_f", [128, NPAIR, CT], BF16)
        rb_sb = sb2("rb_sb", [128, 2, CT], F32)
        de_sb = sb2("de_sb", [NH, CT], F32)
        rcp_bf = sb2("rcp_bf", [NH, CT], BF16)
        t1b2 = sb2("t1b2", [128, 2, D], BF16)
        t3b2 = sb2("t3b2", [128, 2, D], BF16)
        u_b = sb2("u_b", [128, D], BF16)
        u2_b = sb2("u2_b", [128, 2, D], BF16)
        r_b = sb2("r_b", [128, D], F32)
        vv_b = sb2("vv_b", [128, D], F32)
        x2_sb = sb2("x2_sb", [128, TPC, D], F32)
        h2A = sb2("h2A", [128, D], BF16)
        h2_t = sb2("h2_t", [128, TPC, D], BF16)
        st2 = sb2("st2", [128, 16], F32)
        bn3 = sb2("bn3", [128, TPC, 12], F32)
        mv3 = sb2("mv3", [128, TPC, 2], F32)
        DEN_ps = ps2("DEN_ps", [NH, CT])
        RB_ps = ps2("RB_ps", [128, 2 * CT])
        QO_ps = ps2("QO_ps", [128, 2 * CT])
        WO_ps = ps2("WO_ps", [128, D])

        xld2 = [S("xld2a"), S("xld2b")]
        qTl = [S("qTla"), S("qTlb")]
        xld2_mk = [0] * (NCH + 2)
        qTl_mk = [0] * (NCH + 2)
        v2 = S("v2")
        s2 = S("s2")
        g2 = S("g2")
        t2 = S("t2")
        x2st = S("x2st")
        h2st = S("h2st")
        den_free = [0] * NCH
        qT_free = [0] * NCH
        x_free_v = [0] * NCH
        x_free_s = [0] * NCH
        x_free_g = [0] * NCH
        gusA = {}
        vh2mA = {}
        wo_free = [0] * NT
        r_done = [0] * NT
        h2a_done = [0] * NT
        wo_last_ldw = [0] * NCH
        agg3_mk = [0] * NCH
        rs3q_mark = [0] * NCH
        nm3_mark = [0] * NCH

        with nc.Block("p2a") as blk:
            def _sy0(sy):
                qspl[0].wait(sy)
                qspl[1].wait(sy)
                xld2_mk[0] = xld2[0].inc(sy.dma_start(out=x_sb[:, 0], in_=r3(x_in[tok(0)])), 16)
                for kc in range(8):
                    qTl[0].inc(sy.dma_start_transpose(out=qT[:, 0, kc, :],
                                                      in_=q_scr[tok(0), kc * 128:(kc + 1) * 128]), 16)
                qTl_mk[0] = qTl[0].n
            blk.sync(_sy0)

            tden_mk = [0] * NCH
            vrcp_mk = [0] * NCH

            def em_td(cn):
                def _td(tn, cn=cn):
                    tn.wait_ge(qTl[cn % 2].h, qTl_mk[cn])
                    if cn >= 1:
                        tn.wait_ge(v2.h, den_free[cn - 1])
                    for p in range(NPAIR):
                        m = tn.matmul(DEN_ps[:], ksbd[:, p * NH:(p + 1) * NH],
                                      qT[:, cn % 2, p, :], start=(p == 0), stop=(p == NPAIR - 1))
                    tden_mk[cn] = t2.inc(m)
                blk.tensor(_td)

            def em_vd(cn):
                def _vd(v, cn=cn):
                    v.wait_ge(t2.h, tden_mk[cn])
                    den_free[cn] = v2.inc(v.tensor_scalar(out=de_sb[:], in0=DEN_ps[:],
                                                          scalar1=DEN_EPS, scalar2=None, op0=ALU.add))
                    v.wait_ge(v2.h, v2.n)
                    v2.inc(v.reciprocal(de_sb[:], de_sb[:]))
                    v.wait_ge(v2.h, v2.n)
                    vrcp_mk[cn] = v2.inc(v.tensor_copy(out=rcp_bf[:], in_=de_sb[:]))
                blk.vector(_vd)

            for c in range(NCH):
                par = c % 2

                def _syc(sy, c=c):
                    if c + 1 < NCH:
                        if c >= 1:
                            sy.wait_ge(v2.h, x_free_v[c - 1])
                            sy.wait_ge(s2.h, x_free_s[c - 1])
                            sy.wait_ge(t2.h, qT_free[c - 1])
                        sy.wait_ge(xld2[(c + 1) % 2].h, xld2[(c + 1) % 2].n)
                        sy.wait_ge(qTl[(c + 1) % 2].h, qTl[(c + 1) % 2].n)
                        xld2_mk[c + 1] = xld2[(c + 1) % 2].inc(
                            sy.dma_start(out=x_sb[:, (c + 1) % 2], in_=r3(x_in[tok(c + 1)])), 16)
                        for kc in range(8):
                            qTl[(c + 1) % 2].inc(
                                sy.dma_start_transpose(out=qT[:, (c + 1) % 2, kc, :],
                                                       in_=q_scr[tok(c + 1), kc * 128:(kc + 1) * 128]), 16)
                        qTl_mk[c + 1] = qTl[(c + 1) % 2].n
                blk.sync(_syc)

                if c == 0:
                    em_td(0)
                    em_vd(0)
                vrcp = vrcp_mk[c]

                sq2 = SQueue(s2)
                rbev = [0] * NPAIR
                aev = [0] * NPAIR
                for p in range(NPAIR):
                    def _tp(tn, p=p, c=c):
                        if p == 0:
                            tn.wait_ge(v2.h, vrcp)
                        if p >= 2:
                            tn.wait_ge(v2.h, rbev[p - 2])
                        m = tn.matmul(RB_ps[:, (p % 2) * CT:(p % 2 + 1) * CT],
                                      eexp[:, p * 128:(p + 1) * 128], rcp_bf[:],
                                      start=True, stop=True)
                        t2.inc(m)
                        if p >= 2:
                            tn.wait_ge(v2.h, aev[p - 2])
                        if c >= 1 and p == 0:
                            tn.wait_ge(v2.h, wo_free[c * TPC - 1])
                        m = tn.matmul(QO_ps[:, (p % 2) * CT:(p % 2 + 1) * CT],
                                      kv_sb[:, p * 128:(p + 1) * 128],
                                      qT[:, par, p, :], start=True, stop=True)
                        t2.inc(m)
                    blk.tensor(_tp)
                    tpp = t2.n

                    def _vp(v, p=p, tpp=tpp):
                        v.wait_ge(t2.h, tpp - 1)
                        rbev[p] = v2.inc(v.tensor_copy(out=rb_sb[:, p % 2, :],
                                                       in_=RB_ps[:, (p % 2) * CT:(p % 2 + 1) * CT]))
                        v.wait_ge(t2.h, tpp)
                        v.wait_ge(v2.h, rbev[p])
                        aev[p] = v2.inc(v.tensor_mul(out=attn_f[:, p, :],
                                                     in0=QO_ps[:, (p % 2) * CT:(p % 2 + 1) * CT],
                                                     in1=rb_sb[:, p % 2, :]))
                    blk.vector(_vp)
                qT_free[c] = t2.n

                twms = {}
                vx2 = {}
                gus = {}
                t3mks = {}
                vh2m = {}
                for t in range(TPC):
                    gt = c * TPC + t

                    def _tw(tn, t=t, gt=gt):
                        if t == 0:
                            tn.wait_ge(v2.h, aev[NPAIR - 1])
                        if gt > 0:
                            tn.wait_ge(v2.h, wo_free[gt - 1])
                        for kc in range(8):
                            for hh in range(2):
                                m = tn.matmul(WO_ps[:, hh * 512:(hh + 1) * 512],
                                              attn_f[:, kc, t * 128:(t + 1) * 128],
                                              wout_sb[:, kc, hh * 512:(hh + 1) * 512],
                                              start=(kc == 0), stop=(kc == 7))
                        t2.inc(m)
                    blk.tensor(_tw)
                    twms[t] = t2.n

                    def _sep(s, c=c, t=t, gt=gt):
                        s.wait_ge(xld2[c % 2].h, xld2_mk[c])
                        if gt > 1:
                            s.wait_ge(g2.h, gusA[gt - 2])
                        return s.activation(t1b2[:, gt % 2, :], x_sb[:, c % 2, t, :], AF.Identity,
                                            bias=nmurs1[:, gt:gt + 1], scale=rs1[:, gt:gt + 1])
                    st1 = sq2.add(_sep)

                    def _gep(gp, gt=gt, st1=st1, t=t):
                        gp.wait_ge(s2.h, st1)
                        if gt > 1:
                            gp.wait_ge(v2.h, r_done[gt - 2])
                        g2.inc(gp.tensor_mul(out=u_b[:], in0=t1b2[:, gt % 2, :], in1=A_b[:]))
                        gp.wait_ge(g2.h, g2.n)
                        gus[t] = gusA[gt] = g2.inc(gp.tensor_add(out=u2_b[:, gt % 2, :],
                                                                 in0=u_b[:], in1=B_b[:]))
                    blk.gpsimd(_gep)

                    def _vep(v, c=c, t=t, gt=gt):
                        v.wait_ge(g2.h, gus[t])
                        r_done[gt] = v2.inc(v.tensor_add(out=r_b[:], in0=u2_b[:, gt % 2, :],
                                                         in1=x_sb[:, c % 2, t, :]))
                        v.wait_ge(t2.h, twms[t])
                        v.wait_ge(v2.h, v2.n)
                        if c >= 1 and t == 0:
                            v.wait_ge(x2st.h, 16 * c)
                        wo_free[gt] = v2.inc(v.tensor_add(out=x2_sb[:, t, :], in0=r_b[:],
                                                          in1=WO_ps[:]))
                        v.wait_ge(v2.h, v2.n)
                        v2.inc(v.bn_stats(out=bn3[:, t, 0:6], in_=x2_sb[:, t, 0:512]))
                        vx2[t] = v2.inc(v.bn_stats(out=bn3[:, t, 6:12], in_=x2_sb[:, t, 512:1024]))
                    blk.vector(_vep)

                    if t == 2 and c + 1 < NCH:
                        em_td(c + 1)
                        em_vd(c + 1)

                def _vagg3(v, c=c):
                    v.wait_ge(v2.h, vx2[TPC - 1])
                    if c >= 1:
                        v.wait_ge(s2.h, rs3q_mark[c - 1])
                    for t in range(TPC):
                        v2.inc(v.bn_aggr(out=mv3[:, t, :], in_=bn3[:, t, :]))
                    agg3_mk[c] = v2.n
                blk.vector(_vagg3)

                def _srs3(s, c=c):
                    s.wait_ge(v2.h, agg3_mk[c])
                    for t in range(TPC):
                        m = s.activation(mv3[:, t, 1:2], mv3[:, t, 1:2],
                                         AF.Ln, bias=epsc[:, 0:1])
                    return m
                ln3q_mk = sq2.add(_srs3, inc=4)

                def _se3(s, c=c, _lm=ln3q_mk):
                    s.wait_ge(s2.h, _lm)
                    for t in range(TPC):
                        gt = c * TPC + t
                        m = s.activation(rs3[:, gt:gt + 1], mv3[:, t, 1:2],
                                         AF.Exp, scale=-0.5)
                    return m
                rs3q_mark[c] = sq2.add(_se3, inc=4)

                def _vnm3(v, c=c):
                    v.wait_ge(s2.h, rs3q_mark[c])
                    for t in range(TPC):
                        gt = c * TPC + t
                        v2.inc(v.tensor_scalar(out=nmurs3[:, gt:gt + 1], in0=mv3[:, t, 0:1],
                                               scalar1=rs3[:, gt:gt + 1], scalar2=-1.0,
                                               op0=ALU.mult, op1=ALU.mult))
                    nm3_mark[c] = v2.n
                blk.vector(_vnm3)

                for t in range(TPC):
                    gt = c * TPC + t

                    def _sh2(s, c=c, t=t, gt=gt):
                        s.wait_ge(v2.h, nm3_mark[c])
                        if gt > 1:
                            s.wait_ge(v2.h, vh2mA[gt - 2])
                        return s.activation(t3b2[:, gt % 2, :], x2_sb[:, t, :], AF.Identity,
                                            bias=nmurs3[:, gt:gt + 1], scale=rs3[:, gt:gt + 1])
                    t3mks[t] = sq2.add(_sh2)

                    def _vh2(v, c=c, t=t, gt=gt):
                        if t > 0:
                            v.wait_ge(v2.h, v2.n)
                        v.wait_ge(s2.h, t3mks[t])
                        vh2m[t] = vh2mA[gt] = v2.inc(v.tensor_mul(out=h2A[:], in0=t3b2[:, gt % 2, :],
                                                                  in1=cb1_b[:]))
                        v.wait_ge(v2.h, v2.n)
                        if c >= 1 and t == 0:
                            v.wait_ge(h2st.h, 16 * c)
                        v2.inc(v.tensor_add(out=h2_t[:, t, :], in0=h2A[:], in1=sbb_b[:]))
                    blk.vector(_vh2)

                sq2.flush(blk)

                x_free_v[c] = v2.n
                x_free_s[c] = s2.n
                x_free_g[c] = g2.n
                vln = v2.n
                gln = g2.n

                def _gspill(sc, c=c, vln=vln, gln=gln):
                    sc.wait_ge(v2.h, vln)
                    sc.wait_ge(x2st.h, x2st.n)
                    x2st.inc(sc.dma_start(out=r3(x2_scr[tok(c)]), in_=x2_sb[:]), 16)
                    sc.wait_ge(h2st.h, h2st.n)
                    h2st.inc(sc.dma_start(out=r3(h2_scr[tok(c)]), in_=h2_t[:]), 16)
                blk.scalar(_gspill)

            def _gfin(sc):
                x2st.wait(sc)
                h2st.wait(sc)
            blk.scalar(_gfin)

    es_wo.close()

    with ExitStack() as es3:
        sb3 = lambda n, s, dt: es3.enter_context(nc.sbuf_tensor(n, s, dt))
        ps3 = lambda n, s: es3.enter_context(nc.psum_tensor(n, s, F32))
        h2f = sb3("h2f", [128, 2, 8, CT], BF16)
        w13_sb = sb3("w13_sb", [128, 2, 8, D], BF16)
        w2_sb = sb3("w2_sb", [128, 2, 2, 4, D], BF16)
        act_sl = sb3("act_sl", [128, NSLOT, 4, CT], BF16)
        gbuf = sb3("gbuf", [128, 4, CT], BF16)
        cE = sb3("cE", [128, 2, CT], BF16)
        sc_all = sb3("sc_all", [128, TPC * NE], F32)
        bb_all = sb3("bb_all", [128, TPC * NE], F32)
        oh_all = sb3("oh_all", [128, TPC * NE], F32)
        pk_t = sb3("pk_t", [128, TPC * NE], F32)
        sel_pk = sb3("sel_pk", [128, TPC * NE], F32)
        mx_s = sb3("mx_s", [128, TPC], F32)
        ws_s = sb3("ws_s", [128, TPC], F32)
        comb_bf = sb3("comb_bf", [128, TPC * NE], BF16)
        comb_f = sb3("comb_f", [NH, CT], BF16)
        x2_sb3 = sb3("x2_sb3", [128, 2, TPC, D], F32)
        yf_b = sb3("yf_b", [128, TPC, D], F32)
        ob_b = sb3("ob_b", [128, TPC, D], F32)
        I1_ps = ps3("I1_ps", [128, 2 * 512])
        Y_ps = ps3("Y_ps", [128, 2 * D])
        CE_ps = ps3("CE_ps", [128, 2 * 512])

        h2l = [S("h2la"), S("h2lb")]
        w13l = [S("w13l0"), S("w13l1")]
        w2l = {(p, q): S(f"w2l{p}{q}") for p in range(2) for q in range(2)}
        x2l = [S("x2la"), S("x2lb")]
        h2l_mk = [0] * (NCH + 2)
        x2l_mk = [0] * (NCH + 2)
        w13_mk = {}
        w2_mk = {}
        v3 = S("v3")
        s3 = S("s3")
        g3 = S("g3")
        t3 = S("t3")
        outst = S("outst")
        i1done = {}
        tvec = {}
        actv = {}
        ymm = {}
        yev = {}
        ceev = {}
        ce_mm = {}
        gate_done = [0] * NCH
        gate_sig = [0] * NCH
        comb_done = [0] * NCH
        ct_free = [0] * NCH
        h2f_free = [0] * NCH
        x2_free = [0] * NCH
        vzmul = [0] * NCH
        w2slot_user = {}
        w13_user = {}
        gmul_j = {}

        eidx = lambda c, e: c * NEX + e

        with nc.Block("p2b") as blk:
            def _sy00(sy):
                wcast.wait(sy)
                h2st.wait(sy)
                for kc in range(8):
                    h2l[0].inc(sy.dma_start_transpose(out=h2f[:, 0, kc, :],
                                                      in_=h2_scr[tok(0), kc * 128:(kc + 1) * 128]), 16)
                h2l_mk[0] = h2l[0].n
            blk.sync(_sy00)

            def _gx0(gp):
                x2st.wait(gp)
                x2l_mk[0] = x2l[0].inc(gp.dma_start(out=x2_sb3[:, 0], in_=r3(x2_scr[tok(0)])), 16)
            blk.gpsimd(_gx0)

            tct_mm = [0] * NCH
            vct_mark = [0] * NCH

            def em_gate(c):
                def _tg(tn, c=c):
                    tn.wait_ge(h2l[c % 2].h, h2l_mk[c])
                    if c >= 1:
                        tn.wait_ge(v3.h, vct_mark[c - 1])
                        tn.wait_ge(s3.h, gate_sig[c - 1])
                    for t in range(TPC):
                        for kc in range(8):
                            m = tn.matmul(CE_ps[:, 512 + t * NE:512 + (t + 1) * NE],
                                          h2f[:, c % 2, kc, t * 128:(t + 1) * 128],
                                          gatew_sb[:, kc, :],
                                          start=(kc == 0), stop=(kc == 7))
                    gate_done[c] = t3.inc(m)
                blk.tensor(_tg)

            def em_sig(c, sq):
                def _sg(s, c=c):
                    s.wait_ge(t3.h, gate_done[c])
                    if c >= 1:
                        s.wait_ge(v3.h, comb_done[c - 1])
                    return s.activation(sc_all[:], CE_ps[:, 512:512 + TPC * NE], AF.Sigmoid)
                gate_sig[c] = sq.add(_sg)

            def em_topk(c, phase):
                def _v(v, c=c, phase=phase):
                    if phase == 0:
                        v.wait_ge(s3.h, gate_sig[c])
                        for t in range(TPC):
                            v3.inc(v.tensor_add(out=bb_all[:, t * NE:(t + 1) * NE],
                                                in0=sc_all[:, t * NE:(t + 1) * NE], in1=gateb_b[:]))
                        v.wait_ge(v3.h, v3.n)
                        its = [0]
                    elif phase == 1:
                        its = [1, 2]
                    else:
                        its = [3]
                    for it4 in its:
                        for t in range(TPC):
                            v3.inc(v.reduce_max(mx_s[:, t:t + 1], bb_all[:, t * NE:(t + 1) * NE],
                                                axis=AX.X))
                        v.wait_ge(v3.h, v3.n)
                        for t in range(TPC):
                            v3.inc(v.tensor_scalar(out=oh_all[:, t * NE:(t + 1) * NE],
                                                   in0=bb_all[:, t * NE:(t + 1) * NE],
                                                   scalar1=mx_s[:, t:t + 1], scalar2=0.0,
                                                   op0=ALU.subtract, op1=ALU.is_ge))
                        v.wait_ge(v3.h, v3.n)
                        for t in range(TPC):
                            sl = slice(t * NE, (t + 1) * NE)
                            if it4 == 0:
                                v3.inc(v.tensor_mul(out=sel_pk[:, sl], in0=oh_all[:, sl],
                                                    in1=sc_all[:, sl]))
                            else:
                                v3.inc(v.tensor_mul(out=pk_t[:, sl], in0=oh_all[:, sl],
                                                    in1=sc_all[:, sl]))
                        if it4 < 3:
                            for t in range(TPC):
                                sl = slice(t * NE, (t + 1) * NE)
                                v3.inc(v.scalar_tensor_tensor(out=bb_all[:, sl], in0=oh_all[:, sl],
                                                              scalar=-1e30, in1=bb_all[:, sl],
                                                              op0=ALU.mult, op1=ALU.add))
                        v.wait_ge(v3.h, v3.n)
                        if it4 > 0:
                            for t in range(TPC):
                                sl = slice(t * NE, (t + 1) * NE)
                                v3.inc(v.tensor_add(out=sel_pk[:, sl], in0=sel_pk[:, sl],
                                                    in1=pk_t[:, sl]))
                            v.wait_ge(v3.h, v3.n)
                    if phase == 2:
                        for t in range(TPC):
                            v3.inc(v.reduce_sum(ws_s[:, t:t + 1], sel_pk[:, t * NE:(t + 1) * NE],
                                                axis=AX.X))
                        v.wait_ge(v3.h, v3.n)
                        v3.inc(v.reciprocal(ws_s[:], ws_s[:]))
                        v.wait_ge(v3.h, v3.n)
                        if c >= 1:
                            v.wait_ge(t3.h, tct_mm[c - 1])
                        for t in range(TPC):
                            comb_done[c] = v3.inc(v.tensor_scalar(out=comb_bf[:, t * NE:(t + 1) * NE],
                                                                  in0=sel_pk[:, t * NE:(t + 1) * NE],
                                                                  scalar1=ws_s[:, t:t + 1],
                                                                  scalar2=ROUTE_SCALE,
                                                                  op0=ALU.mult, op1=ALU.mult))
                blk.vector(_v)

            def em_tct(c):
                def _t(tn, c=c):
                    tn.wait_ge(v3.h, comb_done[c])
                    for t in range(TPC):
                        m = tn.matmul(CE_ps[0:NH, 512 + t * 128:512 + (t + 1) * 128],
                                      comb_bf[:, t * NE:(t + 1) * NE], ident[:],
                                      start=True, stop=True)
                    tct_mm[c] = t3.inc(m)
                blk.tensor(_t)

            def em_vct(c):
                def _v(v, c=c):
                    v.wait_ge(t3.h, tct_mm[c])
                    if c >= 1:
                        v.wait_ge(t3.h, ce_mm.get(eidx(c - 1, NE - 1), 0))
                    vct_mark[c] = v3.inc(v.tensor_copy(out=comb_f[:], in_=CE_ps[0:NH, 512:1024]))
                blk.vector(_v)

            for c in range(NCH):
                par = c % 2

                def _gxp(gp, c=c):
                    if c + 1 < NCH:
                        if c >= 1:
                            gp.wait_ge(v3.h, x2_free[c - 1])
                        gp.wait_ge(x2l[(c + 1) % 2].h, x2l[(c + 1) % 2].n)
                        x2l_mk[c + 1] = x2l[(c + 1) % 2].inc(
                            gp.dma_start(out=x2_sb3[:, (c + 1) % 2], in_=r3(x2_scr[tok(c + 1)])), 16)
                blk.gpsimd(_gxp)

                sq3 = SQueue(s3)

                if c == 0:
                    em_gate(0)
                    em_sig(0, sq3)
                    em_topk(0, 0)
                    em_topk(0, 1)
                    em_topk(0, 2)
                    em_tct(0)
                    em_vct(0)

                for gi, grp in enumerate(EGROUPS):
                    for e in grp:
                        ei = eidx(c, e)
                        slot = ei % NSLOT
                        pos = grp.index(e)
                        _mc = {}

                        def _syw(sy, c=c, e=e, ei=ei, gi=gi, pos=pos):
                            s13 = ei % 2
                            if s13 in w13_user:
                                sy.wait_ge(t3.h, i1done[w13_user[s13]])
                            w13_user[s13] = ei
                            if c == 0 and e < 2:
                                wcast.wait(sy)
                            sy.wait_ge(w13l[s13].h, w13l[s13].n)
                            w13_mk[ei] = w13l[s13].inc(
                                sy.dma_start(out=w13_sb[:, s13],
                                             in_=w13bf[e].rearrange("(k p) n -> p k n", p=128)), 16)
                            key = (gi % 2, pos)
                            if key in w2slot_user:
                                pc, pgi = w2slot_user[key]
                                sy.wait_ge(t3.h, ymm.get((pc, pgi, TPC - 1), 0))
                            w2slot_user[key] = (c, gi)
                            sy.wait_ge(w2l[key].h, w2l[key].n)
                            w2_mk[ei] = w2l[key].inc(
                                sy.dma_start(out=w2_sb[:, gi % 2, pos],
                                             in_=w2bf[e].rearrange("(k p) n -> p k n", p=128)), 16)
                        blk.sync(_syw)

                        if e < NE:
                            def _tce(tn, c=c, e=e, ei=ei):
                                tn.wait_ge(v3.h, vct_mark[c])
                                if e >= 1:
                                    tn.wait_ge(v3.h, ceev.get(eidx(c, e - 1), 0))
                                elif c >= 1:
                                    tn.wait_ge(v3.h, ceev.get(eidx(c - 1, NE - 1), 0))
                                m = tn.matmul(CE_ps[:, 0:512],
                                              erow[:, e * 128:(e + 1) * 128], comb_f[:],
                                              start=True, stop=True)
                                ce_mm[ei] = t3.inc(m)
                                _mc["ce"] = ce_mm[ei]
                            blk.tensor(_tce)

                            def _vce(v, c=c, e=e, ei=ei):
                                v.wait_ge(t3.h, _mc["ce"])
                                pu = ei - 2 if e >= 2 else (eidx(c - 1, 14 + e) if c >= 1 else None)
                                if pu is not None:
                                    v.wait_ge(g3.h, gmul_j.get((pu, 3), 0))
                                ceev[ei] = v3.inc(v.tensor_copy(out=cE[:, e % 2, :],
                                                                in_=CE_ps[:, 0:512]))
                            blk.vector(_vce)

                        s3_base = s3.n
                        v3_base = v3.n

                        def _tm1(tn, c=c, e=e, ei=ei, s3_base=s3_base, v3_base=v3_base):
                            tn.wait_ge(w13l[ei % 2].h, w13_mk[ei])
                            tn.wait_ge(h2l[c % 2].h, h2l_mk[c])
                            if ei >= 1:
                                tn.wait_ge(s3.h, s3_base)
                                tn.wait_ge(v3.h, tvec.get(ei - 1, 0))
                            for oc in range(8):
                                if oc in (2, 3, 4, 5):
                                    tn.wait_ge(s3.h, s3_base + (oc - 1))
                                elif oc in (6, 7):
                                    tn.wait_ge(v3.h, v3_base + (oc - 5))
                                for kc in range(8):
                                    m = tn.matmul(I1_ps[:, (oc % 2) * 512:(oc % 2 + 1) * 512],
                                                  w13_sb[:, ei % 2, kc, oc * 128:(oc + 1) * 128],
                                                  h2f[:, par, kc, :],
                                                  start=(kc == 0), stop=(kc == 7))
                                _mc[("mm", oc)] = t3.inc(m)
                        blk.tensor(_tm1)
                        i1done[ei] = t3.n
                        h2f_free[c] = t3.n

                        for j in range(4):
                            def _sm1(s, ei=ei, j=j, _mc=_mc):
                                s.wait_ge(t3.h, _mc[("mm", j)])
                                if ei >= 1:
                                    s.wait_ge(v3.h, tvec.get(ei - 1, 0))
                                return s.activation(gbuf[:, j, :],
                                                    I1_ps[:, (j % 2) * 512:(j % 2 + 1) * 512],
                                                    AF.Silu)
                            sq3.add(_sm1)
                        assert s3.n == s3_base + 4

                        if e < NE:
                            def _gce(gp, ei=ei, e=e, s3_base=s3_base):
                                gp.wait_ge(v3.h, ceev[ei])
                                if ei >= 1:
                                    gp.wait_ge(v3.h, tvec.get(ei - 1, 0))
                                for j in range(4):
                                    gp.wait_ge(s3.h, s3_base + j + 1)
                                    gmul_j[(ei, j)] = g3.inc(
                                        gp.tensor_mul(out=gbuf[:, j, :], in0=gbuf[:, j, :],
                                                      in1=cE[:, e % 2, :]))
                            blk.gpsimd(_gce)

                        def _vm1(v, c=c, e=e, ei=ei, slot=slot, s3_base=s3_base):
                            if ei >= 1:
                                v.wait_ge(v3.h, actv[ei - 1])
                            if ei >= NSLOT:
                                pe = ei - NSLOT
                                pc, pee = divmod(pe, NEX)
                                v.wait_ge(t3.h, ymm.get((pc, EGROUPS_OF[pee], TPC - 1), 0))
                            for j in range(4):
                                v.wait_ge(t3.h, _mc[("mm", 4 + j)])
                                if e < NE:
                                    v.wait_ge(g3.h, gmul_j[(ei, j)])
                                else:
                                    v.wait_ge(s3.h, s3_base + j + 1)
                                tvec[ei] = v3.inc(v.tensor_mul(out=act_sl[:, slot, j, :],
                                                               in0=gbuf[:, j, :],
                                                               in1=I1_ps[:, (j % 2) * 512:(j % 2 + 1) * 512]))
                            actv[ei] = tvec[ei]
                        blk.vector(_vm1)

                    if gi == 0:
                        def _syp(sy, c=c):
                            if c + 1 < NCH:
                                if c >= 1:
                                    sy.wait_ge(t3.h, h2f_free[c - 1])
                                sy.wait_ge(h2l[(c + 1) % 2].h, h2l[(c + 1) % 2].n)
                                for kc in range(8):
                                    h2l[(c + 1) % 2].inc(
                                        sy.dma_start_transpose(out=h2f[:, (c + 1) % 2, kc, :],
                                                               in_=h2_scr[tok(c + 1), kc * 128:(kc + 1) * 128]), 16)
                                h2l_mk[c + 1] = h2l[(c + 1) % 2].n
                        blk.sync(_syp)

                    if gi == len(EGROUPS) - 1 and c + 1 < NCH:
                        em_tct(c + 1)
                        em_vct(c + 1)

                    for t in range(TPC):
                        def _tm2(tn, c=c, gi=gi, grp=grp, t=t):
                            for e2 in grp:
                                tn.wait_ge(v3.h, actv[eidx(c, e2)])
                                tn.wait_ge(w2l[(gi % 2, grp.index(e2))].h, w2_mk[eidx(c, e2)])
                            prev = None
                            if t >= 2:
                                prev = (c, gi, t - 2)
                            elif gi >= 1:
                                prev = (c, gi - 1, t)
                            elif c >= 1:
                                prev = (c - 1, len(EGROUPS) - 1, t)
                            if prev is not None and prev in yev:
                                tn.wait_ge(v3.h, yev[prev])
                            for e2 in grp:
                                slot2 = eidx(c, e2) % NSLOT
                                for kc in range(4):
                                    for hh in range(2):
                                        m = tn.matmul(
                                            Y_ps[:, (t % 2) * D + hh * 512:(t % 2) * D + (hh + 1) * 512],
                                            act_sl[:, slot2, kc, t * 128:(t + 1) * 128],
                                            w2_sb[:, gi % 2, grp.index(e2), kc, hh * 512:(hh + 1) * 512],
                                            start=(e2 == grp[0] and kc == 0),
                                            stop=(e2 == grp[-1] and kc == 3))
                            ymm[(c, gi, t)] = t3.inc(m)
                        blk.tensor(_tm2)

                        def _vy(v, c=c, gi=gi, t=t):
                            v.wait_ge(t3.h, ymm[(c, gi, t)])
                            if gi == 0:
                                yev[(c, gi, t)] = v3.inc(v.tensor_copy(out=yf_b[:, t, :],
                                                                       in_=Y_ps[:, (t % 2) * D:(t % 2 + 1) * D]))
                            else:
                                yev[(c, gi, t)] = v3.inc(v.tensor_add(out=yf_b[:, t, :],
                                                                      in0=yf_b[:, t, :],
                                                                      in1=Y_ps[:, (t % 2) * D:(t % 2 + 1) * D]))
                        blk.vector(_vy)

                    if c + 1 < NCH:
                        if gi == 0:
                            em_gate(c + 1)
                            em_sig(c + 1, sq3)
                        elif gi == 2:
                            em_topk(c + 1, 0)
                        elif gi == 4:
                            em_topk(c + 1, 1)
                        elif gi == 6:
                            em_topk(c + 1, 2)

                sq3.flush(blk)

                def _vz(v, c=c, par=par):
                    v.wait_ge(v3.h, yev[(c, len(EGROUPS) - 1, TPC - 1)])
                    if c >= 1:
                        v.wait_ge(outst.h, 16 * c)
                    for t in range(TPC):
                        v3.inc(v.tensor_mul(out=ob_b[:, t, :], in0=yf_b[:, t, :], in1=gb_b[:]))
                    vzmul[c] = v3.n
                    v.wait_ge(v3.h, v3.n)
                    v.wait_ge(x2l[c % 2].h, x2l_mk[c])
                    for t in range(TPC):
                        v3.inc(v.tensor_add(out=ob_b[:, t, :], in0=ob_b[:, t, :],
                                            in1=x2_sb3[:, par, t, :]))
                    x2_free[c] = v3.n
                blk.vector(_vz)
                vzm = v3.n

                def _gout(gp, c=c, vzm=vzm):
                    gp.wait_ge(v3.h, vzm)
                    gp.wait_ge(outst.h, outst.n)
                    outst.inc(gp.dma_start(out=r3(out[tok(c)]), in_=ob_b[:]), 16)
                blk.gpsimd(_gout)

            def _gend(gp):
                outst.wait(gp)
            blk.gpsimd(_gend)

    es.close()
    return nc


def _host_prep(inputs):
    f32 = np.float32
    asn = np.ascontiguousarray
    w13 = np.concatenate([np.asarray(inputs["L0_ffn_w13"], f32),
                          np.asarray(inputs["L0_ffn_w13_s"], f32)[None]], axis=0)
    w2 = np.concatenate([np.asarray(inputs["L0_ffn_w2"], f32),
                         np.asarray(inputs["L0_ffn_w2_s"], f32)[None]], axis=0)
    w13T = asn(np.transpose(w13, (0, 2, 1)))
    w2T = asn(np.transpose(w2, (0, 2, 1)))
    gatewT = asn(np.asarray(inputs["L0_ffn_gate_w"], f32).T)
    ident = np.eye(128, dtype=ml_dtypes.bfloat16)
    onesrow = np.ones((1, 128), dtype=ml_dtypes.bfloat16)
    eexp = np.zeros((NH, NPAIR * 128), dtype=ml_dtypes.bfloat16)
    for p in range(NPAIR):
        for j in range(128):
            eexp[2 * p + j // 64, p * 128 + j] = 1
    erow = np.zeros((NH, NE * 128), dtype=ml_dtypes.bfloat16)
    for e in range(NE):
        erow[e, e * 128:(e + 1) * 128] = 1
    return {
        "wmod_in": asn(np.asarray(inputs["L0_wmod"], f32)),
        "gamma_in": asn(np.asarray(inputs["L0_attn_gamma"], f32).reshape(1, D)),
        "beta_in": asn(np.asarray(inputs["L0_attn_beta"], f32).reshape(1, D)),
        "wqkv_in": asn(np.asarray(inputs["L0_attn_Wqkv"], f32)),
        "wout_in": asn(np.asarray(inputs["L0_attn_Wout"], f32)),
        "gatewT_in": gatewT,
        "gateb_in": asn(np.asarray(inputs["L0_ffn_gate_b"], f32).reshape(1, NE)),
        "w13T_in": w13T,
        "w2T_in": w2T,
        "ident_in": ident,
        "onesrow_in": onesrow,
        "eexp_in": eexp,
        "erow_in": erow,
    }


def kernel(**inputs):
    x = np.asarray(inputs["L0_attn_x"], np.float32)
    cmat = np.asarray(inputs["L0_c"], np.float32)
    B, L, _ = x.shape
    com = _host_prep(inputs)
    nc = build_program(L)
    in_maps = []
    for b in range(B):
        m = dict(com)
        m["x_in"] = np.ascontiguousarray(x[b])
        m["c_row"] = np.ascontiguousarray(cmat[b:b + 1])
        in_maps.append(m)
    res = run_bass_kernel_spmd(nc, in_maps, core_ids=list(range(B)))
    return np.stack([res.results[b]["out"] for b in range(B)], axis=0)
